# version 16
# baseline (speedup 1.0000x reference)
import numpy as np
import ml_dtypes
from contextlib import ExitStack
import concourse.bass as bass
import concourse.mybir as mybir
from concourse.bass_utils import run_bass_kernel_spmd

F32 = mybir.dt.float32
BF16 = mybir.dt.bfloat16
AF = mybir.ActivationFunctionType
ALU = mybir.AluOpType
AX = mybir.AxisListType
BFD = ml_dtypes.bfloat16

D = 1024
NH, HD, NPR = 16, 64, 32
L, B = 8192, 2
CH = 128
NCH = 64
SBC = 8
NSB = NCH // SBC
NEXP = 8
DFF = 2048
OWN_SB0 = 6
N_OWN = 16
EPS = 1e-5
EPS_ATT = 1e-6
THETA = 10000.0
LIMIT = 7.0
ALPHA = 1.702

_CFG = dict(p1_sbs=NSB, experts=True, finish=True, p2=True, p3=True, pre_upto=99, dbg=None, p2_n=N_OWN, p2_upto=99, p2_nh=16, p2_hupto=99)


class Sched:
    ENGS = ("sync", "tensor", "vector", "scalar", "gpsimd")

    def __init__(self):
        self.tasks = {e: [] for e in self.ENGS}
        self.cnt = {e: 0 for e in self.ENGS}

    def _mk(self, eng, fn, deps, inc):
        w = {}
        for d in deps:
            if d is None:
                continue
            de, dv = d
            if de == eng and eng != "sync":
                continue
            if dv <= 0:
                continue
            w[de] = max(w.get(de, 0), dv)
        self.tasks[eng].append((sorted(w.items()), fn))
        self.cnt[eng] += inc
        return (eng, self.cnt[eng])

    def t(self, eng, fn, deps=()):
        return self._mk(eng, fn, deps, 1)

    def dma(self, fn, n, deps=()):
        return self._mk("sync", fn, deps, 16 * n)

    def replay(self, blk, sems):
        def runner(eng_name):
            tasks = self.tasks[eng_name]

            def run(eng):
                for waits, fn in tasks:
                    for (se, v) in waits:
                        eng.wait_ge(sems[se], v)
                    r = fn(eng)
                    rl = r if isinstance(r, (list, tuple)) else [r]
                    if eng_name == "sync":
                        for inst in rl:
                            inst.then_inc(sems["sync"], 16)
                    else:
                        rl[-1].then_inc(sems[eng_name], 1)
            return run
        blk.sync(runner("sync"))
        blk.tensor(runner("tensor"))
        blk.vector(runner("vector"))
        blk.scalar(runner("scalar"))
        blk.gpsimd(runner("gpsimd"))


def build_program():
    nc = bass.Bass("TRN2", target_bir_lowering=False,
                   dynamic_dma_scratch_size=2048)

    dx0 = nc.dram_tensor("x0r", [NCH * CH, D], F32, kind="ExternalInput")
    dw1t = nc.dram_tensor("w1t", [NEXP, D, DFF], BF16, kind="ExternalInput")
    dw2t = nc.dram_tensor("w2t", [NEXP, D, D], BF16, kind="ExternalInput")
    db1c = nc.dram_tensor("b1c", [CH, 128], F32, kind="ExternalInput")
    db2r = nc.dram_tensor("b2r", [NEXP, D], BF16, kind="ExternalInput")
    dgwt = nc.dram_tensor("gwt", [D, NEXP], BF16, kind="ExternalInput")
    dgbc = nc.dram_tensor("gbc", [CH, NEXP], F32, kind="ExternalInput")
    dnwc = nc.dram_tensor("nwc", [CH, D], F32, kind="ExternalInput")
    dwq = nc.dram_tensor("wqf", [D, D], F32, kind="ExternalInput")
    dwk = nc.dram_tensor("wkf", [D, D], F32, kind="ExternalInput")
    dwv = nc.dram_tensor("wvf", [D, D], F32, kind="ExternalInput")
    dwo = nc.dram_tensor("wof", [D, D], F32, kind="ExternalInput")
    drope = nc.dram_tensor("rope", [NCH * CH, 64], F32, kind="ExternalInput")
    dsmk = nc.dram_tensor("smask", [CH, NCH], F32, kind="ExternalInput")
    dtri = nc.dram_tensor("tri", [CH, CH], F32, kind="ExternalInput")
    diota = nc.dram_tensor("iota8", [CH, NEXP], F32, kind="ExternalInput")
    dio9 = nc.dram_tensor("io9", [CH, NEXP], F32, kind="ExternalInput")
    dout = nc.dram_tensor("xout", [N_OWN * CH, D], F32, kind="ExternalOutput")
    swb = nc.dram_tensor("wbf_scratch", [4, D, D], BF16, kind="Internal")

    ctx = ExitStack()

    def sbuf(name, shape, dtype):
        return ctx.enter_context(nc.sbuf_tensor(name, shape, dtype))

    NF = sbuf("NF", [128, 8192], BF16)
    ACC = sbuf("ACCb", [128, 8192], BF16)
    WA = sbuf("WA", [128, 24576], BF16)
    X1O = sbuf("X1O", [128, 16384], F32)
    DS = sbuf("DSo", [128, N_OWN * 520], BF16)
    SRUN = sbuf("SRUN", [128, 520], F32)
    SBF = sbuf("SBF", [128, 520], BF16)
    ROPE = sbuf("ROPEt", [128, SBC * 64], F32)
    AFM = sbuf("ACTFM", [128, 4096], BF16)
    X0P = sbuf("X0P", [128, 2048], F32)
    SC1 = sbuf("SC1", [128, 1024], F32)
    SC2 = sbuf("SC2", [128, 1024], F32)
    SC3 = sbuf("SC3", [128, 1024], F32)
    SB1 = sbuf("SB1", [128, 1024], BF16)
    SB2 = sbuf("SB2", [128, 1024], BF16)
    SB1B = sbuf("SB1B", [128, 1024], BF16)
    SB2B = sbuf("SB2B", [128, 1024], BF16)
    Y2FM = sbuf("Y2FM", [128, 1024], BF16)
    QFM = sbuf("QFM", [128, 1024], BF16)
    KFM = sbuf("KFM", [128, 1024], BF16)
    V65 = sbuf("V65", [128, 1040], BF16)
    ABUF = sbuf("ABUF", [128, 256], BF16)
    CTB = sbuf("CTB", [128, 128], BF16)
    COMB = sbuf("COMB", [128, 64], F32)
    SMALL = sbuf("SMALL", [128, 64], F32)
    VEC = sbuf("VEC", [128, 24], F32)
    cNWC = sbuf("cNWC", [128, 1024], F32)
    cGWT = sbuf("cGWT", [128, 64], BF16)
    cGBC = sbuf("cGBC", [128, 8], F32)
    cB1C = sbuf("cB1C", [128, 128], F32)
    cB2R = sbuf("cB2R", [128, 1024], BF16)
    cIOTA = sbuf("cIOTA", [128, 8], F32)
    cIO9 = sbuf("cIO9", [128, 8], F32)
    cTRI = sbuf("cTRI", [128, 128], F32)
    cIDB = sbuf("cIDB", [128, 128], BF16)
    cIDF = sbuf("cIDF", [128, 128], F32)
    cSMK = sbuf("cSMK", [128, 64], F32)

    pA = ctx.enter_context(nc.psum_tensor("pA", [128, 512], F32))
    pB = ctx.enter_context(nc.psum_tensor("pB", [128, 512], F32))
    pC = ctx.enter_context(nc.psum_tensor("pC", [128, 1024], F32))
    pD = ctx.enter_context(nc.psum_tensor("pD", [128, 1024], F32))
    pT = ctx.enter_context(nc.psum_tensor("pT", [128, 512], F32))
    pM = ctx.enter_context(nc.psum_tensor("pM", [128, 456], F32))

    sems = {e: ctx.enter_context(nc.semaphore("sem_" + e)) for e in Sched.ENGS}
    S = Sched()

    vRED, vM, vVAR, vRSTD, vR2, vDD, vEE, vW1, vW2, vRD, vRR = range(11)

    def vec(i):
        return VEC[:, i:i + 1]

    st = {}

    def f_ld_const(eng):
        return [eng.dma_start(cNWC[:], dnwc[:]),
                eng.dma_start(cGWT[:].rearrange("p (k e) -> p k e", e=NEXP),
                              dgwt.rearrange("(k p) e -> p k e", p=128)),
                eng.dma_start(cGBC[:], dgbc[:]),
                eng.dma_start(cB1C[:], db1c[:]),
                eng.dma_start(cB2R[0:8, :], db2r[:]),
                eng.dma_start(cIOTA[:], diota[:]),
                eng.dma_start(cIO9[:], dio9[:]),
                eng.dma_start(cTRI[:], dtri[:]),
                eng.dma_start(cSMK[:], dsmk[:])]
    tok_const = S.dma(f_ld_const, 9)

    def f_ident_a(eng):
        eng.memset(cIDB[:], 1.0)
        eng.memset(cIDF[:], 1.0)
        eng.memset(V65[:], 1.0)
        eng.memset(VEC[:, 15:16], 1.0)
        eng.memset(SRUN[:], 0.0)
        eng.memset(VEC[:, 11:12], 1.0)
        return eng.affine_select(cIDF[:], cIDF[:], [[-1, 128]], ALU.is_equal,
                                 0.0, base=0, channel_multiplier=1)
    tok_ident_a = S.t("gpsimd", f_ident_a)

    def f_ident_b(eng):
        return eng.affine_select(cIDB[:], cIDB[:], [[-1, 128]], ALU.is_equal,
                                 0.0, base=0, channel_multiplier=1)
    tok_ident = S.t("gpsimd", f_ident_b, [tok_ident_a])

    prev = tok_const
    for wi, dw in enumerate((dwq, dwk, dwv, dwo)):
        def f_ldw(eng, dw=dw):
            return eng.dma_start(
                X1O[:, 0:8192].rearrange("p (k o) -> p k o", o=1024),
                dw.rearrange("(k p) o -> p k o", p=128))
        tok_l = S.dma(f_ldw, 1, [prev])

        def f_cv(eng):
            return eng.tensor_copy(NF[:, 0:8192], X1O[:, 0:8192])
        tok_c = S.t("vector", f_cv, [tok_l])

        def f_stw(eng, wi=wi):
            return eng.dma_start(
                swb[wi].rearrange("(k p) o -> p k o", p=128),
                NF[:, 0:8192].rearrange("p (k o) -> p k o", o=1024))
        prev = S.dma(f_stw, 1, [tok_c])
    tok_wconv = prev

    def transpose8(src_bf, dst_fm, dep, extra=()):
        atok = None
        for g in (0, 4):
            def fpe(eng, g=g, src=src_bf):
                return [eng.matmul(pT[:, j * 128:(j + 1) * 128],
                                   src[:, (g + j) * 128:(g + j + 1) * 128],
                                   cIDB[:], start=True, stop=True)
                        for j in range(4)]
            d = [dep, tok_ident, st.get("pT")] + list(extra)
            ptok = S.t("tensor", fpe, d)

            def fac(eng, g=g, dst=dst_fm):
                return eng.copy(dst[:, g * 128:(g + 4) * 128], pT[:])
            atok = S.t("scalar", fac, [ptok])
            st["pT"] = atok
        return atok

    def transpose8_strided(src_bf, c_local, dep):
        atok = None
        for g in (0, 4):
            def fpe(eng, g=g, src=src_bf):
                return [eng.matmul(pT[:, j * 128:(j + 1) * 128],
                                   src[:, (g + j) * 128:(g + j + 1) * 128],
                                   cIDB[:], start=True, stop=True)
                        for j in range(4)]
            ptok = S.t("tensor", fpe, [dep, tok_ident, st.get("pT")])

            def fac(eng, g=g, c_local=c_local):
                ov = NF[:].rearrange("p (k t) -> p k t", t=1024)[
                    :, g:g + 4, c_local * 128:(c_local + 1) * 128]
                iv = pT[:].rearrange("p (k t) -> p k t", t=128)
                return eng.copy(ov, iv)
            atok = S.t("scalar", fac, [ptok])
            st["pT"] = atok
        return atok

    def prepass_A(x_tok, x_ap, xc_ap, xm_buf, xm_key, nf_dst, nf_key,
                  extra_deps=()):
        LN2 = VEC[:, 16:18]

        def fa1(eng, x_ap=x_ap, xm_buf=xm_buf):
            eng.tensor_tensor(xm_buf, x_ap, x_ap, ALU.mult)
            eng.reduce_sum(vec(vRED), x_ap, AX.X)
            eng.reduce_sum(vec(vVAR), xm_buf, AX.X)
            eng.tensor_scalar(vec(vM), vec(vRED), 1.0 / D, None, ALU.mult)
            eng.tensor_copy(VEC[:, 12:13], VEC[:, 15:16])
            eng.tensor_tensor(VEC[:, 18:19], vec(vM), vec(vM), ALU.mult)
            eng.tensor_scalar(vec(vVAR), vec(vVAR), 1.0 / D, None, ALU.mult)
            eng.tensor_copy(VEC[:, 12:13], VEC[:, 15:16])
            eng.tensor_tensor(vec(vVAR), vec(vVAR), VEC[:, 18:19], ALU.subtract)
            eng.tensor_copy(VEC[:, 12:13], VEC[:, 15:16])
            eng.tensor_scalar(LN2[:, 0:1], vec(vVAR), EPS, None, ALU.add)
            eng.tensor_copy(VEC[:, 12:13], VEC[:, 15:16])
            eng.reciprocal(VEC[:, 19:20], LN2[:, 0:1])
            eng.tensor_copy(VEC[:, 12:13], VEC[:, 15:16])
            return eng.tensor_scalar(LN2[:, 1:2], vec(vVAR), VEC[:, 19:20],
                                     EPS, ALU.mult, ALU.add)
        ta1 = S.t("vector", fa1, [x_tok, st.get(xm_key)] + list(extra_deps))

        def fa2(eng):
            eng.activation(LN2[:], LN2[:], AF.Ln)
            return eng.activation(LN2[:], LN2[:], AF.Exp, scale=-0.5)
        ta2 = S.t("scalar", fa2, [ta1])

        def fa3(eng, xc_ap=xc_ap, nf_dst=nf_dst, x_ap=x_ap):
            eng.tensor_scalar(xc_ap, x_ap, vec(vM), LN2[:, 0:1],
                              ALU.subtract, ALU.mult)
            eng.tensor_copy(VEC[:, 12:13], VEC[:, 15:16])
            return eng.scalar_tensor_tensor(nf_dst, xc_ap, LN2[:, 1:2],
                                            cNWC[:], ALU.mult, ALU.mult)
        ta3 = S.t("vector", fa3, [ta2, st.get(nf_key), tok_const])
        st[xm_key] = ta3
        st[nf_key] = ta3
        return ta3

    def prepass_B(c_local, nf_src, nf_tok, xc_ap):
        tok_nf = transpose8_strided(nf_src, c_local, nf_tok)

        def f7(eng, c_local=c_local):
            return [eng.matmul(
                pM[:, 0:8],
                NF[:, k * 1024 + c_local * 128: k * 1024 + (c_local + 1) * 128],
                cGWT[:, k * 8:(k + 1) * 8],
                start=(k == 0), stop=(k == 7)) for k in range(8)]
        t7 = S.t("tensor", f7, [tok_nf, tok_const, st.get("pMg"),
                                st.get("pMb")])

        def f8(eng):
            GAP = lambda: eng.tensor_copy(VEC[:, 12:13], VEC[:, 15:16])
            lgb = SMALL[:, 8:12].bitcast(BF16)[:, 0:8]
            eng.tensor_tensor(lgb, pM[:, 0:8], cGBC[:], ALU.add)
            GAP()
            lgf = SMALL[:, 16:24]
            eng.tensor_copy(lgf, lgb)
            GAP()
            m8 = SMALL[:, 0:8]
            eng.max(out=m8, in_=lgf)
            GAP()
            e1 = SMALL[:, 24:32]
            eng.tensor_scalar(e1, lgf, m8[:, 0:1], None, ALU.is_equal)
            GAP()
            tq = SMALL[:, 32:40]
            eng.tensor_tensor(tq, e1, cIO9[:], ALU.mult)
            GAP()
            eng.tensor_scalar(tq, tq, 9.0, None, ALU.add)
            GAP()
            eng.tensor_reduce(vec(vDD), tq, AX.X, ALU.min)
            eng.tensor_tensor(vec(vEE), m8[:, 1:2], m8[:, 0:1], ALU.subtract)
            m1 = SMALL[:, 40:48]
            eng.tensor_scalar(m1, cIOTA[:], vec(vDD), None, ALU.is_equal)
            GAP()
            lg2 = SMALL[:, 48:56]
            eng.scalar_tensor_tensor(lg2, m1, -1.0e30, lgf, ALU.mult, ALU.add)
            GAP()
            eng.tensor_scalar(e1, lg2, m8[:, 1:2], None, ALU.is_equal)
            GAP()
            eng.tensor_tensor(tq, e1, cIO9[:], ALU.mult)
            GAP()
            eng.tensor_scalar(tq, tq, 9.0, None, ALU.add)
            GAP()
            eng.tensor_reduce(vec(vDD), tq, AX.X, ALU.min)
            eng.tensor_copy(vec(vRD), vec(vEE))
            m2 = SMALL[:, 56:64]
            return eng.tensor_scalar(m2, cIOTA[:], vec(vDD), None,
                                     ALU.is_equal)
        t8 = S.t("vector", f8, [t7, tok_const])
        st["pMg"] = t8
        st["pMb"] = t8

        def f9(eng):
            return eng.activation(vec(vEE), vec(vEE), AF.Exp)
        t9 = S.t("scalar", f9, [t8])

        def f10(eng, c_local=c_local):
            den = vec(vRD)
            eng.tensor_scalar(den, vec(vEE), 1.0, None, ALU.add)
            eng.tensor_copy(VEC[:, 12:13], VEC[:, 15:16])
            eng.reciprocal(vec(vW1), den)
            eng.tensor_copy(VEC[:, 12:13], VEC[:, 15:16])
            eng.tensor_tensor(vec(vW2), vec(vEE), vec(vW1), ALU.mult)
            m1 = SMALL[:, 40:48]
            m2 = SMALL[:, 56:64]
            cmb = COMB[:, c_local * 8:(c_local + 1) * 8]
            tq = SMALL[:, 32:40]
            eng.tensor_scalar(tq, m1, vec(vW1), None, ALU.mult)
            eng.tensor_copy(VEC[:, 12:13], VEC[:, 15:16])
            return eng.scalar_tensor_tensor(cmb, m2, vec(vW2), tq,
                                            ALU.mult, ALU.add)
        t10 = S.t("vector", f10, [t9, st.get("COMB")])

        def f11(eng, c_local=c_local):
            return eng.transpose(pM[0:8, 72:200],
                                 COMB[:, c_local * 8:(c_local + 1) * 8], cIDF[:])
        t11 = S.t("tensor", f11, [t10, tok_ident_a, tok_ident,
                                  st.get("pMct"), st.get("pMb")])

        def f12(eng):
            return eng.tensor_copy(CTB[0:8, :], pM[0:8, 72:200])
        t12 = S.t("vector", f12, [t11, st.get("CTB")])
        st["pMct"] = t12
        st["pMb"] = t12

        def f13(eng):
            return [eng.matmul(pC[:, hh * 512:(hh + 1) * 512], CTB[0:8, :],
                               cB2R[0:8, hh * 512:(hh + 1) * 512],
                               start=True, stop=True) for hh in range(2)]
        t13 = S.t("tensor", f13, [t12, tok_const, st.get("pC"),
                                  st.get("pCh0"), st.get("pCh1")])
        st["CTB"] = t13

        def f14(eng, xc_ap=xc_ap):
            return eng.tensor_tensor(xc_ap, xc_ap, pC[:], ALU.add)
        t14 = S.t("vector", f14, [t13])
        st["pC"] = t14
        st["pCh0"] = t14
        st["pCh1"] = t14
        return t14, tok_nf

    def ffn_experts_sb(tok_nf_blk, comb_tok, tok_w1_free, tok_w2_free,
                       mid_cb=None):
        ep_tok = [None] * SBC
        tok_w1_read = None
        tok_w2_read = None
        for e in range(NEXP):
            def f_w1(eng, e=e):
                return eng.dma_start(
                    WA[:, 0:16384].rearrange("p (k o) -> p k o", o=2048),
                    dw1t[e].rearrange("(k p) o -> p k o", p=128))
            tok_lw1 = S.dma(f_w1, 1, [tok_w1_read if e > 0 else tok_w1_free])

            def f_w2(eng, e=e):
                return eng.dma_start(
                    WA[:, 16384:24576].rearrange("p (k o) -> p k o", o=1024),
                    dw2t[e].rearrange("(k p) o -> p k o", p=128))
            tok_lw2 = S.dma(f_w2, 1, [tok_w2_read if e > 0 else tok_w2_free])

            for blk in range(2):
                pool_last = None
                tE3_vals = []
                for jp in range(8):
                    if jp % 2 == 0:
                        pg, pl = pA[:], pB[:]
                        kg, kl = "pA", "pB"
                    else:
                        pg, pl = pC[:, 0:512], pC[:, 512:1024]
                        kg, kl = "pCh0", "pCh1"

                    def fG(eng, blk=blk, jp=jp, pg=pg):
                        return [eng.matmul(
                            pg,
                            WA[:, k * 2048 + jp * 128: k * 2048 + (jp + 1) * 128],
                            NF[:, k * 1024 + blk * 512: k * 1024 + (blk + 1) * 512],
                            start=(k == 0), stop=(k == 7)) for k in range(8)]
                    tG = S.t("tensor", fG, [tok_lw1, tok_nf_blk[blk],
                                            st.get(kg), st.get("pC") if jp % 2 else None])

                    def fL(eng, blk=blk, jp=jp, pl=pl):
                        return [eng.matmul(
                            pl,
                            WA[:, k * 2048 + 1024 + jp * 128: k * 2048 + 1024 + (jp + 1) * 128],
                            NF[:, k * 1024 + blk * 512: k * 1024 + (blk + 1) * 512],
                            start=(k == 0), stop=(k == 7)) for k in range(8)]
                    tL = S.t("tensor", fL, [st.get(kl),
                                            st.get("pC") if jp % 2 else None])
                    tok_w1_read = tL

                    par = jp % 2
                    g1 = SC1[:, par * 512:(par + 1) * 512]
                    l1 = SC2[:, par * 512:(par + 1) * 512]
                    sg = SC3[:, par * 512:(par + 1) * 512]

                    def fE1(eng, e=e, jp=jp, g1=g1, l1=l1, pg=pg, pl=pl):
                        eng.tensor_scalar(
                            l1, pl,
                            cB1C[:, e * 16 + 8 + jp: e * 16 + 8 + jp + 1],
                            LIMIT, ALU.add, ALU.min)
                        eng.tensor_scalar(g1, pg,
                                          cB1C[:, e * 16 + jp: e * 16 + jp + 1],
                                          LIMIT, ALU.add, ALU.min)
                        return eng.tensor_scalar(l1, l1, -LIMIT, 1.0,
                                                 ALU.max, ALU.add)
                    tE1 = S.t("vector", fE1, [tG, tL, tok_const,
                                              st.get(("m1sc", par))])
                    st[kg] = tE1
                    st[kl] = tE1

                    def fE2(eng, sg=sg, g1=g1):
                        return eng.activation(sg, g1, AF.Sigmoid, scale=ALPHA)
                    tE2 = S.t("scalar", fE2, [tE1])

                    def fE3(eng, g1=g1, l1=l1, sg=sg, jp=jp):
                        eng.tensor_tensor(g1, g1, sg, ALU.mult)
                        eng.tensor_copy(VEC[:, 11:12], VEC[:, 15:16])
                        return eng.tensor_tensor(AFM[:, jp * 512:(jp + 1) * 512],
                                                 g1, l1, ALU.mult)
                    tE3 = S.t("gpsimd", fE3, [tE2, st.get("AFM")])
                    st[("m1sc", par)] = tE3
                    pool_last = tE3
                    tE3_vals.append(tE3[1])

                for tch in range(4):
                    c_local = blk * 4 + tch
                    for hh in range(2):
                        ph = pD[:, hh * 512:(hh + 1) * 512]
                        pk = "pDh%d" % hh

                        def fM2(eng, tch=tch, hh=hh, ph=ph, tv=tuple(tE3_vals)):
                            r = []
                            for k2 in range(8):
                                eng.wait_ge(sems["gpsimd"], tv[k2])
                                r.append(eng.matmul(
                                    ph,
                                    AFM[:, k2 * 512 + tch * 128: k2 * 512 + (tch + 1) * 128],
                                    WA[:, 16384 + k2 * 1024 + hh * 512: 16384 + k2 * 1024 + (hh + 1) * 512],
                                    start=(k2 == 0), stop=(k2 == 7)))
                            return r
                        tM2 = S.t("tensor", fM2, [tok_lw2,
                                                  st.get(pk), st.get("pD")])
                        tok_w2_read = tM2

                        def fEp(eng, e=e, c_local=c_local, hh=hh, ph=ph):
                            a = ACC[:, c_local * 1024 + hh * 512:
                                    c_local * 1024 + (hh + 1) * 512]
                            return eng.scalar_tensor_tensor(
                                a, ph,
                                COMB[:, c_local * 8 + e: c_local * 8 + e + 1],
                                a, ALU.mult, ALU.add)
                        tEp = S.t("vector", fEp, [tM2, comb_tok[c_local]])
                        st[pk] = tEp
                        ep_tok[c_local] = tEp
                st["AFM"] = tok_w2_read
                if mid_cb is not None and e == 0 and blk == 0:
                    mid_cb()
                    mid_cb = None
        st["COMB"] = ep_tok[SBC - 1]
        st["ACC"] = ep_tok[SBC - 1]
        st["SC1"] = pool_last
        st["SC2"] = pool_last
        st["SC3"] = pool_last
        return tok_w1_read, tok_w2_read, ep_tok

    def ln_to_y2b(x_tok, x_ap, y2dst=None, y2key="SB1", extra_deps=(),
                  scratch=None):
        if scratch is None:
            bA, bB, kA, kB = SC1[:], SC2[:], "SC1", "SC2"
        else:
            bA, bB, kA, kB = scratch

        def f1(eng, x_ap=x_ap, bA=bA, bB=bB):
            eng.reduce_sum(vec(vRED), x_ap, AX.X)
            eng.tensor_tensor(bB, x_ap, x_ap, ALU.mult)
            eng.tensor_scalar(vec(vM), vec(vRED), 1.0 / D, None, ALU.mult)
            eng.reduce_sum(vec(vVAR), bB, AX.X)
            eng.tensor_scalar(bA, x_ap, vec(vM), None, ALU.subtract)
            eng.tensor_tensor(vec(vDD), vec(vM), vec(vM), ALU.mult)
            eng.tensor_scalar(vec(vVAR), vec(vVAR), 1.0 / D, EPS,
                              ALU.mult, ALU.add)
            eng.tensor_copy(VEC[:, 12:13], VEC[:, 15:16])
            return eng.tensor_tensor(vec(vVAR), vec(vVAR), vec(vDD),
                                     ALU.subtract)
        t1 = S.t("vector", f1, [x_tok, st.get(kA), st.get(kB)] + list(extra_deps))
        if _CFG['pre_upto'] <= 1:
            return t1, t1, t1

        def f2(eng):
            eng.activation(vec(vRSTD), vec(vVAR), AF.Ln)
            return eng.activation(vec(vRSTD), vec(vRSTD), AF.Exp, scale=-0.5)
        t2 = S.t("scalar", f2, [t1])

        if y2dst is None:
            y2dst = SB1[:]

        def f3(eng, y2dst=y2dst, bA=bA):
            return eng.tensor_scalar(y2dst, bA, vec(vRSTD), None, ALU.mult)
        t3 = S.t("vector", f3, [t2, st.get(y2key)])
        st[kA] = t3
        st[kB] = t1
        return t3

    def proj16(dst_psum, pkey, y2fm_tok, w_sb, w_base, w_tok):
        def f(eng, dst=dst_psum, w_sb=w_sb, w_base=w_base):
            r = []
            for oh in range(2):
                for k in range(8):
                    r.append(eng.matmul(
                        dst[:, oh * 512:(oh + 1) * 512],
                        Y2FM[:, k * 128:(k + 1) * 128],
                        w_sb[:, w_base + k * 1024 + oh * 512: w_base + k * 1024 + (oh + 1) * 512],
                        start=(k == 0), stop=(k == 7)))
            return r
        tok = S.t("tensor", f, [y2fm_tok, w_tok, st.get(pkey),
                                st.get(pkey + "h0"), st.get(pkey + "h1")])
        return tok

    def rope_relu(p_src, dst_bf, dst_key, c_local, rope_tok, proj_tok,
                  scratch=None):
        co = c_local * 64
        if scratch is None:
            bA, bB, kA, kB = SC1[:], SC2[:], "SC1", "SC2"
        else:
            bA, bB, kA, kB = scratch
        P1v = p_src[:].rearrange("p (h a i) -> p h a i", a=2, i=32)[:, :, 0, :]
        P2v = p_src[:].rearrange("p (h a i) -> p h a i", a=2, i=32)[:, :, 1, :]
        CC = ROPE[:, co:co + 32].unsqueeze(1).to_broadcast([128, 16, 32])
        SS = ROPE[:, co + 32:co + 64].unsqueeze(1).to_broadcast([128, 16, 32])

        def hv(buf, half):
            return buf[:, half * 512:(half + 1) * 512].rearrange(
                "p (h i) -> p h i", i=32)

        def f1(eng):
            eng.tensor_tensor(hv(bA, 0), P1v, CC, ALU.mult)
            eng.tensor_tensor(hv(bA, 1), P2v, SS, ALU.mult)
            eng.tensor_tensor(hv(bB, 0), P1v, SS, ALU.mult)
            return eng.tensor_tensor(hv(bB, 1), P2v, CC, ALU.mult)
        t1 = S.t("vector", f1, [proj_tok, rope_tok, st.get(kA),
                                st.get(kB)])

        D1 = dst_bf[:].rearrange("p (h a i) -> p h a i", a=2, i=32)[:, :, 0, :]
        D2 = dst_bf[:].rearrange("p (h a i) -> p h a i", a=2, i=32)[:, :, 1, :]

        def f2(eng):
            eng.tensor_tensor(hv(bA, 0), hv(bA, 0), hv(bA, 1), ALU.subtract)
            eng.tensor_tensor(hv(bB, 0), hv(bB, 0), hv(bB, 1), ALU.add)
            eng.tensor_scalar(D1, hv(bA, 0), 0.0, None, ALU.max)
            return eng.tensor_scalar(D2, hv(bB, 0), 0.0, None, ALU.max)
        t2 = S.t("gpsimd", f2, [t1, st.get(dst_key)])
        st[kA] = t2
        st[kB] = t2
        return t2, t1

    x0_par_reader = [None, None]
    tok_rope_free = None
    tok_srun = None
    tok_w1_free = tok_wconv
    tok_w2_free = tok_wconv
    own_x1_tok = [None] * N_OWN
    tok_ds_store = [None] * N_OWN
    tok_wk = tok_wv = None
    tok_fin_last_pe = None

    for sb_i in range(_CFG['p1_sbs']):
        def f_rope(eng, sb_i=sb_i):
            return eng.dma_start(
                ROPE[:].rearrange("p (c z) -> p c z", z=64),
                drope[sb_i * 1024:(sb_i + 1) * 1024, :].rearrange(
                    "(c p) z -> p c z", p=128))
        tok_rope = S.dma(f_rope, 1, [tok_rope_free])

        tok_acc = [None] * SBC
        tok_nf_blk = [None, None]
        preA = {}
        NF3 = [(SB2, "bSB2"), (SB2B, "bSB2B"), (SB1B, "bSB1B")]

        def emit_A(c, sb_i=sb_i):
            if c in preA or c >= SBC:
                return
            g = sb_i * SBC + c
            par = c % 2
            tri = c % 3

            def f_x0(eng, g=g, par=par):
                return eng.dma_start(X0P[:, par * 1024:(par + 1) * 1024],
                                     dx0[g * 128:(g + 1) * 128, :])
            tok_x0 = S.dma(f_x0, 1, [x0_par_reader[par]])
            x_ap = X0P[:, par * 1024:(par + 1) * 1024]
            nf_dst, nf_key = NF3[tri]
            ta3 = prepass_A(tok_x0, x_ap, ACC[:, c * 1024:(c + 1) * 1024],
                            SB1[:], "bSB1", nf_dst[:], nf_key)
            x0_par_reader[par] = ta3
            preA[c] = ta3

        def emit_B(c):
            tri = c % 3
            nf_src, nf_key = NF3[tri]
            t_acc, t_nf = prepass_B(c, nf_src[:], preA[c],
                                    ACC[:, c * 1024:(c + 1) * 1024])
            st[nf_key] = t_nf
            tok_acc[c] = t_acc
            if c == 3:
                tok_nf_blk[0] = t_nf
            if c == SBC - 1:
                tok_nf_blk[1] = t_nf

        def emit_pre(c0, c1, sb_i=sb_i):
            for c in range(c0, c1):
                emit_A(c)
                emit_A(c + 1)
                emit_A(c + 2)
                emit_B(c)

        emit_pre(0, 4)
        if _CFG['experts']:
            tok_w1r, tok_w2r, ep_tok = ffn_experts_sb(
                tok_nf_blk, tok_acc, tok_w1_free, tok_w2_free,
                mid_cb=lambda: emit_pre(4, SBC))
        else:
            emit_pre(4, SBC)
            tok_w1r = tok_w2r = None
            ep_tok = list(tok_acc)

        if not _CFG['finish']:
            tok_w1_free = tok_w1r
            tok_w2_free = tok_w2r
            continue

        def f_wk(eng):
            return eng.dma_start(
                WA[:, 0:8192].rearrange("p (k o) -> p k o", o=1024),
                swb[1].rearrange("(k p) o -> p k o", p=128))

        def f_wv(eng):
            return eng.dma_start(
                WA[:, 8192:16384].rearrange("p (k o) -> p k o", o=1024),
                swb[2].rearrange("(k p) o -> p k o", p=128))
        tok_wk = S.dma(f_wk, 1, [tok_w1r, tok_wconv])
        tok_wv = S.dma(f_wv, 1, [tok_w1r])

        AFMf = AFM[:].bitcast(F32)
        st["AFMa"] = st.get("AFM")
        st["AFMb"] = st.get("AFM")
        pend = None

        def emit_ds(pend):
            g_, own_, oc_, t_kr_, t_v65_ = pend

            def f_ds(eng):
                r = []
                for h in range(16):
                    p = h // 2
                    dst = pA if p < 4 else pB
                    r.append(eng.matmul(
                        dst[(h % 2) * 64:(h % 2) * 64 + 64,
                            (p % 4) * 65:(p % 4) * 65 + 65],
                        SB2[:, h * 64:(h + 1) * 64],
                        V65[:, h * 65:(h + 1) * 65],
                        start=True, stop=True))
                return r
            t_ds = S.t("tensor", f_ds, [t_kr_, t_v65_, st.get("pA"),
                                        st.get("pB")])
            st["SB2"] = t_ds
            st["V65"] = t_ds

            def f_su(eng, g=g_):
                eng.scalar_tensor_tensor(SRUN[:, 0:260], pA[:, 0:260],
                                         cSMK[:, g:g + 1], SRUN[:, 0:260],
                                         ALU.mult, ALU.add)
                return eng.scalar_tensor_tensor(SRUN[:, 260:520], pB[:, 0:260],
                                                cSMK[:, g:g + 1],
                                                SRUN[:, 260:520],
                                                ALU.mult, ALU.add)
            t_su = S.t("vector", f_su, [t_ds, tok_const])
            st["pA"] = t_su
            st["pB"] = t_su
            if own_:
                def f_dss(eng, oc=oc_):
                    eng.tensor_copy(DS[:, oc * 520: oc * 520 + 260],
                                    pA[:, 0:260])
                    return eng.tensor_copy(
                        DS[:, oc * 520 + 260: oc * 520 + 520], pB[:, 0:260])
                t_dss = S.t("vector", f_dss, [t_ds])
                tok_ds_store[oc_] = t_dss
                st["pA"] = t_dss
                st["pB"] = t_dss
            return t_ds, t_su

        own = sb_i >= OWN_SB0
        finA = {}

        def emit_Af(c, sb_i=sb_i, own=own):
            g = sb_i * SBC + c
            par = c % 2
            oc = (sb_i - OWN_SB0) * SBC + c if own else None

            def f_x0f(eng, g=g, par=par):
                return eng.dma_start(X0P[:, par * 1024:(par + 1) * 1024],
                                     dx0[g * 128:(g + 1) * 128, :])
            tok_x0f = S.dma(f_x0f, 1, [x0_par_reader[par]])

            if own:
                x1_ap = X1O[:, oc * 1024:(oc + 1) * 1024]
            else:
                x1_ap = X1O[:, par * 1024:(par + 1) * 1024]

            def f_x1(eng, c=c, par=par, x1_ap=x1_ap):
                return eng.scalar_tensor_tensor(
                    x1_ap, ACC[:, c * 1024:(c + 1) * 1024], 0.5,
                    X0P[:, par * 1024:(par + 1) * 1024], ALU.mult, ALU.add)
            t_x1 = S.t("vector", f_x1, [tok_x0f, ep_tok[c]])
            x0_par_reader[par] = t_x1
            if own:
                own_x1_tok[oc] = t_x1
            y2dst = (SB1 if par == 0 else SB1B)[:]
            t_y2b = ln_to_y2b(t_x1, x1_ap, y2dst, "bSB1" if par == 0 else "bSB1B")
            finA[c] = (t_y2b, y2dst, par)

        def emit_Bf(c, sb_i=sb_i, own=own):
            nonlocal pend
            g = sb_i * SBC + c
            oc = (sb_i - OWN_SB0) * SBC + c if own else None
            t_y2b, y2dst, par = finA[c]
            t_y2fm = transpose8(y2dst, Y2FM[:], t_y2b)
            st["bSB1" if par == 0 else "bSB1B"] = t_y2fm

            t_kp = proj16(pC, "pC", t_y2fm, WA, 0, tok_wk)
            if pend is not None:
                emit_ds(pend)
            t_kr, t_kread = rope_relu(
                pC, SB2[:], "SB2", c, tok_rope, t_kp,
                scratch=(AFMf[:, 0:1024], AFMf[:, 1024:2048],
                         "AFMa", "AFMb"))
            st["pC"] = t_kread
            st["pCh0"] = t_kread
            st["pCh1"] = t_kread

            t_vp = proj16(pD, "pD", t_y2fm, WA, 8192, tok_wv)

            def f_v65(eng):
                pv = pD[:].rearrange("p (h d) -> p h d", d=64)
                ov = V65[:].rearrange("p (h z) -> p h z", z=65)[:, :, 0:64]
                return eng.tensor_copy(ov, pv)
            t_v65 = S.t("vector", f_v65, [t_vp, st.get("V65")])
            st["pD"] = t_v65
            st["pDh0"] = t_v65
            st["pDh1"] = t_v65
            pend = (g, own, oc, t_kr, t_v65)

        for c in range(SBC):
            if c == 0:
                emit_Af(0)
            if c + 1 < SBC:
                emit_Af(c + 1)
            emit_Bf(c)

        tok_fin_last_pe, tok_srun = emit_ds(pend)
        st["AFM"] = st["AFMa"]
        tok_rope_free = tok_fin_last_pe
        tok_w1_free = tok_fin_last_pe
        tok_w2_free = tok_w2r

    full_p1 = (_CFG['p1_sbs'] == NSB and _CFG['experts'] and _CFG['finish'])
    do_p2 = _CFG['p2'] and full_p1
    do_p3 = _CFG['p3'] and full_p1

    if do_p2:
        def f_wq(eng):
            return eng.dma_start(
                NF[:, 0:8192].rearrange("p (k o) -> p k o", o=1024),
                swb[0].rearrange("(k p) o -> p k o", p=128))
        tok_wq = S.dma(f_wq, 1, [tok_w1r, tok_wconv])

        def f_wo(eng):
            return eng.dma_start(
                ACC[:, 0:8192].rearrange("p (k o) -> p k o", o=1024),
                swb[3].rearrange("(k p) o -> p k o", p=128))
        tok_wo = S.dma(f_wo, 1, [own_x1_tok[N_OWN - 1]])

    tok_out_x2 = [None] * N_OWN
    tok_sbf_reader = None
    tok_rope = None
    p2_count = min(_CFG['p2_n'], N_OWN) if do_p2 else 0
    p2lnA = {}
    pend_x2 = [None]

    def emit_x2():
        poc, t_wop_p = pend_x2[0]
        pend_x2[0] = None

        def f_x2(eng, poc=poc):
            a = X1O[:, poc * 1024:(poc + 1) * 1024]
            return eng.tensor_tensor(a, a, pD[:], ALU.add)
        t_x2 = S.t("vector", f_x2, [t_wop_p])
        st["pD"] = t_x2
        st["pDh0"] = t_x2
        st["pDh1"] = t_x2
        tok_out_x2[poc] = t_x2
    AFM_SCR = (AFMf[:, 0:1024], AFMf[:, 1024:2048], "AFMa", "AFMb")
    for oc in range(p2_count):
        sb_i = OWN_SB0 + oc // SBC
        c_local = oc % SBC
        if c_local == 0:
            def f_rope2(eng, sb_i=sb_i):
                return eng.dma_start(
                    ROPE[:].rearrange("p (c z) -> p c z", z=64),
                    drope[sb_i * 1024:(sb_i + 1) * 1024, :].rearrange(
                        "(c p) z -> p c z", p=128))
            tok_rope = S.dma(f_rope2, 1, [tok_rope_free])

        x2_ap = X1O[:, oc * 1024:(oc + 1) * 1024]
        par = oc % 2
        lnbuf = (SB1 if par == 0 else SB1B)[:]
        lnkey = "bSB1" if par == 0 else "bSB1B"
        if oc in p2lnA:
            t_y2b = p2lnA.pop(oc)
        else:
            t_y2b = ln_to_y2b(own_x1_tok[oc], x2_ap, lnbuf, lnkey,
                              scratch=AFM_SCR)
        t_y2fm = transpose8(lnbuf, Y2FM[:], t_y2b)
        st[lnkey] = t_y2fm

        if _CFG['p2_upto'] <= 1:
            tok_out_x2[oc] = t_y2fm
            continue
        def f_qp(eng):
            r = []
            for oh in range(2):
                for k in range(8):
                    r.append(eng.matmul(
                        pC[:, oh * 512:(oh + 1) * 512],
                        Y2FM[:, k * 128:(k + 1) * 128],
                        NF[:, k * 1024 + oh * 512: k * 1024 + (oh + 1) * 512],
                        start=(k == 0), stop=(k == 7)))
            return r
        t_qp = S.t("tensor", f_qp, [t_y2fm, tok_wq, st.get("pC"),
                                    st.get("pCh0"), st.get("pCh1")])
        if pend_x2[0] is not None:
            emit_x2()
        t_kp = proj16(pD, "pD", t_y2fm, WA, 0, tok_wk)
        t_qr, t_qread = rope_relu(pC, SB2[:], "SB2", c_local, tok_rope, t_qp)
        st["pC"] = t_qread
        st["pCh0"] = t_qread
        st["pCh1"] = t_qread
        t_qfm = transpose8(SB2[:], QFM[:], t_qr)
        st["SB2"] = t_qfm

        if _CFG['p2_upto'] <= 2:
            tok_out_x2[oc] = t_qfm
            continue
        t_kr, t_kread = rope_relu(pD, SB2[:], "SB2", c_local, tok_rope, t_kp)
        st["pD"] = t_kread
        st["pDh0"] = t_kread
        st["pDh1"] = t_kread
        tok_rope_free = t_kr
        if oc + 1 < p2_count:
            npar = (oc + 1) % 2
            p2lnA[oc + 1] = ln_to_y2b(
                own_x1_tok[oc + 1], X1O[:, (oc + 1) * 1024:(oc + 2) * 1024],
                (SB1 if npar == 0 else SB1B)[:],
                "bSB1" if npar == 0 else "bSB1B", scratch=AFM_SCR)

        t_vp = proj16(pC, "pC", t_y2fm, WA, 8192, tok_wv)
        t_kfm = transpose8(SB2[:], KFM[:], t_kr, extra=[st.get("KFM")])
        st["SB2"] = t_kfm

        def f_v65(eng):
            pv = pC[:].rearrange("p (h d) -> p h d", d=64)
            ov = V65[:].rearrange("p (h z) -> p h z", z=65)[:, :, 0:64]
            return eng.tensor_copy(ov, pv)
        t_v65 = S.t("vector", f_v65, [t_vp, st.get("V65")])
        st["pC"] = t_v65
        st["pCh0"] = t_v65
        st["pCh1"] = t_v65

        def f_sbf(eng):
            return eng.tensor_copy(SBF[:], SRUN[:])
        t_sbfr = S.t("vector", f_sbf, [tok_srun, tok_sbf_reader])

        if _CFG['p2_upto'] <= 4:
            tok_out_x2[oc] = t_sbfr
            continue
        SC_AP = [pA[:, 0:128], pC[:, 0:128], pA[:, 128:256], pC[:, 128:256]]
        NM_AP = [pB[:, 0:65], pC[:, 512:577], pB[:, 65:130], pC[:, 577:642]]
        X0Pb = X0P[:].bitcast(BF16)
        AB_AP = [ABUF[:, 0:128], ABUF[:, 128:256],
                 X0Pb[:, 0:128], X0Pb[:, 128:256]]
        t_mk_h = {}
        t_nm_h = {}
        t_ep_h = {}
        n_heads = min(16, _CFG['p2_nh'])

        def emit_sc_mk(h):
            hp4 = h % 4
            f_tile = h // 2
            b0 = (h % 2) * 64
            sc_ap = SC_AP[hp4]

            def f_sc(eng, f_tile=f_tile, b0=b0, sc_ap=sc_ap):
                return eng.matmul(
                    sc_ap,
                    KFM[b0:b0 + 64, f_tile * 128:(f_tile + 1) * 128],
                    QFM[b0:b0 + 64, f_tile * 128:(f_tile + 1) * 128],
                    start=True, stop=True)
            t_sc = S.t("tensor", f_sc, [t_kfm, t_qfm, t_v65,
                                        t_mk_h.get(h - 2)])

            def f_mk(eng, hp4=hp4, sc_ap=sc_ap):
                return eng.tensor_tensor(AB_AP[hp4], sc_ap, cTRI[:],
                                         ALU.mult)
            xg = [x0_par_reader[0], x0_par_reader[1]] \
                if (oc == 0 and hp4 >= 2) else []
            t_mk_h[h] = S.t("vector", f_mk, [t_sc, tok_const] + xg)

        def emit_nm_ep(h):
            hp4 = h % 4
            f_tile = h // 2
            b0 = (h % 2) * 64
            p = h // 2
            nm_ap = NM_AP[hp4]

            def f_nm(eng, h=h, hp4=hp4, f_tile=f_tile, b0=b0, p=p,
                     nm_ap=nm_ap):
                r = [eng.matmul(nm_ap,
                                AB_AP[hp4],
                                V65[:, h * 65:(h + 1) * 65],
                                start=True, stop=False)]
                base = 0 if p < 4 else 260
                sb_t = SBF[b0:b0 + 64,
                           base + (p % 4) * 65: base + (p % 4) * 65 + 65]
                r.append(eng.matmul(nm_ap,
                                    QFM[b0:b0 + 64,
                                        f_tile * 128:(f_tile + 1) * 128],
                                    sb_t, start=False, stop=True))
                return r
            t_nm = S.t("tensor", f_nm, [t_mk_h[h], t_v65, t_sbfr,
                                        t_ep_h.get(h - 2)])
            t_nm_h[h] = t_nm

            def f_ep(eng, h=h, nm_ap=nm_ap):
                eng.tensor_scalar(vec(vRD), nm_ap[:, 64:65],
                                  EPS_ATT, None, ALU.add)
                eng.tensor_copy(VEC[:, 12:13], VEC[:, 15:16])
                eng.reciprocal(vec(vRR), vec(vRD))
                eng.tensor_copy(VEC[:, 12:13], VEC[:, 15:16])
                return eng.tensor_scalar(SC2[:, h * 64:(h + 1) * 64],
                                         nm_ap[:, 0:64], vec(vRR),
                                         None, ALU.mult)
            dep_sc2 = [st.get("SC2")] if h == 0 else []
            t_ep_h[h] = S.t("vector", f_ep, [t_nm] + dep_sc2)

        for h in range(n_heads):
            emit_sc_mk(h)
            if h >= 3:
                emit_nm_ep(h - 3)
        for h in range(max(0, n_heads - 3), n_heads):
            emit_nm_ep(h)
        last_nm = t_nm_h[n_heads - 1]
        attn_tok = t_ep_h[n_heads - 1]
        st["pA"] = attn_tok
        st["pB"] = attn_tok
        st["pC"] = attn_tok
        st["pCh0"] = attn_tok
        st["pCh1"] = attn_tok
        tok_sbf_reader = last_nm
        st["V65"] = last_nm

        if _CFG['p2_upto'] <= 5:
            tok_out_x2[oc] = attn_tok
            continue
        def f_sup(eng, oc=oc):
            eng.tensor_tensor(SRUN[:, 0:260], SRUN[:, 0:260],
                              DS[:, oc * 520: oc * 520 + 260], ALU.add)
            return eng.tensor_tensor(SRUN[:, 260:520], SRUN[:, 260:520],
                                     DS[:, oc * 520 + 260: oc * 520 + 520],
                                     ALU.add)
        tok_srun = S.t("vector", f_sup, [tok_ds_store[oc], tok_srun])

        def f_atb(eng):
            return eng.tensor_copy(SB2[:], SC2[:])
        t_atb = S.t("vector", f_atb, [attn_tok, st.get("SB2")])
        st["SC2"] = t_atb
        t_afm = transpose8(SB2[:], KFM[:], t_atb, extra=[last_nm])
        st["SB2"] = t_afm
        st["KFM"] = None

        if _CFG['p2_upto'] <= 6:
            tok_out_x2[oc] = t_afm
            continue

        def f_wop(eng):
            r = []
            for oh in range(2):
                for k in range(8):
                    r.append(eng.matmul(
                        pD[:, oh * 512:(oh + 1) * 512],
                        KFM[:, k * 128:(k + 1) * 128],
                        ACC[:, k * 1024 + oh * 512: k * 1024 + (oh + 1) * 512],
                        start=(k == 0), stop=(k == 7)))
            return r
        t_wop = S.t("tensor", f_wop, [t_afm, tok_wo, st.get("pD"),
                                      st.get("pDh0"), st.get("pDh1")])
        st["KFM"] = t_wop

        pend_x2[0] = (oc, t_wop)

    if pend_x2[0] is not None:
        emit_x2()

    tok_out_dma = []
    if not do_p3:
        for oc in range(N_OWN):
            def f_outd(eng, oc=oc):
                return eng.dma_start(dout[oc * 128:(oc + 1) * 128, :],
                                     X1O[:, oc * 1024:(oc + 1) * 1024])
            dep = [tok_out_x2[oc] if (do_p2 and tok_out_x2[oc]) else None,
                   ("vector", S.cnt["vector"]), ("tensor", S.cnt["tensor"]),
                   ("scalar", S.cnt["scalar"]), ("gpsimd", S.cnt["gpsimd"])]
            tok_out_dma.append(S.dma(f_outd, 1, dep))
    else:
        tok_p2_done = tok_out_x2[N_OWN - 1]
        tok_w1_free = tok_p2_done
        tok_w2_free_p3 = tok_w2r
        out_par_reader = [None, None]
        tok_out_dma = []
        tok_w1r_p3 = None
        st["ACC"] = tok_p2_done
        st["SB1"] = tok_p2_done
        st["SB2"] = tok_p2_done
        st["SC2"] = tok_p2_done
        st["AFM"] = tok_p2_done
        st["bSB1"] = tok_p2_done
        st["bSB1B"] = tok_p2_done

        for sb2 in range(2):
            tok_acc = [None] * SBC
            tok_nf_blk = [None, None]
            preA3 = {}
            NF3 = [(SB2, "bSB2"), (SB2B, "bSB2B"), (SB1B, "bSB1B")]

            def emit_A3(c, sb2=sb2):
                if c in preA3 or c >= SBC:
                    return
                oc = sb2 * SBC + c
                tri = c % 3
                x_ap = X1O[:, oc * 1024:(oc + 1) * 1024]
                extra = []
                if sb2 == 0 and c == 0:
                    extra = [tok_p2_done]
                if sb2 == 1 and c == 0:
                    extra = [out_par_reader[0], out_par_reader[1]]
                nf_dst, nf_key = NF3[tri]
                ta3 = prepass_A(tok_out_x2[oc], x_ap,
                                ACC[:, c * 1024:(c + 1) * 1024],
                                SB1[:], "bSB1", nf_dst[:], nf_key,
                                extra_deps=extra)
                preA3[c] = ta3

            def emit_B3(c):
                tri = c % 3
                nf_src, nf_key = NF3[tri]
                t_acc, t_nf = prepass_B(c, nf_src[:], preA3[c],
                                        ACC[:, c * 1024:(c + 1) * 1024])
                st[nf_key] = t_nf
                tok_acc[c] = t_acc
                if c == 3:
                    tok_nf_blk[0] = t_nf
                if c == SBC - 1:
                    tok_nf_blk[1] = t_nf

            def emit_pre3(c0, c1):
                for c in range(c0, c1):
                    emit_A3(c)
                    emit_A3(c + 1)
                    emit_A3(c + 2)
                    emit_B3(c)

            emit_pre3(0, 4)
            tok_w1r_p3, tok_w2r_p3, ep_tok = ffn_experts_sb(
                tok_nf_blk, tok_acc,
                tok_w1_free if sb2 == 0 else tok_w1r_p3,
                tok_w2_free_p3,
                mid_cb=lambda: emit_pre3(4, SBC))
            tok_w2_free_p3 = tok_w2r_p3

            for c in range(SBC):
                oc = sb2 * SBC + c
                par = c % 2
                x3buf = SC2 if par == 0 else SC3

                def f_x3(eng, oc=oc, c=c, x3buf=x3buf):
                    return eng.scalar_tensor_tensor(
                        x3buf[:], ACC[:, c * 1024:(c + 1) * 1024], 0.5,
                        X1O[:, oc * 1024:(oc + 1) * 1024], ALU.mult, ALU.add)
                t_x3 = S.t("vector", f_x3, [ep_tok[c], out_par_reader[par],
                                            st.get("SC2" if par == 0 else "SC3")])
                if par == 0:
                    st["SC2"] = t_x3
                else:
                    st["SC3"] = t_x3

                def f_out(eng, oc=oc, x3buf=x3buf):
                    return eng.dma_start(dout[oc * 128:(oc + 1) * 128, :], x3buf[:])
                tok_o = S.dma(f_out, 1, [t_x3])
                out_par_reader[par] = tok_o
                tok_out_dma.append(tok_o)

    if _CFG['dbg'] is not None:
        barrier = [("tensor", S.cnt["tensor"]), ("vector", S.cnt["vector"]),
                   ("scalar", S.cnt["scalar"]), ("gpsimd", S.cnt["gpsimd"]),
                   ("sync", S.cnt["sync"])]
        dbg_src = {"acc": ACC, "nf": NF, "xc": SB1, "sb2": SB2,
                   "comb": COMB, "ctb": CTB, "small": SMALL,
                   "x0p": X0P, "sc3": SC3, "wa": WA, "gwt": cGWT,
                   "b1c": cB1C, "b2r": cB2R}[_CFG['dbg']]
        ncols = dbg_src.shape[1] if hasattr(dbg_src, 'shape') else 8192
        ncols = {"acc": 8192, "nf": 8192, "xc": 1024, "sb2": 1024,
                 "comb": 64, "ctb": 128, "small": 64, "x0p": 2048,
                 "sc3": 1024, "wa": 16384, "gwt": 64, "b1c": 128,
                 "b2r": 1024}[_CFG['dbg']]
        prevtok = None
        for c in range((ncols + 1023) // 1024):
            w = min(1024, ncols - c * 1024)

            def f_cp(eng, c=c, w=w, dbg_src=dbg_src):
                return eng.tensor_copy(SC1[:, 0:w], dbg_src[:, c * 1024:c * 1024 + w])
            tcp = S.t("vector", f_cp, barrier + [prevtok])

            def f_dmp(eng, c=c, w=w):
                return eng.dma_start(dout[c * 128:(c + 1) * 128, 0:w], SC1[:, 0:w])
            prevtok = S.dma(f_dmp, 1, [tcp])

    def f_fin(eng):
        return eng.engine_nop()
    S.t("vector", f_fin, [("sync", S.cnt["sync"])])

    blk = ctx.enter_context(nc.Block())
    S.replay(blk, sems)
    ctx.close()
    return nc


_NC_CACHE = {}


def _get_nc():
    if "nc" not in _NC_CACHE:
        _NC_CACHE["nc"] = build_program()
    return _NC_CACHE["nc"]


def _host_prep(inputs):
    x = np.asarray(inputs["L0_attn_x"])
    wq = np.asarray(inputs["L0_attn_wq"])
    wk = np.asarray(inputs["L0_attn_wk"])
    wv = np.asarray(inputs["L0_attn_wv"])
    wo = np.asarray(inputs["L0_attn_wo"])
    nw = np.asarray(inputs["L0_ffn_norm_weight"])
    gw = np.asarray(inputs["L0_ffn_gate_weight"])
    gb = np.asarray(inputs["L0_ffn_gate_bias"])
    w1 = np.asarray(inputs["L0_ffn_mlp1_weight"])
    b1 = np.asarray(inputs["L0_ffn_mlp1_bias"])
    w2 = np.asarray(inputs["L0_ffn_mlp2_weight"])
    b2 = np.asarray(inputs["L0_ffn_mlp2_bias"])

    perm = np.concatenate([
        np.concatenate([h * 64 + 2 * np.arange(32),
                        h * 64 + 2 * np.arange(32) + 1])
        for h in range(NH)])
    wqp = np.ascontiguousarray(wq[:, perm], dtype=np.float32)
    wkp = np.ascontiguousarray(wk[:, perm], dtype=np.float32)

    w1t = np.empty((NEXP, D, DFF), dtype=BFD)
    b1c_all = np.empty((CH, 128), dtype=np.float32)
    for e in range(NEXP):
        T = w1[e].T
        w1t[e] = np.concatenate([T[:, 0::2], T[:, 1::2]], axis=1)
        bp = np.concatenate([b1[e][0::2], b1[e][1::2]]).astype(np.float32)
        b1c_all[:, e * 16:(e + 1) * 16] = bp.reshape(16, CH).T
    w2t = np.ascontiguousarray(np.transpose(w2, (0, 2, 1)))
    gwt = np.ascontiguousarray(gw.T)
    gbc = np.ascontiguousarray(np.broadcast_to(gb.astype(np.float32), (CH, NEXP)))
    nwc = np.ascontiguousarray(np.broadcast_to(nw.astype(np.float32), (CH, D)))
    b2r = np.ascontiguousarray(b2)

    tri = (np.arange(CH)[:, None] <= np.arange(CH)[None, :]).astype(np.float32)
    iota8 = np.ascontiguousarray(
        np.broadcast_to(np.arange(NEXP, dtype=np.float32), (CH, NEXP)))
    io9 = iota8 - 9.0
    inv_freq = (1.0 / THETA ** (np.arange(0, HD, 2).astype(np.float32) / HD)
                ).astype(np.float32)

    in_maps = []
    for core in range(8):
        b = core // 4
        q = core % 4
        permq = list(range(q)) + list(range(q + 1, 4)) + [q]
        x0r = np.ascontiguousarray(
            np.concatenate([x[b, 2048 * j:2048 * (j + 1)] for j in permq],
                           axis=0), dtype=np.float32)
        smask = np.zeros((CH, NCH), dtype=np.float32)
        for i in range(NCH):
            if i // 16 < q:
                smask[:, i] = 1.0
        ropet = np.empty((NCH * CH, 64), dtype=np.float32)
        for i in range(NCH):
            quarter = permq[i // 16]
            base = 2048 * quarter + 128 * (i % 16)
            t = (base + np.arange(CH, dtype=np.float32))[:, None]
            freqs = (t * inv_freq[None, :]).astype(np.float32)
            ropet[i * CH:(i + 1) * CH, 0:32] = np.cos(
                freqs.astype(np.float64)).astype(np.float32)
            ropet[i * CH:(i + 1) * CH, 32:64] = np.sin(
                freqs.astype(np.float64)).astype(np.float32)
        in_maps.append({
            "x0r": x0r, "w1t": w1t, "w2t": w2t, "b1c": b1c_all, "b2r": b2r,
            "gwt": gwt, "gbc": gbc, "nwc": nwc,
            "wqf": wqp, "wkf": wkp,
            "wvf": np.ascontiguousarray(wv, dtype=np.float32),
            "wof": np.ascontiguousarray(wo, dtype=np.float32),
            "rope": ropet, "smask": smask, "tri": tri,
            "iota8": iota8, "io9": io9,
        })
    return in_maps


def kernel(**inputs):
    in_maps = _host_prep(inputs)
    nc = _get_nc()
    res = run_bass_kernel_spmd(nc, in_maps, core_ids=list(range(8)))
    out = np.empty((B, L, D), dtype=np.float32)
    for core in range(8):
        b = core // 4
        q = core % 4
        out[b, 2048 * q:2048 * (q + 1), :] = res.results[core]["xout"]
    return out


# version 17
# speedup vs baseline: 1.0290x; 1.0290x over previous
import numpy as np
import ml_dtypes
from contextlib import ExitStack
import concourse.bass as bass
import concourse.mybir as mybir
from concourse.bass_utils import run_bass_kernel_spmd

F32 = mybir.dt.float32
BF16 = mybir.dt.bfloat16
AF = mybir.ActivationFunctionType
ALU = mybir.AluOpType
AX = mybir.AxisListType
BFD = ml_dtypes.bfloat16

D = 1024
NH, HD, NPR = 16, 64, 32
L, B = 8192, 2
CH = 128
NCH = 64
SBC = 8
NSB = NCH // SBC
NEXP = 8
DFF = 2048
OWN_SB0 = 6
N_OWN = 16
EPS = 1e-5
EPS_ATT = 1e-6
THETA = 10000.0
LIMIT = 7.0
ALPHA = 1.702

_CFG = dict(p1_sbs=NSB, experts=True, finish=True, p2=True, p3=True, pre_upto=99, dbg=None, p2_n=N_OWN, p2_upto=99, p2_nh=16, p2_hupto=99)


class Sched:
    ENGS = ("sync", "tensor", "vector", "scalar", "gpsimd")

    def __init__(self):
        self.tasks = {e: [] for e in self.ENGS}
        self.cnt = {e: 0 for e in self.ENGS}

    def _mk(self, eng, fn, deps, inc):
        w = {}
        for d in deps:
            if d is None:
                continue
            de, dv = d
            if de == eng and eng != "sync":
                continue
            if dv <= 0:
                continue
            w[de] = max(w.get(de, 0), dv)
        self.tasks[eng].append((sorted(w.items()), fn))
        self.cnt[eng] += inc
        return (eng, self.cnt[eng])

    def t(self, eng, fn, deps=()):
        return self._mk(eng, fn, deps, 1)

    def dma(self, fn, n, deps=()):
        return self._mk("sync", fn, deps, 16 * n)

    def replay(self, blk, sems):
        def runner(eng_name):
            tasks = self.tasks[eng_name]

            def run(eng):
                for waits, fn in tasks:
                    for (se, v) in waits:
                        eng.wait_ge(sems[se], v)
                    r = fn(eng)
                    rl = r if isinstance(r, (list, tuple)) else [r]
                    if eng_name == "sync":
                        for inst in rl:
                            inst.then_inc(sems["sync"], 16)
                    else:
                        rl[-1].then_inc(sems[eng_name], 1)
            return run
        blk.sync(runner("sync"))
        blk.tensor(runner("tensor"))
        blk.vector(runner("vector"))
        blk.scalar(runner("scalar"))
        blk.gpsimd(runner("gpsimd"))


def build_program():
    nc = bass.Bass("TRN2", target_bir_lowering=False,
                   dynamic_dma_scratch_size=2048)

    dx0 = nc.dram_tensor("x0r", [NCH * CH, D], F32, kind="ExternalInput")
    dw1t = nc.dram_tensor("w1t", [NEXP, D, DFF], BF16, kind="ExternalInput")
    dw2t = nc.dram_tensor("w2t", [NEXP, D, D], BF16, kind="ExternalInput")
    db1c = nc.dram_tensor("b1c", [CH, 128], F32, kind="ExternalInput")
    db2r = nc.dram_tensor("b2r", [NEXP, D], BF16, kind="ExternalInput")
    dgwt = nc.dram_tensor("gwt", [D, NEXP], BF16, kind="ExternalInput")
    dgbc = nc.dram_tensor("gbc", [CH, NEXP], F32, kind="ExternalInput")
    dnwc = nc.dram_tensor("nwc", [CH, D], F32, kind="ExternalInput")
    dwq = nc.dram_tensor("wqf", [D, D], F32, kind="ExternalInput")
    dwk = nc.dram_tensor("wkf", [D, D], F32, kind="ExternalInput")
    dwv = nc.dram_tensor("wvf", [D, D], F32, kind="ExternalInput")
    dwo = nc.dram_tensor("wof", [D, D], F32, kind="ExternalInput")
    drope = nc.dram_tensor("rope", [NCH * CH, 64], F32, kind="ExternalInput")
    dsmk = nc.dram_tensor("smask", [CH, NCH], F32, kind="ExternalInput")
    dtri = nc.dram_tensor("tri", [CH, CH], F32, kind="ExternalInput")
    diota = nc.dram_tensor("iota8", [CH, NEXP], F32, kind="ExternalInput")
    dio9 = nc.dram_tensor("io9", [CH, NEXP], F32, kind="ExternalInput")
    dout = nc.dram_tensor("xout", [N_OWN * CH, D], F32, kind="ExternalOutput")
    swb = nc.dram_tensor("wbf_scratch", [4, D, D], BF16, kind="Internal")

    ctx = ExitStack()

    def sbuf(name, shape, dtype):
        return ctx.enter_context(nc.sbuf_tensor(name, shape, dtype))

    NF = sbuf("NF", [128, 8192], BF16)
    ACC = sbuf("ACCb", [128, 8192], BF16)
    WA = sbuf("WA", [128, 24576], BF16)
    X1O = sbuf("X1O", [128, 16384], F32)
    DS = sbuf("DSo", [128, N_OWN * 520], BF16)
    SRUN = sbuf("SRUN", [128, 520], F32)
    SBF = sbuf("SBF", [128, 520], BF16)
    ROPE = sbuf("ROPEt", [128, SBC * 64], F32)
    AFM = sbuf("ACTFM", [128, 4096], BF16)
    X0P = sbuf("X0P", [128, 2048], F32)
    SC1 = sbuf("SC1", [128, 1024], F32)
    SC2 = sbuf("SC2", [128, 1024], F32)
    SC3 = sbuf("SC3", [128, 1024], F32)
    SB1 = sbuf("SB1", [128, 1024], BF16)
    SB2 = sbuf("SB2", [128, 1024], BF16)
    SB1B = sbuf("SB1B", [128, 1024], BF16)
    SB2B = sbuf("SB2B", [128, 1024], BF16)
    Y2FM = sbuf("Y2FM", [128, 1024], BF16)
    QFM = sbuf("QFM", [128, 1024], BF16)
    KFM = sbuf("KFM", [128, 1024], BF16)
    V65 = sbuf("V65", [128, 1040], BF16)
    ABUF = sbuf("ABUF", [128, 256], BF16)
    CTB = sbuf("CTB", [128, 128], BF16)
    COMB = sbuf("COMB", [128, 64], F32)
    SMALL = sbuf("SMALL", [128, 64], F32)
    VEC = sbuf("VEC", [128, 24], F32)
    cNWC = sbuf("cNWC", [128, 1024], F32)
    cGWT = sbuf("cGWT", [128, 64], BF16)
    cGBC = sbuf("cGBC", [128, 8], F32)
    cB1C = sbuf("cB1C", [128, 128], F32)
    cB2R = sbuf("cB2R", [128, 1024], BF16)
    cIOTA = sbuf("cIOTA", [128, 8], F32)
    cIO9 = sbuf("cIO9", [128, 8], F32)
    cTRI = sbuf("cTRI", [128, 128], F32)
    cIDB = sbuf("cIDB", [128, 128], BF16)
    cIDF = sbuf("cIDF", [128, 128], F32)
    cSMK = sbuf("cSMK", [128, 64], F32)

    pA = ctx.enter_context(nc.psum_tensor("pA", [128, 512], F32))
    pB = ctx.enter_context(nc.psum_tensor("pB", [128, 512], F32))
    pC = ctx.enter_context(nc.psum_tensor("pC", [128, 1024], F32))
    pD = ctx.enter_context(nc.psum_tensor("pD", [128, 1024], F32))
    pT = ctx.enter_context(nc.psum_tensor("pT", [128, 512], F32))
    pM = ctx.enter_context(nc.psum_tensor("pM", [128, 456], F32))

    sems = {e: ctx.enter_context(nc.semaphore("sem_" + e)) for e in Sched.ENGS}
    S = Sched()

    vRED, vM, vVAR, vRSTD, vR2, vDD, vEE, vW1, vW2, vRD, vRR = range(11)

    def vec(i):
        return VEC[:, i:i + 1]

    st = {}

    def f_ld_const(eng):
        return [eng.dma_start(cNWC[:], dnwc[:]),
                eng.dma_start(cGWT[:].rearrange("p (k e) -> p k e", e=NEXP),
                              dgwt.rearrange("(k p) e -> p k e", p=128)),
                eng.dma_start(cGBC[:], dgbc[:]),
                eng.dma_start(cB1C[:], db1c[:]),
                eng.dma_start(cB2R[0:8, :], db2r[:]),
                eng.dma_start(cIOTA[:], diota[:]),
                eng.dma_start(cIO9[:], dio9[:]),
                eng.dma_start(cTRI[:], dtri[:]),
                eng.dma_start(cSMK[:], dsmk[:])]
    tok_const = S.dma(f_ld_const, 9)

    def f_ident_a(eng):
        eng.memset(cIDB[:], 1.0)
        eng.memset(cIDF[:], 1.0)
        eng.memset(V65[:], 1.0)
        eng.memset(VEC[:, 15:16], 1.0)
        eng.memset(SRUN[:], 0.0)
        eng.memset(VEC[:, 11:12], 1.0)
        return eng.affine_select(cIDF[:], cIDF[:], [[-1, 128]], ALU.is_equal,
                                 0.0, base=0, channel_multiplier=1)
    tok_ident_a = S.t("gpsimd", f_ident_a)

    def f_ident_b(eng):
        return eng.affine_select(cIDB[:], cIDB[:], [[-1, 128]], ALU.is_equal,
                                 0.0, base=0, channel_multiplier=1)
    tok_ident = S.t("gpsimd", f_ident_b, [tok_ident_a])

    prev = tok_const
    for wi, dw in enumerate((dwq, dwk, dwv, dwo)):
        def f_ldw(eng, dw=dw):
            return eng.dma_start(
                X1O[:, 0:8192].rearrange("p (k o) -> p k o", o=1024),
                dw.rearrange("(k p) o -> p k o", p=128))
        tok_l = S.dma(f_ldw, 1, [prev])

        def f_cv(eng):
            return eng.tensor_copy(NF[:, 0:8192], X1O[:, 0:8192])
        tok_c = S.t("vector", f_cv, [tok_l])

        def f_stw(eng, wi=wi):
            return eng.dma_start(
                swb[wi].rearrange("(k p) o -> p k o", p=128),
                NF[:, 0:8192].rearrange("p (k o) -> p k o", o=1024))
        prev = S.dma(f_stw, 1, [tok_c])
    tok_wconv = prev

    def transpose8(src_bf, dst_fm, dep, extra=()):
        atok = None
        for g in (0, 4):
            def fpe(eng, g=g, src=src_bf):
                return [eng.matmul(pT[:, j * 128:(j + 1) * 128],
                                   src[:, (g + j) * 128:(g + j + 1) * 128],
                                   cIDB[:], start=True, stop=True)
                        for j in range(4)]
            d = [dep, tok_ident, st.get("pT")] + list(extra)
            ptok = S.t("tensor", fpe, d)

            def fac(eng, g=g, dst=dst_fm):
                return eng.copy(dst[:, g * 128:(g + 4) * 128], pT[:])
            atok = S.t("scalar", fac, [ptok])
            st["pT"] = atok
        return atok

    def transpose8_strided(src_bf, c_local, dep):
        atok = None
        for g in (0, 4):
            def fpe(eng, g=g, src=src_bf):
                return [eng.matmul(pT[:, j * 128:(j + 1) * 128],
                                   src[:, (g + j) * 128:(g + j + 1) * 128],
                                   cIDB[:], start=True, stop=True)
                        for j in range(4)]
            ptok = S.t("tensor", fpe, [dep, tok_ident, st.get("pT")])

            def fac(eng, g=g, c_local=c_local):
                ov = NF[:].rearrange("p (k t) -> p k t", t=1024)[
                    :, g:g + 4, c_local * 128:(c_local + 1) * 128]
                iv = pT[:].rearrange("p (k t) -> p k t", t=128)
                return eng.copy(ov, iv)
            atok = S.t("scalar", fac, [ptok])
            st["pT"] = atok
        return atok

    def prepass_A(x_tok, x_ap, xc_ap, xm_buf, xm_key, nf_dst, nf_key,
                  extra_deps=()):
        LN2 = VEC[:, 16:18]

        def fa1(eng, x_ap=x_ap, xm_buf=xm_buf):
            eng.tensor_tensor(xm_buf, x_ap, x_ap, ALU.mult)
            eng.reduce_sum(vec(vRED), x_ap, AX.X)
            eng.reduce_sum(vec(vVAR), xm_buf, AX.X)
            eng.tensor_scalar(vec(vM), vec(vRED), 1.0 / D, None, ALU.mult)
            eng.tensor_copy(VEC[:, 12:13], VEC[:, 15:16])
            eng.tensor_tensor(VEC[:, 18:19], vec(vM), vec(vM), ALU.mult)
            eng.tensor_scalar(vec(vVAR), vec(vVAR), 1.0 / D, None, ALU.mult)
            eng.tensor_copy(VEC[:, 12:13], VEC[:, 15:16])
            eng.tensor_tensor(vec(vVAR), vec(vVAR), VEC[:, 18:19], ALU.subtract)
            eng.tensor_copy(VEC[:, 12:13], VEC[:, 15:16])
            eng.tensor_scalar(LN2[:, 0:1], vec(vVAR), EPS, None, ALU.add)
            eng.tensor_copy(VEC[:, 12:13], VEC[:, 15:16])
            eng.reciprocal(VEC[:, 19:20], LN2[:, 0:1])
            eng.tensor_copy(VEC[:, 12:13], VEC[:, 15:16])
            return eng.tensor_scalar(LN2[:, 1:2], vec(vVAR), VEC[:, 19:20],
                                     EPS, ALU.mult, ALU.add)
        ta1 = S.t("vector", fa1, [x_tok, st.get(xm_key)] + list(extra_deps))

        def fa2(eng):
            eng.activation(LN2[:], LN2[:], AF.Ln)
            return eng.activation(LN2[:], LN2[:], AF.Exp, scale=-0.5)
        ta2 = S.t("scalar", fa2, [ta1])

        def fa3(eng, xc_ap=xc_ap, nf_dst=nf_dst, x_ap=x_ap):
            eng.tensor_scalar(xc_ap, x_ap, vec(vM), LN2[:, 0:1],
                              ALU.subtract, ALU.mult)
            eng.tensor_copy(VEC[:, 12:13], VEC[:, 15:16])
            return eng.scalar_tensor_tensor(nf_dst, xc_ap, LN2[:, 1:2],
                                            cNWC[:], ALU.mult, ALU.mult)
        ta3 = S.t("vector", fa3, [ta2, st.get(nf_key), tok_const])
        st[xm_key] = ta3
        st[nf_key] = ta3
        return ta3

    def prepass_B(c_local, nf_src, nf_tok, xc_ap):
        tok_nf = transpose8_strided(nf_src, c_local, nf_tok)

        def f7(eng, c_local=c_local):
            return [eng.matmul(
                pM[:, 0:8],
                NF[:, k * 1024 + c_local * 128: k * 1024 + (c_local + 1) * 128],
                cGWT[:, k * 8:(k + 1) * 8],
                start=(k == 0), stop=(k == 7)) for k in range(8)]
        t7 = S.t("tensor", f7, [tok_nf, tok_const, st.get("pMg"),
                                st.get("pMb")])

        def f8(eng):
            GAP = lambda: eng.tensor_copy(VEC[:, 12:13], VEC[:, 15:16])
            lgb = SMALL[:, 8:12].bitcast(BF16)[:, 0:8]
            eng.tensor_tensor(lgb, pM[:, 0:8], cGBC[:], ALU.add)
            GAP()
            lgf = SMALL[:, 16:24]
            eng.tensor_copy(lgf, lgb)
            GAP()
            m8 = SMALL[:, 0:8]
            eng.max(out=m8, in_=lgf)
            GAP()
            e1 = SMALL[:, 24:32]
            eng.tensor_scalar(e1, lgf, m8[:, 0:1], None, ALU.is_equal)
            GAP()
            tq = SMALL[:, 32:40]
            eng.tensor_tensor(tq, e1, cIO9[:], ALU.mult)
            GAP()
            eng.tensor_scalar(tq, tq, 9.0, None, ALU.add)
            GAP()
            eng.tensor_reduce(vec(vDD), tq, AX.X, ALU.min)
            eng.tensor_tensor(vec(vEE), m8[:, 1:2], m8[:, 0:1], ALU.subtract)
            m1 = SMALL[:, 40:48]
            eng.tensor_scalar(m1, cIOTA[:], vec(vDD), None, ALU.is_equal)
            GAP()
            lg2 = SMALL[:, 48:56]
            eng.scalar_tensor_tensor(lg2, m1, -1.0e30, lgf, ALU.mult, ALU.add)
            GAP()
            eng.tensor_scalar(e1, lg2, m8[:, 1:2], None, ALU.is_equal)
            GAP()
            eng.tensor_tensor(tq, e1, cIO9[:], ALU.mult)
            GAP()
            eng.tensor_scalar(tq, tq, 9.0, None, ALU.add)
            GAP()
            eng.tensor_reduce(vec(vDD), tq, AX.X, ALU.min)
            eng.tensor_copy(vec(vRD), vec(vEE))
            m2 = SMALL[:, 56:64]
            return eng.tensor_scalar(m2, cIOTA[:], vec(vDD), None,
                                     ALU.is_equal)
        t8 = S.t("vector", f8, [t7, tok_const])
        st["pMg"] = t8
        st["pMb"] = t8

        def f9(eng):
            return eng.activation(vec(vEE), vec(vEE), AF.Exp)
        t9 = S.t("scalar", f9, [t8])

        def f10(eng, c_local=c_local):
            den = vec(vRD)
            eng.tensor_scalar(den, vec(vEE), 1.0, None, ALU.add)
            eng.tensor_copy(VEC[:, 12:13], VEC[:, 15:16])
            eng.reciprocal(vec(vW1), den)
            eng.tensor_copy(VEC[:, 12:13], VEC[:, 15:16])
            eng.tensor_tensor(vec(vW2), vec(vEE), vec(vW1), ALU.mult)
            m1 = SMALL[:, 40:48]
            m2 = SMALL[:, 56:64]
            cmb = COMB[:, c_local * 8:(c_local + 1) * 8]
            tq = SMALL[:, 32:40]
            eng.tensor_scalar(tq, m1, vec(vW1), None, ALU.mult)
            eng.tensor_copy(VEC[:, 12:13], VEC[:, 15:16])
            return eng.scalar_tensor_tensor(cmb, m2, vec(vW2), tq,
                                            ALU.mult, ALU.add)
        t10 = S.t("vector", f10, [t9, st.get("COMB")])

        def f11(eng, c_local=c_local):
            return eng.transpose(pM[0:8, 72:200],
                                 COMB[:, c_local * 8:(c_local + 1) * 8], cIDF[:])
        t11 = S.t("tensor", f11, [t10, tok_ident_a, tok_ident,
                                  st.get("pMct"), st.get("pMb")])

        def f12(eng):
            return eng.tensor_copy(CTB[0:8, :], pM[0:8, 72:200])
        t12 = S.t("vector", f12, [t11, st.get("CTB")])
        st["pMct"] = t12
        st["pMb"] = t12

        def f13(eng):
            return [eng.matmul(pC[:, hh * 512:(hh + 1) * 512], CTB[0:8, :],
                               cB2R[0:8, hh * 512:(hh + 1) * 512],
                               start=True, stop=True) for hh in range(2)]
        t13 = S.t("tensor", f13, [t12, tok_const, st.get("pC"),
                                  st.get("pCh0"), st.get("pCh1")])
        st["CTB"] = t13

        def f14(eng, xc_ap=xc_ap):
            return eng.tensor_tensor(xc_ap, xc_ap, pC[:], ALU.add)
        t14 = S.t("vector", f14, [t13])
        st["pC"] = t14
        st["pCh0"] = t14
        st["pCh1"] = t14
        return t14, tok_nf

    def ffn_experts_sb(tok_nf_blk, comb_tok, tok_w1_free, tok_w2_free,
                       mid_cb=None):
        ep_tok = [None] * SBC
        tok_w1_read = None
        tok_w2_read = None
        for e in range(NEXP):
            def f_w1(eng, e=e):
                return eng.dma_start(
                    WA[:, 0:16384].rearrange("p (k o) -> p k o", o=2048),
                    dw1t[e].rearrange("(k p) o -> p k o", p=128))
            tok_lw1 = S.dma(f_w1, 1, [tok_w1_read if e > 0 else tok_w1_free])

            def f_w2(eng, e=e):
                return eng.dma_start(
                    WA[:, 16384:24576].rearrange("p (k o) -> p k o", o=1024),
                    dw2t[e].rearrange("(k p) o -> p k o", p=128))
            tok_lw2 = S.dma(f_w2, 1, [tok_w2_read if e > 0 else tok_w2_free])

            for blk in range(2):
                pool_last = None
                tE3_vals = []
                for jp in range(8):
                    if jp % 2 == 0:
                        pg, pl = pA[:], pB[:]
                        kg, kl = "pA", "pB"
                    else:
                        pg, pl = pC[:, 0:512], pC[:, 512:1024]
                        kg, kl = "pCh0", "pCh1"

                    def fG(eng, blk=blk, jp=jp, pg=pg):
                        return [eng.matmul(
                            pg,
                            WA[:, k * 2048 + jp * 128: k * 2048 + (jp + 1) * 128],
                            NF[:, k * 1024 + blk * 512: k * 1024 + (blk + 1) * 512],
                            start=(k == 0), stop=(k == 7)) for k in range(8)]
                    tG = S.t("tensor", fG, [tok_lw1, tok_nf_blk[blk],
                                            st.get(kg), st.get("pC") if jp % 2 else None])

                    def fL(eng, blk=blk, jp=jp, pl=pl):
                        return [eng.matmul(
                            pl,
                            WA[:, k * 2048 + 1024 + jp * 128: k * 2048 + 1024 + (jp + 1) * 128],
                            NF[:, k * 1024 + blk * 512: k * 1024 + (blk + 1) * 512],
                            start=(k == 0), stop=(k == 7)) for k in range(8)]
                    tL = S.t("tensor", fL, [st.get(kl),
                                            st.get("pC") if jp % 2 else None])
                    tok_w1_read = tL

                    par = jp % 2
                    g1 = SC1[:, par * 512:(par + 1) * 512]
                    l1 = SC2[:, par * 512:(par + 1) * 512]
                    sg = SC3[:, par * 512:(par + 1) * 512]

                    def fE1(eng, e=e, jp=jp, g1=g1, l1=l1, pg=pg, pl=pl):
                        eng.tensor_scalar(
                            l1, pl,
                            cB1C[:, e * 16 + 8 + jp: e * 16 + 8 + jp + 1],
                            LIMIT, ALU.add, ALU.min)
                        eng.tensor_scalar(g1, pg,
                                          cB1C[:, e * 16 + jp: e * 16 + jp + 1],
                                          LIMIT, ALU.add, ALU.min)
                        return eng.tensor_scalar(l1, l1, -LIMIT, 1.0,
                                                 ALU.max, ALU.add)
                    tE1 = S.t("vector", fE1, [tG, tL, tok_const,
                                              st.get(("m1sc", par))])
                    st[kg] = tE1
                    st[kl] = tE1

                    def fE2(eng, sg=sg, g1=g1):
                        return eng.activation(sg, g1, AF.Sigmoid, scale=ALPHA)
                    tE2 = S.t("scalar", fE2, [tE1])

                    def fE3(eng, g1=g1, l1=l1, sg=sg, jp=jp):
                        eng.tensor_tensor(g1, g1, sg, ALU.mult)
                        eng.tensor_copy(VEC[:, 11:12], VEC[:, 15:16])
                        return eng.tensor_tensor(AFM[:, jp * 512:(jp + 1) * 512],
                                                 g1, l1, ALU.mult)
                    tE3 = S.t("gpsimd", fE3, [tE2, st.get("AFM")])
                    st[("m1sc", par)] = tE3
                    pool_last = tE3
                    tE3_vals.append(tE3[1])

                for tch in range(4):
                    c_local = blk * 4 + tch
                    for hh in range(2):
                        ph = pD[:, hh * 512:(hh + 1) * 512]
                        pk = "pDh%d" % hh

                        def fM2(eng, tch=tch, hh=hh, ph=ph, tv=tuple(tE3_vals)):
                            r = []
                            for k2 in range(8):
                                eng.wait_ge(sems["gpsimd"], tv[k2])
                                r.append(eng.matmul(
                                    ph,
                                    AFM[:, k2 * 512 + tch * 128: k2 * 512 + (tch + 1) * 128],
                                    WA[:, 16384 + k2 * 1024 + hh * 512: 16384 + k2 * 1024 + (hh + 1) * 512],
                                    start=(k2 == 0), stop=(k2 == 7)))
                            return r
                        tM2 = S.t("tensor", fM2, [tok_lw2,
                                                  st.get(pk), st.get("pD")])
                        tok_w2_read = tM2

                        def fEp(eng, e=e, c_local=c_local, hh=hh, ph=ph):
                            a = ACC[:, c_local * 1024 + hh * 512:
                                    c_local * 1024 + (hh + 1) * 512]
                            return eng.scalar_tensor_tensor(
                                a, ph,
                                COMB[:, c_local * 8 + e: c_local * 8 + e + 1],
                                a, ALU.mult, ALU.add)
                        tEp = S.t("vector", fEp, [tM2, comb_tok[c_local]])
                        st[pk] = tEp
                        ep_tok[c_local] = tEp
                st["AFM"] = tok_w2_read
                if mid_cb is not None and e == 0 and blk == 0:
                    mid_cb()
                    mid_cb = None
        st["COMB"] = ep_tok[SBC - 1]
        st["ACC"] = ep_tok[SBC - 1]
        st["SC1"] = pool_last
        st["SC2"] = pool_last
        st["SC3"] = pool_last
        return tok_w1_read, tok_w2_read, ep_tok

    def ln_to_y2b(x_tok, x_ap, y2dst=None, y2key="SB1", extra_deps=(),
                  scratch=None):
        if scratch is None:
            bA, bB, kA, kB = SC1[:], SC2[:], "SC1", "SC2"
        else:
            bA, bB, kA, kB = scratch

        def f1(eng, x_ap=x_ap, bA=bA, bB=bB):
            eng.reduce_sum(vec(vRED), x_ap, AX.X)
            eng.tensor_tensor(bB, x_ap, x_ap, ALU.mult)
            eng.tensor_scalar(vec(vM), vec(vRED), 1.0 / D, None, ALU.mult)
            eng.reduce_sum(vec(vVAR), bB, AX.X)
            eng.tensor_scalar(bA, x_ap, vec(vM), None, ALU.subtract)
            eng.tensor_tensor(vec(vDD), vec(vM), vec(vM), ALU.mult)
            eng.tensor_scalar(vec(vVAR), vec(vVAR), 1.0 / D, EPS,
                              ALU.mult, ALU.add)
            eng.tensor_copy(VEC[:, 12:13], VEC[:, 15:16])
            return eng.tensor_tensor(vec(vVAR), vec(vVAR), vec(vDD),
                                     ALU.subtract)
        t1 = S.t("vector", f1, [x_tok, st.get(kA), st.get(kB)] + list(extra_deps))
        if _CFG['pre_upto'] <= 1:
            return t1, t1, t1

        def f2(eng):
            eng.activation(vec(vRSTD), vec(vVAR), AF.Ln)
            return eng.activation(vec(vRSTD), vec(vRSTD), AF.Exp, scale=-0.5)
        t2 = S.t("scalar", f2, [t1])

        if y2dst is None:
            y2dst = SB1[:]

        def f3(eng, y2dst=y2dst, bA=bA):
            return eng.tensor_scalar(y2dst, bA, vec(vRSTD), None, ALU.mult)
        t3 = S.t("vector", f3, [t2, st.get(y2key)])
        st[kA] = t3
        st[kB] = t1
        return t3

    def proj16(dst_psum, pkey, y2fm_tok, w_sb, w_base, w_tok):
        def f(eng, dst=dst_psum, w_sb=w_sb, w_base=w_base):
            r = []
            for oh in range(2):
                for k in range(8):
                    r.append(eng.matmul(
                        dst[:, oh * 512:(oh + 1) * 512],
                        Y2FM[:, k * 128:(k + 1) * 128],
                        w_sb[:, w_base + k * 1024 + oh * 512: w_base + k * 1024 + (oh + 1) * 512],
                        start=(k == 0), stop=(k == 7)))
            return r
        tok = S.t("tensor", f, [y2fm_tok, w_tok, st.get(pkey),
                                st.get(pkey + "h0"), st.get(pkey + "h1")])
        return tok

    def rope_relu(p_src, dst_bf, dst_key, c_local, rope_tok, proj_tok,
                  scratch=None):
        co = c_local * 64
        if scratch is None:
            bA, bB, kA, kB = SC1[:], SC2[:], "SC1", "SC2"
        else:
            bA, bB, kA, kB = scratch
        P1v = p_src[:].rearrange("p (h a i) -> p h a i", a=2, i=32)[:, :, 0, :]
        P2v = p_src[:].rearrange("p (h a i) -> p h a i", a=2, i=32)[:, :, 1, :]
        CC = ROPE[:, co:co + 32].unsqueeze(1).to_broadcast([128, 16, 32])
        SS = ROPE[:, co + 32:co + 64].unsqueeze(1).to_broadcast([128, 16, 32])

        def hv(buf, half):
            return buf[:, half * 512:(half + 1) * 512].rearrange(
                "p (h i) -> p h i", i=32)

        def f1(eng):
            eng.tensor_tensor(hv(bA, 0), P1v, CC, ALU.mult)
            eng.tensor_tensor(hv(bA, 1), P2v, SS, ALU.mult)
            eng.tensor_tensor(hv(bB, 0), P1v, SS, ALU.mult)
            return eng.tensor_tensor(hv(bB, 1), P2v, CC, ALU.mult)
        t1 = S.t("vector", f1, [proj_tok, rope_tok, st.get(kA),
                                st.get(kB)])

        D1 = dst_bf[:].rearrange("p (h a i) -> p h a i", a=2, i=32)[:, :, 0, :]
        D2 = dst_bf[:].rearrange("p (h a i) -> p h a i", a=2, i=32)[:, :, 1, :]

        def f2(eng):
            eng.tensor_tensor(hv(bA, 0), hv(bA, 0), hv(bA, 1), ALU.subtract)
            eng.tensor_tensor(hv(bB, 0), hv(bB, 0), hv(bB, 1), ALU.add)
            eng.tensor_scalar(D1, hv(bA, 0), 0.0, None, ALU.max)
            return eng.tensor_scalar(D2, hv(bB, 0), 0.0, None, ALU.max)
        t2 = S.t("gpsimd", f2, [t1, st.get(dst_key)])
        st[kA] = t2
        st[kB] = t2
        return t2, t1

    x0_par_reader = [None, None]
    tok_rope_free = None
    tok_srun = None
    tok_w1_free = tok_wconv
    tok_w2_free = tok_wconv
    own_x1_tok = [None] * N_OWN
    tok_ds_store = [None] * N_OWN
    tok_wk = tok_wv = None
    tok_fin_last_pe = None

    for sb_i in range(_CFG['p1_sbs']):
        def f_rope(eng, sb_i=sb_i):
            return eng.dma_start(
                ROPE[:].rearrange("p (c z) -> p c z", z=64),
                drope[sb_i * 1024:(sb_i + 1) * 1024, :].rearrange(
                    "(c p) z -> p c z", p=128))
        tok_rope = S.dma(f_rope, 1, [tok_rope_free])

        tok_acc = [None] * SBC
        tok_nf_blk = [None, None]
        preA = {}
        NF3 = [(SB2, "bSB2"), (SB2B, "bSB2B"), (SB1B, "bSB1B")]

        def emit_A(c, sb_i=sb_i):
            if c in preA or c >= SBC:
                return
            g = sb_i * SBC + c
            par = c % 2
            tri = c % 3

            def f_x0(eng, g=g, par=par):
                return eng.dma_start(X0P[:, par * 1024:(par + 1) * 1024],
                                     dx0[g * 128:(g + 1) * 128, :])
            tok_x0 = S.dma(f_x0, 1, [x0_par_reader[par]])
            x_ap = X0P[:, par * 1024:(par + 1) * 1024]
            nf_dst, nf_key = NF3[tri]
            ta3 = prepass_A(tok_x0, x_ap, ACC[:, c * 1024:(c + 1) * 1024],
                            SB1[:], "bSB1", nf_dst[:], nf_key)
            x0_par_reader[par] = ta3
            preA[c] = ta3

        def emit_B(c):
            tri = c % 3
            nf_src, nf_key = NF3[tri]
            t_acc, t_nf = prepass_B(c, nf_src[:], preA[c],
                                    ACC[:, c * 1024:(c + 1) * 1024])
            st[nf_key] = t_nf
            tok_acc[c] = t_acc
            if c == 3:
                tok_nf_blk[0] = t_nf
            if c == SBC - 1:
                tok_nf_blk[1] = t_nf

        def emit_pre(c0, c1, sb_i=sb_i):
            for c in range(c0, c1):
                emit_A(c)
                emit_A(c + 1)
                emit_A(c + 2)
                emit_B(c)

        emit_pre(0, 4)
        if _CFG['experts']:
            tok_w1r, tok_w2r, ep_tok = ffn_experts_sb(
                tok_nf_blk, tok_acc, tok_w1_free, tok_w2_free,
                mid_cb=lambda: emit_pre(4, SBC))
        else:
            emit_pre(4, SBC)
            tok_w1r = tok_w2r = None
            ep_tok = list(tok_acc)

        if not _CFG['finish']:
            tok_w1_free = tok_w1r
            tok_w2_free = tok_w2r
            continue

        def f_wk(eng):
            return eng.dma_start(
                WA[:, 0:8192].rearrange("p (k o) -> p k o", o=1024),
                swb[1].rearrange("(k p) o -> p k o", p=128))

        def f_wv(eng):
            return eng.dma_start(
                WA[:, 8192:16384].rearrange("p (k o) -> p k o", o=1024),
                swb[2].rearrange("(k p) o -> p k o", p=128))
        tok_wk = S.dma(f_wk, 1, [tok_w1r, tok_wconv])
        tok_wv = S.dma(f_wv, 1, [tok_w1r])

        AFMf = AFM[:].bitcast(F32)
        st["AFMa"] = st.get("AFM")
        st["AFMb"] = st.get("AFM")
        pend = None

        def emit_ds(pend):
            g_, own_, oc_, t_kr_, t_v65_ = pend

            def f_ds(eng):
                r = []
                for h in range(16):
                    p = h // 2
                    dst = pA if p < 4 else pB
                    r.append(eng.matmul(
                        dst[(h % 2) * 64:(h % 2) * 64 + 64,
                            (p % 4) * 65:(p % 4) * 65 + 65],
                        SB2[:, h * 64:(h + 1) * 64],
                        V65[:, h * 65:(h + 1) * 65],
                        start=True, stop=True))
                return r
            t_ds = S.t("tensor", f_ds, [t_kr_, t_v65_, st.get("pA"),
                                        st.get("pB")])
            st["SB2"] = t_ds
            st["V65"] = t_ds

            def f_su(eng, g=g_):
                eng.scalar_tensor_tensor(SRUN[:, 0:260], pA[:, 0:260],
                                         cSMK[:, g:g + 1], SRUN[:, 0:260],
                                         ALU.mult, ALU.add)
                return eng.scalar_tensor_tensor(SRUN[:, 260:520], pB[:, 0:260],
                                                cSMK[:, g:g + 1],
                                                SRUN[:, 260:520],
                                                ALU.mult, ALU.add)
            t_su = S.t("vector", f_su, [t_ds, tok_const])
            st["pA"] = t_su
            st["pB"] = t_su
            if own_:
                def f_dss(eng, oc=oc_):
                    eng.tensor_copy(DS[:, oc * 520: oc * 520 + 260],
                                    pA[:, 0:260])
                    return eng.tensor_copy(
                        DS[:, oc * 520 + 260: oc * 520 + 520], pB[:, 0:260])
                t_dss = S.t("vector", f_dss, [t_ds])
                tok_ds_store[oc_] = t_dss
                st["pA"] = t_dss
                st["pB"] = t_dss
            return t_ds, t_su

        own = sb_i >= OWN_SB0
        finA = {}

        def emit_Af(c, sb_i=sb_i, own=own):
            g = sb_i * SBC + c
            par = c % 2
            oc = (sb_i - OWN_SB0) * SBC + c if own else None

            def f_x0f(eng, g=g, par=par):
                return eng.dma_start(X0P[:, par * 1024:(par + 1) * 1024],
                                     dx0[g * 128:(g + 1) * 128, :])
            tok_x0f = S.dma(f_x0f, 1, [x0_par_reader[par]])

            if own:
                x1_ap = X1O[:, oc * 1024:(oc + 1) * 1024]
            else:
                x1_ap = X1O[:, par * 1024:(par + 1) * 1024]

            def f_x1(eng, c=c, par=par, x1_ap=x1_ap):
                return eng.scalar_tensor_tensor(
                    x1_ap, ACC[:, c * 1024:(c + 1) * 1024], 0.5,
                    X0P[:, par * 1024:(par + 1) * 1024], ALU.mult, ALU.add)
            t_x1 = S.t("vector", f_x1, [tok_x0f, ep_tok[c]])
            x0_par_reader[par] = t_x1
            if own:
                own_x1_tok[oc] = t_x1
            y2dst = (SB1 if par == 0 else SB1B)[:]
            t_y2b = ln_to_y2b(t_x1, x1_ap, y2dst, "bSB1" if par == 0 else "bSB1B")
            finA[c] = (t_y2b, y2dst, par)

        def emit_Bf(c, sb_i=sb_i, own=own):
            nonlocal pend
            g = sb_i * SBC + c
            oc = (sb_i - OWN_SB0) * SBC + c if own else None
            t_y2b, y2dst, par = finA[c]
            t_y2fm = transpose8(y2dst, Y2FM[:], t_y2b)
            st["bSB1" if par == 0 else "bSB1B"] = t_y2fm

            t_kp = proj16(pC, "pC", t_y2fm, WA, 0, tok_wk)
            if pend is not None:
                emit_ds(pend)
            t_kr, t_kread = rope_relu(
                pC, SB2[:], "SB2", c, tok_rope, t_kp,
                scratch=(AFMf[:, 0:1024], AFMf[:, 1024:2048],
                         "AFMa", "AFMb"))
            st["pC"] = t_kread
            st["pCh0"] = t_kread
            st["pCh1"] = t_kread

            t_vp = proj16(pD, "pD", t_y2fm, WA, 8192, tok_wv)

            def f_v65(eng):
                pv = pD[:].rearrange("p (h d) -> p h d", d=64)
                ov = V65[:].rearrange("p (h z) -> p h z", z=65)[:, :, 0:64]
                return eng.tensor_copy(ov, pv)
            t_v65 = S.t("vector", f_v65, [t_vp, st.get("V65")])
            st["pD"] = t_v65
            st["pDh0"] = t_v65
            st["pDh1"] = t_v65
            pend = (g, own, oc, t_kr, t_v65)

        for c in range(SBC):
            if c == 0:
                emit_Af(0)
            if c + 1 < SBC:
                emit_Af(c + 1)
            emit_Bf(c)

        tok_fin_last_pe, tok_srun = emit_ds(pend)
        st["AFM"] = st["AFMa"]
        tok_rope_free = tok_fin_last_pe
        tok_w1_free = tok_fin_last_pe
        tok_w2_free = tok_w2r

    full_p1 = (_CFG['p1_sbs'] == NSB and _CFG['experts'] and _CFG['finish'])
    do_p2 = _CFG['p2'] and full_p1
    do_p3 = _CFG['p3'] and full_p1

    if do_p2:
        def f_wq(eng):
            return eng.dma_start(
                NF[:, 0:8192].rearrange("p (k o) -> p k o", o=1024),
                swb[0].rearrange("(k p) o -> p k o", p=128))
        tok_wq = S.dma(f_wq, 1, [tok_w1r, tok_wconv])

        def f_wo(eng):
            return eng.dma_start(
                ACC[:, 0:8192].rearrange("p (k o) -> p k o", o=1024),
                swb[3].rearrange("(k p) o -> p k o", p=128))
        tok_wo = S.dma(f_wo, 1, [own_x1_tok[N_OWN - 1]])

    tok_out_x2 = [None] * N_OWN
    tok_sbf_reader = None
    tok_rope = None
    p2_count = min(_CFG['p2_n'], N_OWN) if do_p2 else 0
    p2lnA = {}
    pend_x2 = [None]

    def emit_x2():
        poc, t_wop_p = pend_x2[0]
        pend_x2[0] = None

        def f_x2(eng, poc=poc):
            a = X1O[:, poc * 1024:(poc + 1) * 1024]
            return eng.tensor_tensor(a, a, pD[:], ALU.add)
        t_x2 = S.t("vector", f_x2, [t_wop_p])
        st["pD"] = t_x2
        st["pDh0"] = t_x2
        st["pDh1"] = t_x2
        tok_out_x2[poc] = t_x2
    AFM_SCR = (AFMf[:, 0:1024], AFMf[:, 1024:2048], "AFMa", "AFMb")
    for oc in range(p2_count):
        sb_i = OWN_SB0 + oc // SBC
        c_local = oc % SBC
        if c_local == 0:
            def f_rope2(eng, sb_i=sb_i):
                return eng.dma_start(
                    ROPE[:].rearrange("p (c z) -> p c z", z=64),
                    drope[sb_i * 1024:(sb_i + 1) * 1024, :].rearrange(
                        "(c p) z -> p c z", p=128))
            tok_rope = S.dma(f_rope2, 1, [tok_rope_free])

        x2_ap = X1O[:, oc * 1024:(oc + 1) * 1024]
        par = oc % 2
        lnbuf = (SB1 if par == 0 else SB1B)[:]
        lnkey = "bSB1" if par == 0 else "bSB1B"
        if oc in p2lnA:
            t_y2b = p2lnA.pop(oc)
        else:
            t_y2b = ln_to_y2b(own_x1_tok[oc], x2_ap, lnbuf, lnkey,
                              scratch=AFM_SCR)
        t_y2fm = transpose8(lnbuf, Y2FM[:], t_y2b)
        st[lnkey] = t_y2fm

        if _CFG['p2_upto'] <= 1:
            tok_out_x2[oc] = t_y2fm
            continue
        def f_qp(eng):
            r = []
            for oh in range(2):
                for k in range(8):
                    r.append(eng.matmul(
                        pC[:, oh * 512:(oh + 1) * 512],
                        Y2FM[:, k * 128:(k + 1) * 128],
                        NF[:, k * 1024 + oh * 512: k * 1024 + (oh + 1) * 512],
                        start=(k == 0), stop=(k == 7)))
            return r
        t_qp = S.t("tensor", f_qp, [t_y2fm, tok_wq, st.get("pC"),
                                    st.get("pCh0"), st.get("pCh1")])
        t_kp = proj16(pD, "pD", t_y2fm, WA, 0, tok_wk)
        t_qr, t_qread = rope_relu(pC, SB2[:], "SB2", c_local, tok_rope, t_qp)
        st["pC"] = t_qread
        st["pCh0"] = t_qread
        st["pCh1"] = t_qread
        t_qfm = transpose8(SB2[:], QFM[:], t_qr)
        st["SB2"] = t_qfm

        if _CFG['p2_upto'] <= 2:
            tok_out_x2[oc] = t_qfm
            continue
        t_kr, t_kread = rope_relu(pD, SB2[:], "SB2", c_local, tok_rope, t_kp)
        st["pD"] = t_kread
        st["pDh0"] = t_kread
        st["pDh1"] = t_kread
        tok_rope_free = t_kr
        if oc + 1 < p2_count:
            npar = (oc + 1) % 2
            p2lnA[oc + 1] = ln_to_y2b(
                own_x1_tok[oc + 1], X1O[:, (oc + 1) * 1024:(oc + 2) * 1024],
                (SB1 if npar == 0 else SB1B)[:],
                "bSB1" if npar == 0 else "bSB1B", scratch=AFM_SCR)

        t_vp = proj16(pC, "pC", t_y2fm, WA, 8192, tok_wv)
        t_kfm = transpose8(SB2[:], KFM[:], t_kr, extra=[st.get("KFM")])
        st["SB2"] = t_kfm

        def f_v65(eng):
            pv = pC[:].rearrange("p (h d) -> p h d", d=64)
            ov = V65[:].rearrange("p (h z) -> p h z", z=65)[:, :, 0:64]
            return eng.tensor_copy(ov, pv)
        t_v65 = S.t("vector", f_v65, [t_vp, st.get("V65")])
        st["pC"] = t_v65
        st["pCh0"] = t_v65
        st["pCh1"] = t_v65

        def f_sbf(eng):
            return eng.tensor_copy(SBF[:], SRUN[:])
        t_sbfr = S.t("vector", f_sbf, [tok_srun, tok_sbf_reader])

        if _CFG['p2_upto'] <= 4:
            tok_out_x2[oc] = t_sbfr
            continue
        SC_AP = [pA[:, 0:128], pC[:, 0:128], pA[:, 128:256], pC[:, 128:256]]
        NM_AP = [pB[:, 0:65], pC[:, 512:577], pB[:, 65:130], pC[:, 577:642]]
        X0Pb = X0P[:].bitcast(BF16)
        AB_AP = [ABUF[:, 0:128], ABUF[:, 128:256],
                 X0Pb[:, 0:128], X0Pb[:, 128:256]]
        t_mk_h = {}
        t_nm_h = {}
        t_ep_h = {}
        n_heads = min(16, _CFG['p2_nh'])

        def emit_sc_mk(h):
            hp4 = h % 4
            f_tile = h // 2
            b0 = (h % 2) * 64
            sc_ap = SC_AP[hp4]

            def f_sc(eng, f_tile=f_tile, b0=b0, sc_ap=sc_ap):
                return eng.matmul(
                    sc_ap,
                    KFM[b0:b0 + 64, f_tile * 128:(f_tile + 1) * 128],
                    QFM[b0:b0 + 64, f_tile * 128:(f_tile + 1) * 128],
                    start=True, stop=True)
            t_sc = S.t("tensor", f_sc, [t_kfm, t_qfm, t_v65,
                                        t_mk_h.get(h - 2)])

            def f_mk(eng, hp4=hp4, sc_ap=sc_ap):
                return eng.tensor_tensor(AB_AP[hp4], sc_ap, cTRI[:],
                                         ALU.mult)
            xg = [x0_par_reader[0], x0_par_reader[1]] \
                if (oc == 0 and hp4 >= 2) else []
            t_mk_h[h] = S.t("vector", f_mk, [t_sc, tok_const] + xg)

        def emit_nm_ep(h):
            hp4 = h % 4
            f_tile = h // 2
            b0 = (h % 2) * 64
            p = h // 2
            nm_ap = NM_AP[hp4]

            def f_nm(eng, h=h, hp4=hp4, f_tile=f_tile, b0=b0, p=p,
                     nm_ap=nm_ap):
                r = [eng.matmul(nm_ap,
                                AB_AP[hp4],
                                V65[:, h * 65:(h + 1) * 65],
                                start=True, stop=False)]
                base = 0 if p < 4 else 260
                sb_t = SBF[b0:b0 + 64,
                           base + (p % 4) * 65: base + (p % 4) * 65 + 65]
                r.append(eng.matmul(nm_ap,
                                    QFM[b0:b0 + 64,
                                        f_tile * 128:(f_tile + 1) * 128],
                                    sb_t, start=False, stop=True))
                return r
            t_nm = S.t("tensor", f_nm, [t_mk_h[h], t_v65, t_sbfr,
                                        t_ep_h.get(h - 2)])
            t_nm_h[h] = t_nm

            def f_ep(eng, h=h, nm_ap=nm_ap):
                eng.tensor_scalar(vec(vRD), nm_ap[:, 64:65],
                                  EPS_ATT, None, ALU.add)
                eng.tensor_copy(VEC[:, 12:13], VEC[:, 15:16])
                eng.reciprocal(vec(vRR), vec(vRD))
                eng.tensor_copy(VEC[:, 12:13], VEC[:, 15:16])
                return eng.tensor_scalar(SC2[:, h * 64:(h + 1) * 64],
                                         nm_ap[:, 0:64], vec(vRR),
                                         None, ALU.mult)
            dep_sc2 = [st.get("SC2")] if h == 0 else []
            t_ep_h[h] = S.t("vector", f_ep, [t_nm] + dep_sc2)

        for h in range(n_heads):
            emit_sc_mk(h)
            if h >= 3:
                emit_nm_ep(h - 3)
        for h in range(max(0, n_heads - 3), n_heads):
            emit_nm_ep(h)
        last_nm = t_nm_h[n_heads - 1]
        attn_tok = t_ep_h[n_heads - 1]
        st["pA"] = attn_tok
        st["pB"] = attn_tok
        st["pC"] = attn_tok
        st["pCh0"] = attn_tok
        st["pCh1"] = attn_tok
        tok_sbf_reader = last_nm
        st["V65"] = last_nm

        if _CFG['p2_upto'] <= 5:
            tok_out_x2[oc] = attn_tok
            continue
        def f_sup(eng, oc=oc):
            eng.tensor_tensor(SRUN[:, 0:260], SRUN[:, 0:260],
                              DS[:, oc * 520: oc * 520 + 260], ALU.add)
            return eng.tensor_tensor(SRUN[:, 260:520], SRUN[:, 260:520],
                                     DS[:, oc * 520 + 260: oc * 520 + 520],
                                     ALU.add)
        tok_srun = S.t("vector", f_sup, [tok_ds_store[oc], tok_srun])

        def f_atb(eng):
            return eng.tensor_copy(SB2[:], SC2[:])
        t_atb = S.t("vector", f_atb, [attn_tok, st.get("SB2")])
        st["SC2"] = t_atb
        t_afm = transpose8(SB2[:], KFM[:], t_atb, extra=[last_nm])
        st["SB2"] = t_afm
        st["KFM"] = None

        if _CFG['p2_upto'] <= 6:
            tok_out_x2[oc] = t_afm
            continue

        def f_wop(eng):
            r = []
            for oh in range(2):
                for k in range(8):
                    r.append(eng.matmul(
                        pD[:, oh * 512:(oh + 1) * 512],
                        KFM[:, k * 128:(k + 1) * 128],
                        ACC[:, k * 1024 + oh * 512: k * 1024 + (oh + 1) * 512],
                        start=(k == 0), stop=(k == 7)))
            return r
        t_wop = S.t("tensor", f_wop, [t_afm, tok_wo, st.get("pD"),
                                      st.get("pDh0"), st.get("pDh1")])
        st["KFM"] = t_wop

        def f_x2(eng, oc=oc):
            a = X1O[:, oc * 1024:(oc + 1) * 1024]
            return eng.tensor_tensor(a, a, pD[:], ALU.add)
        t_x2 = S.t("vector", f_x2, [t_wop])
        st["pD"] = t_x2
        st["pDh0"] = t_x2
        st["pDh1"] = t_x2
        tok_out_x2[oc] = t_x2

    tok_out_dma = []
    if not do_p3:
        for oc in range(N_OWN):
            def f_outd(eng, oc=oc):
                return eng.dma_start(dout[oc * 128:(oc + 1) * 128, :],
                                     X1O[:, oc * 1024:(oc + 1) * 1024])
            dep = [tok_out_x2[oc] if (do_p2 and tok_out_x2[oc]) else None,
                   ("vector", S.cnt["vector"]), ("tensor", S.cnt["tensor"]),
                   ("scalar", S.cnt["scalar"]), ("gpsimd", S.cnt["gpsimd"])]
            tok_out_dma.append(S.dma(f_outd, 1, dep))
    else:
        tok_p2_done = tok_out_x2[N_OWN - 1]
        tok_w1_free = tok_p2_done
        tok_w2_free_p3 = tok_w2r
        out_par_reader = [None, None]
        tok_out_dma = []
        tok_w1r_p3 = None
        st["ACC"] = tok_p2_done
        st["SB1"] = tok_p2_done
        st["SB2"] = tok_p2_done
        st["SC2"] = tok_p2_done
        st["AFM"] = tok_p2_done
        st["bSB1"] = tok_p2_done
        st["bSB1B"] = tok_p2_done

        for sb2 in range(2):
            tok_acc = [None] * SBC
            tok_nf_blk = [None, None]
            preA3 = {}
            NF3 = [(SB2, "bSB2"), (SB2B, "bSB2B"), (SB1B, "bSB1B")]

            def emit_A3(c, sb2=sb2):
                if c in preA3 or c >= SBC:
                    return
                oc = sb2 * SBC + c
                tri = c % 3
                x_ap = X1O[:, oc * 1024:(oc + 1) * 1024]
                extra = []
                if sb2 == 0 and c == 0:
                    extra = [tok_p2_done]
                if sb2 == 1 and c == 0:
                    extra = [out_par_reader[0], out_par_reader[1]]
                nf_dst, nf_key = NF3[tri]
                ta3 = prepass_A(tok_out_x2[oc], x_ap,
                                ACC[:, c * 1024:(c + 1) * 1024],
                                SB1[:], "bSB1", nf_dst[:], nf_key,
                                extra_deps=extra)
                preA3[c] = ta3

            def emit_B3(c):
                tri = c % 3
                nf_src, nf_key = NF3[tri]
                t_acc, t_nf = prepass_B(c, nf_src[:], preA3[c],
                                        ACC[:, c * 1024:(c + 1) * 1024])
                st[nf_key] = t_nf
                tok_acc[c] = t_acc
                if c == 3:
                    tok_nf_blk[0] = t_nf
                if c == SBC - 1:
                    tok_nf_blk[1] = t_nf

            def emit_pre3(c0, c1):
                for c in range(c0, c1):
                    emit_A3(c)
                    emit_A3(c + 1)
                    emit_A3(c + 2)
                    emit_B3(c)

            emit_pre3(0, 4)
            tok_w1r_p3, tok_w2r_p3, ep_tok = ffn_experts_sb(
                tok_nf_blk, tok_acc,
                tok_w1_free if sb2 == 0 else tok_w1r_p3,
                tok_w2_free_p3,
                mid_cb=lambda: emit_pre3(4, SBC))
            tok_w2_free_p3 = tok_w2r_p3

            for c in range(SBC):
                oc = sb2 * SBC + c
                par = c % 2
                x3buf = SC2 if par == 0 else SC3

                def f_x3(eng, oc=oc, c=c, x3buf=x3buf):
                    return eng.scalar_tensor_tensor(
                        x3buf[:], ACC[:, c * 1024:(c + 1) * 1024], 0.5,
                        X1O[:, oc * 1024:(oc + 1) * 1024], ALU.mult, ALU.add)
                t_x3 = S.t("vector", f_x3, [ep_tok[c], out_par_reader[par],
                                            st.get("SC2" if par == 0 else "SC3")])
                if par == 0:
                    st["SC2"] = t_x3
                else:
                    st["SC3"] = t_x3

                def f_out(eng, oc=oc, x3buf=x3buf):
                    return eng.dma_start(dout[oc * 128:(oc + 1) * 128, :], x3buf[:])
                tok_o = S.dma(f_out, 1, [t_x3])
                out_par_reader[par] = tok_o
                tok_out_dma.append(tok_o)

    if _CFG['dbg'] is not None:
        barrier = [("tensor", S.cnt["tensor"]), ("vector", S.cnt["vector"]),
                   ("scalar", S.cnt["scalar"]), ("gpsimd", S.cnt["gpsimd"]),
                   ("sync", S.cnt["sync"])]
        dbg_src = {"acc": ACC, "nf": NF, "xc": SB1, "sb2": SB2,
                   "comb": COMB, "ctb": CTB, "small": SMALL,
                   "x0p": X0P, "sc3": SC3, "wa": WA, "gwt": cGWT,
                   "b1c": cB1C, "b2r": cB2R}[_CFG['dbg']]
        ncols = dbg_src.shape[1] if hasattr(dbg_src, 'shape') else 8192
        ncols = {"acc": 8192, "nf": 8192, "xc": 1024, "sb2": 1024,
                 "comb": 64, "ctb": 128, "small": 64, "x0p": 2048,
                 "sc3": 1024, "wa": 16384, "gwt": 64, "b1c": 128,
                 "b2r": 1024}[_CFG['dbg']]
        prevtok = None
        for c in range((ncols + 1023) // 1024):
            w = min(1024, ncols - c * 1024)

            def f_cp(eng, c=c, w=w, dbg_src=dbg_src):
                return eng.tensor_copy(SC1[:, 0:w], dbg_src[:, c * 1024:c * 1024 + w])
            tcp = S.t("vector", f_cp, barrier + [prevtok])

            def f_dmp(eng, c=c, w=w):
                return eng.dma_start(dout[c * 128:(c + 1) * 128, 0:w], SC1[:, 0:w])
            prevtok = S.dma(f_dmp, 1, [tcp])

    def f_fin(eng):
        return eng.engine_nop()
    S.t("vector", f_fin, [("sync", S.cnt["sync"])])

    blk = ctx.enter_context(nc.Block())
    S.replay(blk, sems)
    ctx.close()
    return nc


_NC_CACHE = {}


def _get_nc():
    if "nc" not in _NC_CACHE:
        _NC_CACHE["nc"] = build_program()
    return _NC_CACHE["nc"]


def _host_prep(inputs):
    x = np.asarray(inputs["L0_attn_x"])
    wq = np.asarray(inputs["L0_attn_wq"])
    wk = np.asarray(inputs["L0_attn_wk"])
    wv = np.asarray(inputs["L0_attn_wv"])
    wo = np.asarray(inputs["L0_attn_wo"])
    nw = np.asarray(inputs["L0_ffn_norm_weight"])
    gw = np.asarray(inputs["L0_ffn_gate_weight"])
    gb = np.asarray(inputs["L0_ffn_gate_bias"])
    w1 = np.asarray(inputs["L0_ffn_mlp1_weight"])
    b1 = np.asarray(inputs["L0_ffn_mlp1_bias"])
    w2 = np.asarray(inputs["L0_ffn_mlp2_weight"])
    b2 = np.asarray(inputs["L0_ffn_mlp2_bias"])

    perm = np.concatenate([
        np.concatenate([h * 64 + 2 * np.arange(32),
                        h * 64 + 2 * np.arange(32) + 1])
        for h in range(NH)])
    wqp = np.ascontiguousarray(wq[:, perm], dtype=np.float32)
    wkp = np.ascontiguousarray(wk[:, perm], dtype=np.float32)

    w1t = np.empty((NEXP, D, DFF), dtype=BFD)
    b1c_all = np.empty((CH, 128), dtype=np.float32)
    for e in range(NEXP):
        T = w1[e].T
        w1t[e] = np.concatenate([T[:, 0::2], T[:, 1::2]], axis=1)
        bp = np.concatenate([b1[e][0::2], b1[e][1::2]]).astype(np.float32)
        b1c_all[:, e * 16:(e + 1) * 16] = bp.reshape(16, CH).T
    w2t = np.ascontiguousarray(np.transpose(w2, (0, 2, 1)))
    gwt = np.ascontiguousarray(gw.T)
    gbc = np.ascontiguousarray(np.broadcast_to(gb.astype(np.float32), (CH, NEXP)))
    nwc = np.ascontiguousarray(np.broadcast_to(nw.astype(np.float32), (CH, D)))
    b2r = np.ascontiguousarray(b2)

    tri = (np.arange(CH)[:, None] <= np.arange(CH)[None, :]).astype(np.float32)
    iota8 = np.ascontiguousarray(
        np.broadcast_to(np.arange(NEXP, dtype=np.float32), (CH, NEXP)))
    io9 = iota8 - 9.0
    inv_freq = (1.0 / THETA ** (np.arange(0, HD, 2).astype(np.float32) / HD)
                ).astype(np.float32)

    in_maps = []
    for core in range(8):
        b = core // 4
        q = core % 4
        permq = list(range(q)) + list(range(q + 1, 4)) + [q]
        x0r = np.ascontiguousarray(
            np.concatenate([x[b, 2048 * j:2048 * (j + 1)] for j in permq],
                           axis=0), dtype=np.float32)
        smask = np.zeros((CH, NCH), dtype=np.float32)
        for i in range(NCH):
            if i // 16 < q:
                smask[:, i] = 1.0
        ropet = np.empty((NCH * CH, 64), dtype=np.float32)
        for i in range(NCH):
            quarter = permq[i // 16]
            base = 2048 * quarter + 128 * (i % 16)
            t = (base + np.arange(CH, dtype=np.float32))[:, None]
            freqs = (t * inv_freq[None, :]).astype(np.float32)
            ropet[i * CH:(i + 1) * CH, 0:32] = np.cos(
                freqs.astype(np.float64)).astype(np.float32)
            ropet[i * CH:(i + 1) * CH, 32:64] = np.sin(
                freqs.astype(np.float64)).astype(np.float32)
        in_maps.append({
            "x0r": x0r, "w1t": w1t, "w2t": w2t, "b1c": b1c_all, "b2r": b2r,
            "gwt": gwt, "gbc": gbc, "nwc": nwc,
            "wqf": wqp, "wkf": wkp,
            "wvf": np.ascontiguousarray(wv, dtype=np.float32),
            "wof": np.ascontiguousarray(wo, dtype=np.float32),
            "rope": ropet, "smask": smask, "tri": tri,
            "iota8": iota8, "io9": io9,
        })
    return in_maps


def kernel(**inputs):
    in_maps = _host_prep(inputs)
    nc = _get_nc()
    res = run_bass_kernel_spmd(nc, in_maps, core_ids=list(range(8)))
    out = np.empty((B, L, D), dtype=np.float32)
    for core in range(8):
        b = core // 4
        q = core % 4
        out[b, 2048 * q:2048 * (q + 1), :] = res.results[core]["xout"]
    return out


# version 18
# speedup vs baseline: 1.0586x; 1.0288x over previous
import numpy as np
import ml_dtypes
from contextlib import ExitStack
import concourse.bass as bass
import concourse.mybir as mybir
from concourse.bass_utils import run_bass_kernel_spmd

F32 = mybir.dt.float32
BF16 = mybir.dt.bfloat16
AF = mybir.ActivationFunctionType
ALU = mybir.AluOpType
AX = mybir.AxisListType
BFD = ml_dtypes.bfloat16

D = 1024
NH, HD, NPR = 16, 64, 32
L, B = 8192, 2
CH = 128
NCH = 64
SBC = 8
NSB = NCH // SBC
NEXP = 8
DFF = 2048
OWN_SB0 = 6
N_OWN = 16
EPS = 1e-5
EPS_ATT = 1e-6
THETA = 10000.0
LIMIT = 7.0
ALPHA = 1.702

_CFG = dict(p1_sbs=NSB, experts=True, finish=True, p2=True, p3=True, pre_upto=99, dbg=None, p2_n=N_OWN, p2_upto=99, p2_nh=16, p2_hupto=99)


class Sched:
    ENGS = ("sync", "tensor", "vector", "scalar", "gpsimd")

    def __init__(self):
        self.tasks = {e: [] for e in self.ENGS}
        self.cnt = {e: 0 for e in self.ENGS}

    def _mk(self, eng, fn, deps, inc):
        w = {}
        for d in deps:
            if d is None:
                continue
            de, dv = d
            if de == eng and eng != "sync":
                continue
            if dv <= 0:
                continue
            w[de] = max(w.get(de, 0), dv)
        self.tasks[eng].append((sorted(w.items()), fn))
        self.cnt[eng] += inc
        return (eng, self.cnt[eng])

    def t(self, eng, fn, deps=()):
        return self._mk(eng, fn, deps, 1)

    def dma(self, fn, n, deps=()):
        return self._mk("sync", fn, deps, 16 * n)

    def replay(self, blk, sems):
        def runner(eng_name):
            tasks = self.tasks[eng_name]

            def run(eng):
                for waits, fn in tasks:
                    for (se, v) in waits:
                        eng.wait_ge(sems[se], v)
                    r = fn(eng)
                    rl = r if isinstance(r, (list, tuple)) else [r]
                    if eng_name == "sync":
                        for inst in rl:
                            inst.then_inc(sems["sync"], 16)
                    else:
                        rl[-1].then_inc(sems[eng_name], 1)
            return run
        blk.sync(runner("sync"))
        blk.tensor(runner("tensor"))
        blk.vector(runner("vector"))
        blk.scalar(runner("scalar"))
        blk.gpsimd(runner("gpsimd"))


def build_program():
    nc = bass.Bass("TRN2", target_bir_lowering=False,
                   dynamic_dma_scratch_size=2048)

    dx0 = nc.dram_tensor("x0r", [NCH * CH, D], F32, kind="ExternalInput")
    dw1t = nc.dram_tensor("w1t", [NEXP, D, DFF], BF16, kind="ExternalInput")
    dw2t = nc.dram_tensor("w2t", [NEXP, D, D], BF16, kind="ExternalInput")
    db1c = nc.dram_tensor("b1c", [CH, 128], F32, kind="ExternalInput")
    db2r = nc.dram_tensor("b2r", [NEXP, D], BF16, kind="ExternalInput")
    dgwt = nc.dram_tensor("gwt", [D, NEXP], BF16, kind="ExternalInput")
    dgbc = nc.dram_tensor("gbc", [CH, NEXP], F32, kind="ExternalInput")
    dnwc = nc.dram_tensor("nwc", [CH, D], F32, kind="ExternalInput")
    dwq = nc.dram_tensor("wqf", [D, D], F32, kind="ExternalInput")
    dwk = nc.dram_tensor("wkf", [D, D], F32, kind="ExternalInput")
    dwv = nc.dram_tensor("wvf", [D, D], F32, kind="ExternalInput")
    dwo = nc.dram_tensor("wof", [D, D], F32, kind="ExternalInput")
    drope = nc.dram_tensor("rope", [NCH * CH, 64], F32, kind="ExternalInput")
    dsmk = nc.dram_tensor("smask", [CH, NCH], F32, kind="ExternalInput")
    dtri = nc.dram_tensor("tri", [CH, CH], F32, kind="ExternalInput")
    diota = nc.dram_tensor("iota8", [CH, NEXP], F32, kind="ExternalInput")
    dio9 = nc.dram_tensor("io9", [CH, NEXP], F32, kind="ExternalInput")
    dout = nc.dram_tensor("xout", [N_OWN * CH, D], F32, kind="ExternalOutput")
    swb = nc.dram_tensor("wbf_scratch", [4, D, D], BF16, kind="Internal")

    ctx = ExitStack()

    def sbuf(name, shape, dtype):
        return ctx.enter_context(nc.sbuf_tensor(name, shape, dtype))

    NF = sbuf("NF", [128, 8192], BF16)
    ACC = sbuf("ACCb", [128, 8192], BF16)
    WA = sbuf("WA", [128, 24576], BF16)
    X1O = sbuf("X1O", [128, 16384], F32)
    DS = sbuf("DSo", [128, N_OWN * 520], BF16)
    SRUN = sbuf("SRUN", [128, 520], F32)
    SBF = sbuf("SBF", [128, 520], BF16)
    ROPE = sbuf("ROPEt", [128, SBC * 64], F32)
    AFM = sbuf("ACTFM", [128, 4096], BF16)
    X0P = sbuf("X0P", [128, 2048], F32)
    SC1 = sbuf("SC1", [128, 1024], F32)
    SC2 = sbuf("SC2", [128, 1024], F32)
    SC3 = sbuf("SC3", [128, 1024], F32)
    SB1 = sbuf("SB1", [128, 1024], BF16)
    SB2 = sbuf("SB2", [128, 1024], BF16)
    SB1B = sbuf("SB1B", [128, 1024], BF16)
    SB2B = sbuf("SB2B", [128, 1024], BF16)
    Y2FM = sbuf("Y2FM", [128, 1024], BF16)
    QFM = sbuf("QFM", [128, 1024], BF16)
    KFM = sbuf("KFM", [128, 1024], BF16)
    V65 = sbuf("V65", [128, 1040], BF16)
    ABUF = sbuf("ABUF", [128, 256], BF16)
    CTB = sbuf("CTB", [128, 128], BF16)
    COMB = sbuf("COMB", [128, 64], F32)
    SMALL = sbuf("SMALL", [128, 64], F32)
    VEC = sbuf("VEC", [128, 24], F32)
    cNWC = sbuf("cNWC", [128, 1024], F32)
    cGWT = sbuf("cGWT", [128, 64], BF16)
    cGBC = sbuf("cGBC", [128, 8], F32)
    cB1C = sbuf("cB1C", [128, 128], F32)
    cB2R = sbuf("cB2R", [128, 1024], BF16)
    cIOTA = sbuf("cIOTA", [128, 8], F32)
    cIO9 = sbuf("cIO9", [128, 8], F32)
    cTRI = sbuf("cTRI", [128, 128], F32)
    cIDB = sbuf("cIDB", [128, 128], BF16)
    cIDF = sbuf("cIDF", [128, 128], F32)
    cSMK = sbuf("cSMK", [128, 64], F32)

    pA = ctx.enter_context(nc.psum_tensor("pA", [128, 512], F32))
    pB = ctx.enter_context(nc.psum_tensor("pB", [128, 512], F32))
    pC = ctx.enter_context(nc.psum_tensor("pC", [128, 1024], F32))
    pD = ctx.enter_context(nc.psum_tensor("pD", [128, 1024], F32))
    pT = ctx.enter_context(nc.psum_tensor("pT", [128, 512], F32))
    pM = ctx.enter_context(nc.psum_tensor("pM", [128, 456], F32))

    sems = {e: ctx.enter_context(nc.semaphore("sem_" + e)) for e in Sched.ENGS}
    S = Sched()

    vRED, vM, vVAR, vRSTD, vR2, vDD, vEE, vW1, vW2, vRD, vRR = range(11)

    def vec(i):
        return VEC[:, i:i + 1]

    st = {}

    def f_ld_const(eng):
        return [eng.dma_start(cNWC[:], dnwc[:]),
                eng.dma_start(cGWT[:].rearrange("p (k e) -> p k e", e=NEXP),
                              dgwt.rearrange("(k p) e -> p k e", p=128)),
                eng.dma_start(cGBC[:], dgbc[:]),
                eng.dma_start(cB1C[:], db1c[:]),
                eng.dma_start(cB2R[0:8, :], db2r[:]),
                eng.dma_start(cIOTA[:], diota[:]),
                eng.dma_start(cIO9[:], dio9[:]),
                eng.dma_start(cTRI[:], dtri[:]),
                eng.dma_start(cSMK[:], dsmk[:])]
    tok_const = S.dma(f_ld_const, 9)

    def f_ident_a(eng):
        eng.memset(cIDB[:], 1.0)
        eng.memset(cIDF[:], 1.0)
        eng.memset(V65[:], 1.0)
        eng.memset(VEC[:, 15:16], 1.0)
        eng.memset(SRUN[:], 0.0)
        eng.memset(VEC[:, 11:12], 1.0)
        return eng.affine_select(cIDF[:], cIDF[:], [[-1, 128]], ALU.is_equal,
                                 0.0, base=0, channel_multiplier=1)
    tok_ident_a = S.t("gpsimd", f_ident_a)

    def f_ident_b(eng):
        return eng.affine_select(cIDB[:], cIDB[:], [[-1, 128]], ALU.is_equal,
                                 0.0, base=0, channel_multiplier=1)
    tok_ident = S.t("gpsimd", f_ident_b, [tok_ident_a])

    prev = tok_const
    for wi, dw in enumerate((dwq, dwk, dwv, dwo)):
        def f_ldw(eng, dw=dw):
            return eng.dma_start(
                X1O[:, 0:8192].rearrange("p (k o) -> p k o", o=1024),
                dw.rearrange("(k p) o -> p k o", p=128))
        tok_l = S.dma(f_ldw, 1, [prev])

        def f_cv(eng):
            return eng.tensor_copy(NF[:, 0:8192], X1O[:, 0:8192])
        tok_c = S.t("vector", f_cv, [tok_l])

        def f_stw(eng, wi=wi):
            return eng.dma_start(
                swb[wi].rearrange("(k p) o -> p k o", p=128),
                NF[:, 0:8192].rearrange("p (k o) -> p k o", o=1024))
        prev = S.dma(f_stw, 1, [tok_c])
    tok_wconv = prev

    def transpose8(src_bf, dst_fm, dep, extra=()):
        atok = None
        for g in (0, 4):
            def fpe(eng, g=g, src=src_bf):
                return [eng.matmul(pT[:, j * 128:(j + 1) * 128],
                                   src[:, (g + j) * 128:(g + j + 1) * 128],
                                   cIDB[:], start=True, stop=True)
                        for j in range(4)]
            d = [dep, tok_ident, st.get("pT")] + list(extra)
            ptok = S.t("tensor", fpe, d)

            def fac(eng, g=g, dst=dst_fm):
                return eng.copy(dst[:, g * 128:(g + 4) * 128], pT[:])
            atok = S.t("scalar", fac, [ptok])
            st["pT"] = atok
        return atok

    def transpose8_strided(src_bf, c_local, dep):
        atok = None
        for g in (0, 4):
            def fpe(eng, g=g, src=src_bf):
                return [eng.matmul(pT[:, j * 128:(j + 1) * 128],
                                   src[:, (g + j) * 128:(g + j + 1) * 128],
                                   cIDB[:], start=True, stop=True)
                        for j in range(4)]
            ptok = S.t("tensor", fpe, [dep, tok_ident, st.get("pT")])

            def fac(eng, g=g, c_local=c_local):
                ov = NF[:].rearrange("p (k t) -> p k t", t=1024)[
                    :, g:g + 4, c_local * 128:(c_local + 1) * 128]
                iv = pT[:].rearrange("p (k t) -> p k t", t=128)
                return eng.copy(ov, iv)
            atok = S.t("scalar", fac, [ptok])
            st["pT"] = atok
        return atok

    def prepass_A(x_tok, x_ap, xc_ap, xm_buf, xm_key, nf_dst, nf_key,
                  extra_deps=()):
        LN2 = VEC[:, 16:18]

        def fa1(eng, x_ap=x_ap, xm_buf=xm_buf):
            eng.tensor_tensor(xm_buf, x_ap, x_ap, ALU.mult)
            eng.reduce_sum(vec(vRED), x_ap, AX.X)
            eng.reduce_sum(vec(vVAR), xm_buf, AX.X)
            eng.tensor_scalar(vec(vM), vec(vRED), 1.0 / D, None, ALU.mult)
            eng.tensor_copy(VEC[:, 12:13], VEC[:, 15:16])
            eng.tensor_tensor(VEC[:, 18:19], vec(vM), vec(vM), ALU.mult)
            eng.tensor_scalar(vec(vVAR), vec(vVAR), 1.0 / D, None, ALU.mult)
            eng.tensor_copy(VEC[:, 12:13], VEC[:, 15:16])
            eng.tensor_tensor(vec(vVAR), vec(vVAR), VEC[:, 18:19], ALU.subtract)
            eng.tensor_copy(VEC[:, 12:13], VEC[:, 15:16])
            eng.tensor_scalar(LN2[:, 0:1], vec(vVAR), EPS, None, ALU.add)
            eng.tensor_copy(VEC[:, 12:13], VEC[:, 15:16])
            eng.reciprocal(VEC[:, 19:20], LN2[:, 0:1])
            eng.tensor_copy(VEC[:, 12:13], VEC[:, 15:16])
            return eng.tensor_scalar(LN2[:, 1:2], vec(vVAR), VEC[:, 19:20],
                                     EPS, ALU.mult, ALU.add)
        ta1 = S.t("vector", fa1, [x_tok, st.get(xm_key)] + list(extra_deps))

        def fa2(eng):
            eng.activation(LN2[:], LN2[:], AF.Ln)
            return eng.activation(LN2[:], LN2[:], AF.Exp, scale=-0.5)
        ta2 = S.t("scalar", fa2, [ta1])

        def fa3(eng, xc_ap=xc_ap, nf_dst=nf_dst, x_ap=x_ap):
            eng.tensor_scalar(xc_ap, x_ap, vec(vM), LN2[:, 0:1],
                              ALU.subtract, ALU.mult)
            eng.tensor_copy(VEC[:, 12:13], VEC[:, 15:16])
            return eng.scalar_tensor_tensor(nf_dst, xc_ap, LN2[:, 1:2],
                                            cNWC[:], ALU.mult, ALU.mult)
        ta3 = S.t("vector", fa3, [ta2, st.get(nf_key), tok_const])
        st[xm_key] = ta3
        st[nf_key] = ta3
        return ta3

    def prepass_B(c_local, nf_src, nf_tok, xc_ap):
        tok_nf = transpose8_strided(nf_src, c_local, nf_tok)

        def f7(eng, c_local=c_local):
            return [eng.matmul(
                pM[:, 0:8],
                NF[:, k * 1024 + c_local * 128: k * 1024 + (c_local + 1) * 128],
                cGWT[:, k * 8:(k + 1) * 8],
                start=(k == 0), stop=(k == 7)) for k in range(8)]
        t7 = S.t("tensor", f7, [tok_nf, tok_const, st.get("pMg"),
                                st.get("pMb")])

        def f8(eng):
            GAP = lambda: eng.tensor_copy(VEC[:, 12:13], VEC[:, 15:16])
            lgb = SMALL[:, 8:12].bitcast(BF16)[:, 0:8]
            eng.tensor_tensor(lgb, pM[:, 0:8], cGBC[:], ALU.add)
            GAP()
            lgf = SMALL[:, 16:24]
            eng.tensor_copy(lgf, lgb)
            GAP()
            m8 = SMALL[:, 0:8]
            eng.max(out=m8, in_=lgf)
            GAP()
            e1 = SMALL[:, 24:32]
            eng.tensor_scalar(e1, lgf, m8[:, 0:1], None, ALU.is_equal)
            GAP()
            tq = SMALL[:, 32:40]
            eng.tensor_tensor(tq, e1, cIO9[:], ALU.mult)
            GAP()
            eng.tensor_scalar(tq, tq, 9.0, None, ALU.add)
            GAP()
            eng.tensor_reduce(vec(vDD), tq, AX.X, ALU.min)
            eng.tensor_tensor(vec(vEE), m8[:, 1:2], m8[:, 0:1], ALU.subtract)
            m1 = SMALL[:, 40:48]
            eng.tensor_scalar(m1, cIOTA[:], vec(vDD), None, ALU.is_equal)
            GAP()
            lg2 = SMALL[:, 48:56]
            eng.scalar_tensor_tensor(lg2, m1, -1.0e30, lgf, ALU.mult, ALU.add)
            GAP()
            eng.tensor_scalar(e1, lg2, m8[:, 1:2], None, ALU.is_equal)
            GAP()
            eng.tensor_tensor(tq, e1, cIO9[:], ALU.mult)
            GAP()
            eng.tensor_scalar(tq, tq, 9.0, None, ALU.add)
            GAP()
            eng.tensor_reduce(vec(vDD), tq, AX.X, ALU.min)
            eng.tensor_copy(vec(vRD), vec(vEE))
            m2 = SMALL[:, 56:64]
            return eng.tensor_scalar(m2, cIOTA[:], vec(vDD), None,
                                     ALU.is_equal)
        t8 = S.t("vector", f8, [t7, tok_const])
        st["pMg"] = t8
        st["pMb"] = t8

        def f9(eng):
            return eng.activation(vec(vEE), vec(vEE), AF.Exp)
        t9 = S.t("scalar", f9, [t8])

        def f10(eng, c_local=c_local):
            den = vec(vRD)
            eng.tensor_scalar(den, vec(vEE), 1.0, None, ALU.add)
            eng.tensor_copy(VEC[:, 12:13], VEC[:, 15:16])
            eng.reciprocal(vec(vW1), den)
            eng.tensor_copy(VEC[:, 12:13], VEC[:, 15:16])
            eng.tensor_tensor(vec(vW2), vec(vEE), vec(vW1), ALU.mult)
            m1 = SMALL[:, 40:48]
            m2 = SMALL[:, 56:64]
            cmb = COMB[:, c_local * 8:(c_local + 1) * 8]
            tq = SMALL[:, 32:40]
            eng.tensor_scalar(tq, m1, vec(vW1), None, ALU.mult)
            eng.tensor_copy(VEC[:, 12:13], VEC[:, 15:16])
            return eng.scalar_tensor_tensor(cmb, m2, vec(vW2), tq,
                                            ALU.mult, ALU.add)
        t10 = S.t("vector", f10, [t9, st.get("COMB")])

        def f11(eng, c_local=c_local):
            return eng.transpose(pM[0:8, 72:200],
                                 COMB[:, c_local * 8:(c_local + 1) * 8], cIDF[:])
        t11 = S.t("tensor", f11, [t10, tok_ident_a, tok_ident,
                                  st.get("pMct"), st.get("pMb")])

        def f12(eng):
            return eng.tensor_copy(CTB[0:8, :], pM[0:8, 72:200])
        t12 = S.t("vector", f12, [t11, st.get("CTB")])
        st["pMct"] = t12
        st["pMb"] = t12

        def f13(eng):
            return [eng.matmul(pC[:, hh * 512:(hh + 1) * 512], CTB[0:8, :],
                               cB2R[0:8, hh * 512:(hh + 1) * 512],
                               start=True, stop=True) for hh in range(2)]
        t13 = S.t("tensor", f13, [t12, tok_const, st.get("pC"),
                                  st.get("pCh0"), st.get("pCh1")])
        st["CTB"] = t13

        def f14(eng, xc_ap=xc_ap):
            return eng.tensor_tensor(xc_ap, xc_ap, pC[:], ALU.add)
        t14 = S.t("vector", f14, [t13])
        st["pC"] = t14
        st["pCh0"] = t14
        st["pCh1"] = t14
        return t14, tok_nf

    def ffn_experts_sb(tok_nf_blk, comb_tok, tok_w1_free, tok_w2_free,
                       mid_cb=None):
        ep_tok = [None] * SBC
        tok_w1_read = None
        tok_w2_read = None
        for e in range(NEXP):
            def f_w1(eng, e=e):
                return eng.dma_start(
                    WA[:, 0:16384].rearrange("p (k o) -> p k o", o=2048),
                    dw1t[e].rearrange("(k p) o -> p k o", p=128))
            tok_lw1 = S.dma(f_w1, 1, [tok_w1_read if e > 0 else tok_w1_free])

            def f_w2(eng, e=e):
                return eng.dma_start(
                    WA[:, 16384:24576].rearrange("p (k o) -> p k o", o=1024),
                    dw2t[e].rearrange("(k p) o -> p k o", p=128))
            tok_lw2 = S.dma(f_w2, 1, [tok_w2_read if e > 0 else tok_w2_free])

            for blk in range(2):
                pool_last = None
                tE3_vals = []
                for jp in range(8):
                    if jp % 2 == 0:
                        pg, pl = pA[:], pB[:]
                        kg, kl = "pA", "pB"
                    else:
                        pg, pl = pC[:, 0:512], pC[:, 512:1024]
                        kg, kl = "pCh0", "pCh1"

                    def fG(eng, blk=blk, jp=jp, pg=pg):
                        return [eng.matmul(
                            pg,
                            WA[:, k * 2048 + jp * 128: k * 2048 + (jp + 1) * 128],
                            NF[:, k * 1024 + blk * 512: k * 1024 + (blk + 1) * 512],
                            start=(k == 0), stop=(k == 7)) for k in range(8)]
                    tG = S.t("tensor", fG, [tok_lw1, tok_nf_blk[blk],
                                            st.get(kg), st.get("pC") if jp % 2 else None])

                    def fL(eng, blk=blk, jp=jp, pl=pl):
                        return [eng.matmul(
                            pl,
                            WA[:, k * 2048 + 1024 + jp * 128: k * 2048 + 1024 + (jp + 1) * 128],
                            NF[:, k * 1024 + blk * 512: k * 1024 + (blk + 1) * 512],
                            start=(k == 0), stop=(k == 7)) for k in range(8)]
                    tL = S.t("tensor", fL, [st.get(kl),
                                            st.get("pC") if jp % 2 else None])
                    tok_w1_read = tL

                    par = jp % 2
                    g1 = SC1[:, par * 512:(par + 1) * 512]
                    l1 = SC2[:, par * 512:(par + 1) * 512]
                    sg = SC3[:, par * 512:(par + 1) * 512]

                    def fE1(eng, e=e, jp=jp, g1=g1, l1=l1, pg=pg, pl=pl):
                        eng.tensor_scalar(
                            l1, pl,
                            cB1C[:, e * 16 + 8 + jp: e * 16 + 8 + jp + 1],
                            LIMIT, ALU.add, ALU.min)
                        eng.tensor_scalar(g1, pg,
                                          cB1C[:, e * 16 + jp: e * 16 + jp + 1],
                                          LIMIT, ALU.add, ALU.min)
                        return eng.tensor_scalar(l1, l1, -LIMIT, 1.0,
                                                 ALU.max, ALU.add)
                    tE1 = S.t("vector", fE1, [tG, tL, tok_const,
                                              st.get(("m1sc", par))])
                    st[kg] = tE1
                    st[kl] = tE1

                    def fE2(eng, sg=sg, g1=g1):
                        return eng.activation(sg, g1, AF.Sigmoid, scale=ALPHA)
                    tE2 = S.t("scalar", fE2, [tE1])

                    def fE3(eng, g1=g1, l1=l1, sg=sg, jp=jp):
                        eng.tensor_tensor(g1, g1, sg, ALU.mult)
                        eng.tensor_copy(VEC[:, 11:12], VEC[:, 15:16])
                        return eng.tensor_tensor(AFM[:, jp * 512:(jp + 1) * 512],
                                                 g1, l1, ALU.mult)
                    tE3 = S.t("gpsimd", fE3, [tE2, st.get("AFM")])
                    st[("m1sc", par)] = tE3
                    pool_last = tE3
                    tE3_vals.append(tE3[1])

                for tch in range(4):
                    c_local = blk * 4 + tch
                    for hh in range(2):
                        ph = pD[:, hh * 512:(hh + 1) * 512]
                        pk = "pDh%d" % hh

                        def fM2(eng, tch=tch, hh=hh, ph=ph, tv=tuple(tE3_vals)):
                            r = []
                            for k2 in range(8):
                                eng.wait_ge(sems["gpsimd"], tv[k2])
                                r.append(eng.matmul(
                                    ph,
                                    AFM[:, k2 * 512 + tch * 128: k2 * 512 + (tch + 1) * 128],
                                    WA[:, 16384 + k2 * 1024 + hh * 512: 16384 + k2 * 1024 + (hh + 1) * 512],
                                    start=(k2 == 0), stop=(k2 == 7)))
                            return r
                        tM2 = S.t("tensor", fM2, [tok_lw2,
                                                  st.get(pk), st.get("pD")])
                        tok_w2_read = tM2

                        def fEp(eng, e=e, c_local=c_local, hh=hh, ph=ph):
                            a = ACC[:, c_local * 1024 + hh * 512:
                                    c_local * 1024 + (hh + 1) * 512]
                            return eng.scalar_tensor_tensor(
                                a, ph,
                                COMB[:, c_local * 8 + e: c_local * 8 + e + 1],
                                a, ALU.mult, ALU.add)
                        tEp = S.t("vector", fEp, [tM2, comb_tok[c_local]])
                        st[pk] = tEp
                        ep_tok[c_local] = tEp
                st["AFM"] = tok_w2_read
                if mid_cb is not None and e == 0 and blk == 0:
                    mid_cb()
                    mid_cb = None
        st["COMB"] = ep_tok[SBC - 1]
        st["ACC"] = ep_tok[SBC - 1]
        st["SC1"] = pool_last
        st["SC2"] = pool_last
        st["SC3"] = pool_last
        return tok_w1_read, tok_w2_read, ep_tok

    def ln_to_y2b(x_tok, x_ap, y2dst=None, y2key="SB1", extra_deps=(),
                  scratch=None):
        if scratch is None:
            bA, bB, kA, kB = SC1[:], SC2[:], "SC1", "SC2"
        else:
            bA, bB, kA, kB = scratch

        def f1(eng, x_ap=x_ap, bA=bA, bB=bB):
            eng.reduce_sum(vec(vRED), x_ap, AX.X)
            eng.tensor_tensor(bB, x_ap, x_ap, ALU.mult)
            eng.tensor_scalar(vec(vM), vec(vRED), 1.0 / D, None, ALU.mult)
            eng.reduce_sum(vec(vVAR), bB, AX.X)
            eng.tensor_scalar(bA, x_ap, vec(vM), None, ALU.subtract)
            eng.tensor_tensor(vec(vDD), vec(vM), vec(vM), ALU.mult)
            eng.tensor_scalar(vec(vVAR), vec(vVAR), 1.0 / D, EPS,
                              ALU.mult, ALU.add)
            eng.tensor_copy(VEC[:, 12:13], VEC[:, 15:16])
            return eng.tensor_tensor(vec(vVAR), vec(vVAR), vec(vDD),
                                     ALU.subtract)
        t1 = S.t("vector", f1, [x_tok, st.get(kA), st.get(kB)] + list(extra_deps))
        if _CFG['pre_upto'] <= 1:
            return t1, t1, t1

        def f2(eng):
            eng.activation(vec(vRSTD), vec(vVAR), AF.Ln)
            return eng.activation(vec(vRSTD), vec(vRSTD), AF.Exp, scale=-0.5)
        t2 = S.t("scalar", f2, [t1])

        if y2dst is None:
            y2dst = SB1[:]

        def f3(eng, y2dst=y2dst, bA=bA):
            return eng.tensor_scalar(y2dst, bA, vec(vRSTD), None, ALU.mult)
        t3 = S.t("vector", f3, [t2, st.get(y2key)])
        st[kA] = t3
        st[kB] = t1
        return t3

    def proj16(dst_psum, pkey, y2fm_tok, w_sb, w_base, w_tok):
        def f(eng, dst=dst_psum, w_sb=w_sb, w_base=w_base):
            r = []
            for oh in range(2):
                for k in range(8):
                    r.append(eng.matmul(
                        dst[:, oh * 512:(oh + 1) * 512],
                        Y2FM[:, k * 128:(k + 1) * 128],
                        w_sb[:, w_base + k * 1024 + oh * 512: w_base + k * 1024 + (oh + 1) * 512],
                        start=(k == 0), stop=(k == 7)))
            return r
        tok = S.t("tensor", f, [y2fm_tok, w_tok, st.get(pkey),
                                st.get(pkey + "h0"), st.get(pkey + "h1")])
        return tok

    def rope_relu(p_src, dst_bf, dst_key, c_local, rope_tok, proj_tok,
                  scratch=None):
        co = c_local * 64
        if scratch is None:
            bA, bB, kA, kB = SC1[:], SC2[:], "SC1", "SC2"
        else:
            bA, bB, kA, kB = scratch
        P1v = p_src[:].rearrange("p (h a i) -> p h a i", a=2, i=32)[:, :, 0, :]
        P2v = p_src[:].rearrange("p (h a i) -> p h a i", a=2, i=32)[:, :, 1, :]
        CC = ROPE[:, co:co + 32].unsqueeze(1).to_broadcast([128, 16, 32])
        SS = ROPE[:, co + 32:co + 64].unsqueeze(1).to_broadcast([128, 16, 32])

        def hv(buf, half):
            return buf[:, half * 512:(half + 1) * 512].rearrange(
                "p (h i) -> p h i", i=32)

        D1 = dst_bf[:].rearrange("p (h a i) -> p h a i", a=2, i=32)[:, :, 0, :]
        D2 = dst_bf[:].rearrange("p (h a i) -> p h a i", a=2, i=32)[:, :, 1, :]

        def f1(eng):
            eng.tensor_tensor(hv(bA, 0), P1v, CC, ALU.mult)
            eng.tensor_tensor(hv(bA, 1), P2v, SS, ALU.mult)
            eng.tensor_tensor(hv(bB, 0), P1v, SS, ALU.mult)
            eng.tensor_tensor(hv(bB, 1), P2v, CC, ALU.mult)
            eng.tensor_tensor(hv(bA, 0), hv(bA, 0), hv(bA, 1), ALU.subtract)
            eng.tensor_tensor(hv(bB, 0), hv(bB, 0), hv(bB, 1), ALU.add)
            eng.tensor_scalar(D1, hv(bA, 0), 0.0, None, ALU.max)
            return eng.tensor_scalar(D2, hv(bB, 0), 0.0, None, ALU.max)
        t1 = S.t("vector", f1, [proj_tok, rope_tok, st.get(kA),
                                st.get(kB), st.get(dst_key)])
        t2 = t1
        st[kA] = t2
        st[kB] = t2
        return t2, t1

    x0_par_reader = [None, None]
    tok_rope_free = None
    tok_srun = None
    tok_w1_free = tok_wconv
    tok_w2_free = tok_wconv
    own_x1_tok = [None] * N_OWN
    tok_ds_store = [None] * N_OWN
    tok_wk = tok_wv = None
    tok_fin_last_pe = None

    for sb_i in range(_CFG['p1_sbs']):
        def f_rope(eng, sb_i=sb_i):
            return eng.dma_start(
                ROPE[:].rearrange("p (c z) -> p c z", z=64),
                drope[sb_i * 1024:(sb_i + 1) * 1024, :].rearrange(
                    "(c p) z -> p c z", p=128))
        tok_rope = S.dma(f_rope, 1, [tok_rope_free])

        tok_acc = [None] * SBC
        tok_nf_blk = [None, None]
        preA = {}
        NF3 = [(SB2, "bSB2"), (SB2B, "bSB2B"), (SB1B, "bSB1B")]

        def emit_A(c, sb_i=sb_i):
            if c in preA or c >= SBC:
                return
            g = sb_i * SBC + c
            par = c % 2
            tri = c % 3

            def f_x0(eng, g=g, par=par):
                return eng.dma_start(X0P[:, par * 1024:(par + 1) * 1024],
                                     dx0[g * 128:(g + 1) * 128, :])
            tok_x0 = S.dma(f_x0, 1, [x0_par_reader[par]])
            x_ap = X0P[:, par * 1024:(par + 1) * 1024]
            nf_dst, nf_key = NF3[tri]
            ta3 = prepass_A(tok_x0, x_ap, ACC[:, c * 1024:(c + 1) * 1024],
                            SB1[:], "bSB1", nf_dst[:], nf_key)
            x0_par_reader[par] = ta3
            preA[c] = ta3

        def emit_B(c):
            tri = c % 3
            nf_src, nf_key = NF3[tri]
            t_acc, t_nf = prepass_B(c, nf_src[:], preA[c],
                                    ACC[:, c * 1024:(c + 1) * 1024])
            st[nf_key] = t_nf
            tok_acc[c] = t_acc
            if c == 3:
                tok_nf_blk[0] = t_nf
            if c == SBC - 1:
                tok_nf_blk[1] = t_nf

        def emit_pre(c0, c1, sb_i=sb_i):
            for c in range(c0, c1):
                emit_A(c)
                emit_A(c + 1)
                emit_A(c + 2)
                emit_B(c)

        emit_pre(0, 4)
        if _CFG['experts']:
            tok_w1r, tok_w2r, ep_tok = ffn_experts_sb(
                tok_nf_blk, tok_acc, tok_w1_free, tok_w2_free,
                mid_cb=lambda: emit_pre(4, SBC))
        else:
            emit_pre(4, SBC)
            tok_w1r = tok_w2r = None
            ep_tok = list(tok_acc)

        if not _CFG['finish']:
            tok_w1_free = tok_w1r
            tok_w2_free = tok_w2r
            continue

        def f_wk(eng):
            return eng.dma_start(
                WA[:, 0:8192].rearrange("p (k o) -> p k o", o=1024),
                swb[1].rearrange("(k p) o -> p k o", p=128))

        def f_wv(eng):
            return eng.dma_start(
                WA[:, 8192:16384].rearrange("p (k o) -> p k o", o=1024),
                swb[2].rearrange("(k p) o -> p k o", p=128))
        tok_wk = S.dma(f_wk, 1, [tok_w1r, tok_wconv])
        tok_wv = S.dma(f_wv, 1, [tok_w1r])

        AFMf = AFM[:].bitcast(F32)
        st["AFMa"] = st.get("AFM")
        st["AFMb"] = st.get("AFM")
        pend = None

        def emit_ds(pend):
            g_, own_, oc_, t_kr_, t_v65_ = pend

            def f_ds(eng):
                r = []
                for h in range(16):
                    p = h // 2
                    dst = pA if p < 4 else pB
                    r.append(eng.matmul(
                        dst[(h % 2) * 64:(h % 2) * 64 + 64,
                            (p % 4) * 65:(p % 4) * 65 + 65],
                        SB2[:, h * 64:(h + 1) * 64],
                        V65[:, h * 65:(h + 1) * 65],
                        start=True, stop=True))
                return r
            t_ds = S.t("tensor", f_ds, [t_kr_, t_v65_, st.get("pA"),
                                        st.get("pB")])
            st["SB2"] = t_ds
            st["V65"] = t_ds

            def f_su(eng, g=g_):
                eng.scalar_tensor_tensor(SRUN[:, 0:260], pA[:, 0:260],
                                         cSMK[:, g:g + 1], SRUN[:, 0:260],
                                         ALU.mult, ALU.add)
                return eng.scalar_tensor_tensor(SRUN[:, 260:520], pB[:, 0:260],
                                                cSMK[:, g:g + 1],
                                                SRUN[:, 260:520],
                                                ALU.mult, ALU.add)
            t_su = S.t("vector", f_su, [t_ds, tok_const])
            st["pA"] = t_su
            st["pB"] = t_su
            if own_:
                def f_dss(eng, oc=oc_):
                    eng.tensor_copy(DS[:, oc * 520: oc * 520 + 260],
                                    pA[:, 0:260])
                    return eng.tensor_copy(
                        DS[:, oc * 520 + 260: oc * 520 + 520], pB[:, 0:260])
                t_dss = S.t("vector", f_dss, [t_ds])
                tok_ds_store[oc_] = t_dss
                st["pA"] = t_dss
                st["pB"] = t_dss
            return t_ds, t_su

        own = sb_i >= OWN_SB0
        finA = {}

        def emit_Af(c, sb_i=sb_i, own=own):
            g = sb_i * SBC + c
            par = c % 2
            oc = (sb_i - OWN_SB0) * SBC + c if own else None

            def f_x0f(eng, g=g, par=par):
                return eng.dma_start(X0P[:, par * 1024:(par + 1) * 1024],
                                     dx0[g * 128:(g + 1) * 128, :])
            tok_x0f = S.dma(f_x0f, 1, [x0_par_reader[par]])

            if own:
                x1_ap = X1O[:, oc * 1024:(oc + 1) * 1024]
            else:
                x1_ap = X1O[:, par * 1024:(par + 1) * 1024]

            def f_x1(eng, c=c, par=par, x1_ap=x1_ap):
                return eng.scalar_tensor_tensor(
                    x1_ap, ACC[:, c * 1024:(c + 1) * 1024], 0.5,
                    X0P[:, par * 1024:(par + 1) * 1024], ALU.mult, ALU.add)
            t_x1 = S.t("vector", f_x1, [tok_x0f, ep_tok[c]])
            x0_par_reader[par] = t_x1
            if own:
                own_x1_tok[oc] = t_x1
            y2dst = (SB1 if par == 0 else SB1B)[:]
            t_y2b = ln_to_y2b(t_x1, x1_ap, y2dst, "bSB1" if par == 0 else "bSB1B")
            finA[c] = (t_y2b, y2dst, par)

        def emit_Bf(c, sb_i=sb_i, own=own):
            nonlocal pend
            g = sb_i * SBC + c
            oc = (sb_i - OWN_SB0) * SBC + c if own else None
            t_y2b, y2dst, par = finA[c]
            t_y2fm = transpose8(y2dst, Y2FM[:], t_y2b)
            st["bSB1" if par == 0 else "bSB1B"] = t_y2fm

            t_kp = proj16(pC, "pC", t_y2fm, WA, 0, tok_wk)
            if pend is not None:
                emit_ds(pend)
            t_kr, t_kread = rope_relu(
                pC, SB2[:], "SB2", c, tok_rope, t_kp,
                scratch=(AFMf[:, 0:1024], AFMf[:, 1024:2048],
                         "AFMa", "AFMb"))
            st["pC"] = t_kread
            st["pCh0"] = t_kread
            st["pCh1"] = t_kread

            t_vp = proj16(pD, "pD", t_y2fm, WA, 8192, tok_wv)

            def f_v65(eng):
                pv = pD[:].rearrange("p (h d) -> p h d", d=64)
                ov = V65[:].rearrange("p (h z) -> p h z", z=65)[:, :, 0:64]
                return eng.tensor_copy(ov, pv)
            t_v65 = S.t("vector", f_v65, [t_vp, st.get("V65")])
            st["pD"] = t_v65
            st["pDh0"] = t_v65
            st["pDh1"] = t_v65
            pend = (g, own, oc, t_kr, t_v65)

        for c in range(SBC):
            if c == 0:
                emit_Af(0)
            if c + 1 < SBC:
                emit_Af(c + 1)
            emit_Bf(c)

        tok_fin_last_pe, tok_srun = emit_ds(pend)
        st["AFM"] = st["AFMa"]
        tok_rope_free = tok_fin_last_pe
        tok_w1_free = tok_fin_last_pe
        tok_w2_free = tok_w2r

    full_p1 = (_CFG['p1_sbs'] == NSB and _CFG['experts'] and _CFG['finish'])
    do_p2 = _CFG['p2'] and full_p1
    do_p3 = _CFG['p3'] and full_p1

    if do_p2:
        def f_wq(eng):
            return eng.dma_start(
                NF[:, 0:8192].rearrange("p (k o) -> p k o", o=1024),
                swb[0].rearrange("(k p) o -> p k o", p=128))
        tok_wq = S.dma(f_wq, 1, [tok_w1r, tok_wconv])

        def f_wo(eng):
            return eng.dma_start(
                ACC[:, 0:8192].rearrange("p (k o) -> p k o", o=1024),
                swb[3].rearrange("(k p) o -> p k o", p=128))
        tok_wo = S.dma(f_wo, 1, [own_x1_tok[N_OWN - 1]])

    tok_out_x2 = [None] * N_OWN
    tok_sbf_reader = None
    tok_rope = None
    p2_count = min(_CFG['p2_n'], N_OWN) if do_p2 else 0
    p2lnA = {}
    pend_x2 = [None]

    def emit_x2():
        poc, t_wop_p = pend_x2[0]
        pend_x2[0] = None

        def f_x2(eng, poc=poc):
            a = X1O[:, poc * 1024:(poc + 1) * 1024]
            return eng.tensor_tensor(a, a, pD[:], ALU.add)
        t_x2 = S.t("vector", f_x2, [t_wop_p])
        st["pD"] = t_x2
        st["pDh0"] = t_x2
        st["pDh1"] = t_x2
        tok_out_x2[poc] = t_x2
    AFM_SCR = (AFMf[:, 0:1024], AFMf[:, 1024:2048], "AFMa", "AFMb")
    for oc in range(p2_count):
        sb_i = OWN_SB0 + oc // SBC
        c_local = oc % SBC
        if c_local == 0:
            def f_rope2(eng, sb_i=sb_i):
                return eng.dma_start(
                    ROPE[:].rearrange("p (c z) -> p c z", z=64),
                    drope[sb_i * 1024:(sb_i + 1) * 1024, :].rearrange(
                        "(c p) z -> p c z", p=128))
            tok_rope = S.dma(f_rope2, 1, [tok_rope_free])

        x2_ap = X1O[:, oc * 1024:(oc + 1) * 1024]
        par = oc % 2
        lnbuf = (SB1 if par == 0 else SB1B)[:]
        lnkey = "bSB1" if par == 0 else "bSB1B"
        if oc in p2lnA:
            t_y2b = p2lnA.pop(oc)
        else:
            t_y2b = ln_to_y2b(own_x1_tok[oc], x2_ap, lnbuf, lnkey,
                              scratch=AFM_SCR)
        t_y2fm = transpose8(lnbuf, Y2FM[:], t_y2b)
        st[lnkey] = t_y2fm

        if _CFG['p2_upto'] <= 1:
            tok_out_x2[oc] = t_y2fm
            continue
        def f_qp(eng):
            r = []
            for oh in range(2):
                for k in range(8):
                    r.append(eng.matmul(
                        pC[:, oh * 512:(oh + 1) * 512],
                        Y2FM[:, k * 128:(k + 1) * 128],
                        NF[:, k * 1024 + oh * 512: k * 1024 + (oh + 1) * 512],
                        start=(k == 0), stop=(k == 7)))
            return r
        t_qp = S.t("tensor", f_qp, [t_y2fm, tok_wq, st.get("pC"),
                                    st.get("pCh0"), st.get("pCh1")])
        t_kp = proj16(pD, "pD", t_y2fm, WA, 0, tok_wk)
        t_qr, t_qread = rope_relu(pC, SB2[:], "SB2", c_local, tok_rope, t_qp)
        st["pC"] = t_qread
        st["pCh0"] = t_qread
        st["pCh1"] = t_qread
        t_qfm = transpose8(SB2[:], QFM[:], t_qr)
        st["SB2"] = t_qfm

        if _CFG['p2_upto'] <= 2:
            tok_out_x2[oc] = t_qfm
            continue
        t_kr, t_kread = rope_relu(pD, SB2[:], "SB2", c_local, tok_rope, t_kp)
        st["pD"] = t_kread
        st["pDh0"] = t_kread
        st["pDh1"] = t_kread
        tok_rope_free = t_kr
        if oc + 1 < p2_count:
            npar = (oc + 1) % 2
            p2lnA[oc + 1] = ln_to_y2b(
                own_x1_tok[oc + 1], X1O[:, (oc + 1) * 1024:(oc + 2) * 1024],
                (SB1 if npar == 0 else SB1B)[:],
                "bSB1" if npar == 0 else "bSB1B", scratch=AFM_SCR)

        t_vp = proj16(pC, "pC", t_y2fm, WA, 8192, tok_wv)
        t_kfm = transpose8(SB2[:], KFM[:], t_kr, extra=[st.get("KFM")])
        st["SB2"] = t_kfm

        def f_v65(eng):
            pv = pC[:].rearrange("p (h d) -> p h d", d=64)
            ov = V65[:].rearrange("p (h z) -> p h z", z=65)[:, :, 0:64]
            return eng.tensor_copy(ov, pv)
        t_v65 = S.t("vector", f_v65, [t_vp, st.get("V65")])
        st["pC"] = t_v65
        st["pCh0"] = t_v65
        st["pCh1"] = t_v65

        def f_sbf(eng):
            return eng.tensor_copy(SBF[:], SRUN[:])
        t_sbfr = S.t("vector", f_sbf, [tok_srun, tok_sbf_reader])

        if _CFG['p2_upto'] <= 4:
            tok_out_x2[oc] = t_sbfr
            continue
        SC_AP = [pA[:, 0:128], pC[:, 0:128], pA[:, 128:256], pC[:, 128:256]]
        NM_AP = [pB[:, 0:65], pC[:, 512:577], pB[:, 65:130], pC[:, 577:642]]
        X0Pb = X0P[:].bitcast(BF16)
        AB_AP = [ABUF[:, 0:128], ABUF[:, 128:256],
                 X0Pb[:, 0:128], X0Pb[:, 128:256]]
        t_mk_h = {}
        t_nm_h = {}
        t_ep_h = {}
        n_heads = min(16, _CFG['p2_nh'])

        def emit_sc_mk(h):
            hp4 = h % 4
            f_tile = h // 2
            b0 = (h % 2) * 64
            sc_ap = SC_AP[hp4]

            def f_sc(eng, f_tile=f_tile, b0=b0, sc_ap=sc_ap):
                return eng.matmul(
                    sc_ap,
                    KFM[b0:b0 + 64, f_tile * 128:(f_tile + 1) * 128],
                    QFM[b0:b0 + 64, f_tile * 128:(f_tile + 1) * 128],
                    start=True, stop=True)
            t_sc = S.t("tensor", f_sc, [t_kfm, t_qfm, t_v65,
                                        t_mk_h.get(h - 2)])

            def f_mk(eng, hp4=hp4, sc_ap=sc_ap):
                return eng.tensor_tensor(AB_AP[hp4], sc_ap, cTRI[:],
                                         ALU.mult)
            xg = [x0_par_reader[0], x0_par_reader[1]] \
                if (oc == 0 and hp4 >= 2) else []
            t_mk_h[h] = S.t("vector", f_mk, [t_sc, tok_const] + xg)

        def emit_nm_ep(h):
            hp4 = h % 4
            f_tile = h // 2
            b0 = (h % 2) * 64
            p = h // 2
            nm_ap = NM_AP[hp4]

            def f_nm(eng, h=h, hp4=hp4, f_tile=f_tile, b0=b0, p=p,
                     nm_ap=nm_ap):
                r = [eng.matmul(nm_ap,
                                AB_AP[hp4],
                                V65[:, h * 65:(h + 1) * 65],
                                start=True, stop=False)]
                base = 0 if p < 4 else 260
                sb_t = SBF[b0:b0 + 64,
                           base + (p % 4) * 65: base + (p % 4) * 65 + 65]
                r.append(eng.matmul(nm_ap,
                                    QFM[b0:b0 + 64,
                                        f_tile * 128:(f_tile + 1) * 128],
                                    sb_t, start=False, stop=True))
                return r
            t_nm = S.t("tensor", f_nm, [t_mk_h[h], t_v65, t_sbfr,
                                        t_ep_h.get(h - 2)])
            t_nm_h[h] = t_nm

            def f_ep(eng, h=h, nm_ap=nm_ap):
                eng.tensor_scalar(vec(vRD), nm_ap[:, 64:65],
                                  EPS_ATT, None, ALU.add)
                eng.tensor_copy(VEC[:, 12:13], VEC[:, 15:16])
                eng.reciprocal(vec(vRR), vec(vRD))
                eng.tensor_copy(VEC[:, 12:13], VEC[:, 15:16])
                return eng.tensor_scalar(SC2[:, h * 64:(h + 1) * 64],
                                         nm_ap[:, 0:64], vec(vRR),
                                         None, ALU.mult)
            dep_sc2 = [st.get("SC2")] if h == 0 else []
            t_ep_h[h] = S.t("vector", f_ep, [t_nm] + dep_sc2)

        for h in range(n_heads):
            emit_sc_mk(h)
            if h >= 3:
                emit_nm_ep(h - 3)
        for h in range(max(0, n_heads - 3), n_heads):
            emit_nm_ep(h)
        last_nm = t_nm_h[n_heads - 1]
        attn_tok = t_ep_h[n_heads - 1]
        st["pA"] = attn_tok
        st["pB"] = attn_tok
        st["pC"] = attn_tok
        st["pCh0"] = attn_tok
        st["pCh1"] = attn_tok
        tok_sbf_reader = last_nm
        st["V65"] = last_nm

        if _CFG['p2_upto'] <= 5:
            tok_out_x2[oc] = attn_tok
            continue
        def f_sup(eng, oc=oc):
            eng.tensor_tensor(SRUN[:, 0:260], SRUN[:, 0:260],
                              DS[:, oc * 520: oc * 520 + 260], ALU.add)
            return eng.tensor_tensor(SRUN[:, 260:520], SRUN[:, 260:520],
                                     DS[:, oc * 520 + 260: oc * 520 + 520],
                                     ALU.add)
        tok_srun = S.t("vector", f_sup, [tok_ds_store[oc], tok_srun])

        def f_atb(eng):
            return eng.tensor_copy(SB2[:], SC2[:])
        t_atb = S.t("vector", f_atb, [attn_tok, st.get("SB2")])
        st["SC2"] = t_atb
        t_afm = transpose8(SB2[:], KFM[:], t_atb, extra=[last_nm])
        st["SB2"] = t_afm
        st["KFM"] = None

        if _CFG['p2_upto'] <= 6:
            tok_out_x2[oc] = t_afm
            continue

        def f_wop(eng):
            r = []
            for oh in range(2):
                for k in range(8):
                    r.append(eng.matmul(
                        pD[:, oh * 512:(oh + 1) * 512],
                        KFM[:, k * 128:(k + 1) * 128],
                        ACC[:, k * 1024 + oh * 512: k * 1024 + (oh + 1) * 512],
                        start=(k == 0), stop=(k == 7)))
            return r
        t_wop = S.t("tensor", f_wop, [t_afm, tok_wo, st.get("pD"),
                                      st.get("pDh0"), st.get("pDh1")])
        st["KFM"] = t_wop

        def f_x2(eng, oc=oc):
            a = X1O[:, oc * 1024:(oc + 1) * 1024]
            return eng.tensor_tensor(a, a, pD[:], ALU.add)
        t_x2 = S.t("vector", f_x2, [t_wop])
        st["pD"] = t_x2
        st["pDh0"] = t_x2
        st["pDh1"] = t_x2
        tok_out_x2[oc] = t_x2

    tok_out_dma = []
    if not do_p3:
        for oc in range(N_OWN):
            def f_outd(eng, oc=oc):
                return eng.dma_start(dout[oc * 128:(oc + 1) * 128, :],
                                     X1O[:, oc * 1024:(oc + 1) * 1024])
            dep = [tok_out_x2[oc] if (do_p2 and tok_out_x2[oc]) else None,
                   ("vector", S.cnt["vector"]), ("tensor", S.cnt["tensor"]),
                   ("scalar", S.cnt["scalar"]), ("gpsimd", S.cnt["gpsimd"])]
            tok_out_dma.append(S.dma(f_outd, 1, dep))
    else:
        tok_p2_done = tok_out_x2[N_OWN - 1]
        tok_w1_free = tok_p2_done
        tok_w2_free_p3 = tok_w2r
        out_par_reader = [None, None]
        tok_out_dma = []
        tok_w1r_p3 = None
        st["ACC"] = tok_p2_done
        st["SB1"] = tok_p2_done
        st["SB2"] = tok_p2_done
        st["SC2"] = tok_p2_done
        st["AFM"] = tok_p2_done
        st["bSB1"] = tok_p2_done
        st["bSB1B"] = tok_p2_done

        for sb2 in range(2):
            tok_acc = [None] * SBC
            tok_nf_blk = [None, None]
            preA3 = {}
            NF3 = [(SB2, "bSB2"), (SB2B, "bSB2B"), (SB1B, "bSB1B")]

            def emit_A3(c, sb2=sb2):
                if c in preA3 or c >= SBC:
                    return
                oc = sb2 * SBC + c
                tri = c % 3
                x_ap = X1O[:, oc * 1024:(oc + 1) * 1024]
                extra = []
                if sb2 == 0 and c == 0:
                    extra = [tok_p2_done]
                if sb2 == 1 and c == 0:
                    extra = [out_par_reader[0], out_par_reader[1]]
                nf_dst, nf_key = NF3[tri]
                ta3 = prepass_A(tok_out_x2[oc], x_ap,
                                ACC[:, c * 1024:(c + 1) * 1024],
                                SB1[:], "bSB1", nf_dst[:], nf_key,
                                extra_deps=extra)
                preA3[c] = ta3

            def emit_B3(c):
                tri = c % 3
                nf_src, nf_key = NF3[tri]
                t_acc, t_nf = prepass_B(c, nf_src[:], preA3[c],
                                        ACC[:, c * 1024:(c + 1) * 1024])
                st[nf_key] = t_nf
                tok_acc[c] = t_acc
                if c == 3:
                    tok_nf_blk[0] = t_nf
                if c == SBC - 1:
                    tok_nf_blk[1] = t_nf

            def emit_pre3(c0, c1):
                for c in range(c0, c1):
                    emit_A3(c)
                    emit_A3(c + 1)
                    emit_A3(c + 2)
                    emit_B3(c)

            emit_pre3(0, 4)
            tok_w1r_p3, tok_w2r_p3, ep_tok = ffn_experts_sb(
                tok_nf_blk, tok_acc,
                tok_w1_free if sb2 == 0 else tok_w1r_p3,
                tok_w2_free_p3,
                mid_cb=lambda: emit_pre3(4, SBC))
            tok_w2_free_p3 = tok_w2r_p3

            for c in range(SBC):
                oc = sb2 * SBC + c
                par = c % 2
                x3buf = SC2 if par == 0 else SC3

                def f_x3(eng, oc=oc, c=c, x3buf=x3buf):
                    return eng.scalar_tensor_tensor(
                        x3buf[:], ACC[:, c * 1024:(c + 1) * 1024], 0.5,
                        X1O[:, oc * 1024:(oc + 1) * 1024], ALU.mult, ALU.add)
                t_x3 = S.t("vector", f_x3, [ep_tok[c], out_par_reader[par],
                                            st.get("SC2" if par == 0 else "SC3")])
                if par == 0:
                    st["SC2"] = t_x3
                else:
                    st["SC3"] = t_x3

                def f_out(eng, oc=oc, x3buf=x3buf):
                    return eng.dma_start(dout[oc * 128:(oc + 1) * 128, :], x3buf[:])
                tok_o = S.dma(f_out, 1, [t_x3])
                out_par_reader[par] = tok_o
                tok_out_dma.append(tok_o)

    if _CFG['dbg'] is not None:
        barrier = [("tensor", S.cnt["tensor"]), ("vector", S.cnt["vector"]),
                   ("scalar", S.cnt["scalar"]), ("gpsimd", S.cnt["gpsimd"]),
                   ("sync", S.cnt["sync"])]
        dbg_src = {"acc": ACC, "nf": NF, "xc": SB1, "sb2": SB2,
                   "comb": COMB, "ctb": CTB, "small": SMALL,
                   "x0p": X0P, "sc3": SC3, "wa": WA, "gwt": cGWT,
                   "b1c": cB1C, "b2r": cB2R}[_CFG['dbg']]
        ncols = dbg_src.shape[1] if hasattr(dbg_src, 'shape') else 8192
        ncols = {"acc": 8192, "nf": 8192, "xc": 1024, "sb2": 1024,
                 "comb": 64, "ctb": 128, "small": 64, "x0p": 2048,
                 "sc3": 1024, "wa": 16384, "gwt": 64, "b1c": 128,
                 "b2r": 1024}[_CFG['dbg']]
        prevtok = None
        for c in range((ncols + 1023) // 1024):
            w = min(1024, ncols - c * 1024)

            def f_cp(eng, c=c, w=w, dbg_src=dbg_src):
                return eng.tensor_copy(SC1[:, 0:w], dbg_src[:, c * 1024:c * 1024 + w])
            tcp = S.t("vector", f_cp, barrier + [prevtok])

            def f_dmp(eng, c=c, w=w):
                return eng.dma_start(dout[c * 128:(c + 1) * 128, 0:w], SC1[:, 0:w])
            prevtok = S.dma(f_dmp, 1, [tcp])

    def f_fin(eng):
        return eng.engine_nop()
    S.t("vector", f_fin, [("sync", S.cnt["sync"])])

    blk = ctx.enter_context(nc.Block())
    S.replay(blk, sems)
    ctx.close()
    return nc


_NC_CACHE = {}


def _get_nc():
    if "nc" not in _NC_CACHE:
        _NC_CACHE["nc"] = build_program()
    return _NC_CACHE["nc"]


def _host_prep(inputs):
    x = np.asarray(inputs["L0_attn_x"])
    wq = np.asarray(inputs["L0_attn_wq"])
    wk = np.asarray(inputs["L0_attn_wk"])
    wv = np.asarray(inputs["L0_attn_wv"])
    wo = np.asarray(inputs["L0_attn_wo"])
    nw = np.asarray(inputs["L0_ffn_norm_weight"])
    gw = np.asarray(inputs["L0_ffn_gate_weight"])
    gb = np.asarray(inputs["L0_ffn_gate_bias"])
    w1 = np.asarray(inputs["L0_ffn_mlp1_weight"])
    b1 = np.asarray(inputs["L0_ffn_mlp1_bias"])
    w2 = np.asarray(inputs["L0_ffn_mlp2_weight"])
    b2 = np.asarray(inputs["L0_ffn_mlp2_bias"])

    perm = np.concatenate([
        np.concatenate([h * 64 + 2 * np.arange(32),
                        h * 64 + 2 * np.arange(32) + 1])
        for h in range(NH)])
    wqp = np.ascontiguousarray(wq[:, perm], dtype=np.float32)
    wkp = np.ascontiguousarray(wk[:, perm], dtype=np.float32)

    w1t = np.empty((NEXP, D, DFF), dtype=BFD)
    b1c_all = np.empty((CH, 128), dtype=np.float32)
    for e in range(NEXP):
        T = w1[e].T
        w1t[e] = np.concatenate([T[:, 0::2], T[:, 1::2]], axis=1)
        bp = np.concatenate([b1[e][0::2], b1[e][1::2]]).astype(np.float32)
        b1c_all[:, e * 16:(e + 1) * 16] = bp.reshape(16, CH).T
    w2t = np.ascontiguousarray(np.transpose(w2, (0, 2, 1)))
    gwt = np.ascontiguousarray(gw.T)
    gbc = np.ascontiguousarray(np.broadcast_to(gb.astype(np.float32), (CH, NEXP)))
    nwc = np.ascontiguousarray(np.broadcast_to(nw.astype(np.float32), (CH, D)))
    b2r = np.ascontiguousarray(b2)

    tri = (np.arange(CH)[:, None] <= np.arange(CH)[None, :]).astype(np.float32)
    iota8 = np.ascontiguousarray(
        np.broadcast_to(np.arange(NEXP, dtype=np.float32), (CH, NEXP)))
    io9 = iota8 - 9.0
    inv_freq = (1.0 / THETA ** (np.arange(0, HD, 2).astype(np.float32) / HD)
                ).astype(np.float32)

    in_maps = []
    for core in range(8):
        b = core // 4
        q = core % 4
        permq = list(range(q)) + list(range(q + 1, 4)) + [q]
        x0r = np.ascontiguousarray(
            np.concatenate([x[b, 2048 * j:2048 * (j + 1)] for j in permq],
                           axis=0), dtype=np.float32)
        smask = np.zeros((CH, NCH), dtype=np.float32)
        for i in range(NCH):
            if i // 16 < q:
                smask[:, i] = 1.0
        ropet = np.empty((NCH * CH, 64), dtype=np.float32)
        for i in range(NCH):
            quarter = permq[i // 16]
            base = 2048 * quarter + 128 * (i % 16)
            t = (base + np.arange(CH, dtype=np.float32))[:, None]
            freqs = (t * inv_freq[None, :]).astype(np.float32)
            ropet[i * CH:(i + 1) * CH, 0:32] = np.cos(
                freqs.astype(np.float64)).astype(np.float32)
            ropet[i * CH:(i + 1) * CH, 32:64] = np.sin(
                freqs.astype(np.float64)).astype(np.float32)
        in_maps.append({
            "x0r": x0r, "w1t": w1t, "w2t": w2t, "b1c": b1c_all, "b2r": b2r,
            "gwt": gwt, "gbc": gbc, "nwc": nwc,
            "wqf": wqp, "wkf": wkp,
            "wvf": np.ascontiguousarray(wv, dtype=np.float32),
            "wof": np.ascontiguousarray(wo, dtype=np.float32),
            "rope": ropet, "smask": smask, "tri": tri,
            "iota8": iota8, "io9": io9,
        })
    return in_maps


def kernel(**inputs):
    in_maps = _host_prep(inputs)
    nc = _get_nc()
    res = run_bass_kernel_spmd(nc, in_maps, core_ids=list(range(8)))
    out = np.empty((B, L, D), dtype=np.float32)
    for core in range(8):
        b = core // 4
        q = core % 4
        out[b, 2048 * q:2048 * (q + 1), :] = res.results[core]["xout"]
    return out


# version 21
# speedup vs baseline: 1.1178x; 1.0559x over previous
import numpy as np
import ml_dtypes
from contextlib import ExitStack
import concourse.bass as bass
import concourse.mybir as mybir
from concourse.bass_utils import run_bass_kernel_spmd

F32 = mybir.dt.float32
BF16 = mybir.dt.bfloat16
AF = mybir.ActivationFunctionType
ALU = mybir.AluOpType
AX = mybir.AxisListType
BFD = ml_dtypes.bfloat16

D = 1024
NH, HD, NPR = 16, 64, 32
L, B = 8192, 2
CH = 128
NCH = 64
SBC = 8
NSB = NCH // SBC
NEXP = 8
DFF = 2048
OWN_SB0 = 6
N_OWN = 16
EPS = 1e-5
EPS_ATT = 1e-6
THETA = 10000.0
LIMIT = 7.0
ALPHA = 1.702

_CFG = dict(p1_sbs=NSB, experts=True, finish=True, p2=True, p3=True, pre_upto=99, dbg=None, p2_n=N_OWN, p2_upto=99, p2_nh=16, p2_hupto=99)


class Sched:
    ENGS = ("sync", "tensor", "vector", "scalar", "gpsimd")

    def __init__(self):
        self.tasks = {e: [] for e in self.ENGS}
        self.cnt = {e: 0 for e in self.ENGS}

    def _mk(self, eng, fn, deps, inc):
        w = {}
        for d in deps:
            if d is None:
                continue
            de, dv = d
            if de == eng and eng != "sync":
                continue
            if dv <= 0:
                continue
            w[de] = max(w.get(de, 0), dv)
        self.tasks[eng].append((sorted(w.items()), fn))
        self.cnt[eng] += inc
        return (eng, self.cnt[eng])

    def t(self, eng, fn, deps=()):
        return self._mk(eng, fn, deps, 1)

    def dma(self, fn, n, deps=()):
        return self._mk("sync", fn, deps, 16 * n)

    def replay(self, blk, sems):
        def runner(eng_name):
            tasks = self.tasks[eng_name]

            def run(eng):
                for waits, fn in tasks:
                    for (se, v) in waits:
                        eng.wait_ge(sems[se], v)
                    r = fn(eng)
                    rl = r if isinstance(r, (list, tuple)) else [r]
                    if eng_name == "sync":
                        for inst in rl:
                            inst.then_inc(sems["sync"], 16)
                    else:
                        rl[-1].then_inc(sems[eng_name], 1)
            return run
        blk.sync(runner("sync"))
        blk.tensor(runner("tensor"))
        blk.vector(runner("vector"))
        blk.scalar(runner("scalar"))
        blk.gpsimd(runner("gpsimd"))


def build_program():
    nc = bass.Bass("TRN2", target_bir_lowering=False,
                   dynamic_dma_scratch_size=2048)

    dx0 = nc.dram_tensor("x0r", [NCH * CH, D], F32, kind="ExternalInput")
    dw1t = nc.dram_tensor("w1t", [NEXP, D, DFF], BF16, kind="ExternalInput")
    dw2t = nc.dram_tensor("w2t", [NEXP, D, D], BF16, kind="ExternalInput")
    db1c = nc.dram_tensor("b1c", [CH, 128], F32, kind="ExternalInput")
    db2r = nc.dram_tensor("b2r", [NEXP, D], BF16, kind="ExternalInput")
    dgwt = nc.dram_tensor("gwt", [D, NEXP], BF16, kind="ExternalInput")
    dgbc = nc.dram_tensor("gbc", [CH, NEXP], F32, kind="ExternalInput")
    dnwc = nc.dram_tensor("nwc", [CH, D], F32, kind="ExternalInput")
    dwq = nc.dram_tensor("wqf", [D, D], F32, kind="ExternalInput")
    dwk = nc.dram_tensor("wkf", [D, D], F32, kind="ExternalInput")
    dwv = nc.dram_tensor("wvf", [D, D], F32, kind="ExternalInput")
    dwo = nc.dram_tensor("wof", [D, D], F32, kind="ExternalInput")
    drope = nc.dram_tensor("rope", [NCH * CH, 64], F32, kind="ExternalInput")
    dsmk = nc.dram_tensor("smask", [CH, NCH], F32, kind="ExternalInput")
    dtri = nc.dram_tensor("tri", [CH, CH], F32, kind="ExternalInput")
    diota = nc.dram_tensor("iota8", [CH, NEXP], F32, kind="ExternalInput")
    dio9 = nc.dram_tensor("io9", [CH, NEXP], F32, kind="ExternalInput")
    dout = nc.dram_tensor("xout", [N_OWN * CH, D], F32, kind="ExternalOutput")
    swb = nc.dram_tensor("wbf_scratch", [4, D, D], BF16, kind="Internal")

    ctx = ExitStack()

    def sbuf(name, shape, dtype):
        return ctx.enter_context(nc.sbuf_tensor(name, shape, dtype))

    NF = sbuf("NF", [128, 8192], BF16)
    ACC = sbuf("ACCb", [128, 8192], BF16)
    WA = sbuf("WA", [128, 24576], BF16)
    X1O = sbuf("X1O", [128, 16384], F32)
    DS = sbuf("DSo", [128, N_OWN * 520], BF16)
    SRUN = sbuf("SRUN", [128, 520], F32)
    SBF = sbuf("SBF", [128, 520], BF16)
    ROPE = sbuf("ROPEt", [128, SBC * 64], F32)
    AFM = sbuf("ACTFM", [128, 4096], BF16)
    X0P = sbuf("X0P", [128, 2048], F32)
    SC1 = sbuf("SC1", [128, 1024], F32)
    SC2 = sbuf("SC2", [128, 1024], F32)
    SC3 = sbuf("SC3", [128, 1024], F32)
    SB1 = sbuf("SB1", [128, 1024], BF16)
    SB2 = sbuf("SB2", [128, 1024], BF16)
    SB1B = sbuf("SB1B", [128, 1024], BF16)
    SB2B = sbuf("SB2B", [128, 1024], BF16)
    Y2FM = sbuf("Y2FM", [128, 1024], BF16)
    QFM = sbuf("QFM", [128, 1024], BF16)
    KFM = sbuf("KFM", [128, 1024], BF16)
    V65 = sbuf("V65", [128, 1040], BF16)
    ABUF = sbuf("ABUF", [128, 256], BF16)
    CTB = sbuf("CTB", [128, 128], BF16)
    COMB = sbuf("COMB", [128, 64], F32)
    SMALL = sbuf("SMALL", [128, 64], F32)
    VEC = sbuf("VEC", [128, 24], F32)
    AFMf = AFM[:].bitcast(F32)
    cNWC = sbuf("cNWC", [128, 1024], F32)
    cGWT = sbuf("cGWT", [128, 64], BF16)
    cGBC = sbuf("cGBC", [128, 8], F32)
    cB1C = sbuf("cB1C", [128, 128], F32)
    cB2R = sbuf("cB2R", [128, 1024], BF16)
    cIOTA = sbuf("cIOTA", [128, 8], F32)
    cIO9 = sbuf("cIO9", [128, 8], F32)
    cTRI = sbuf("cTRI", [128, 128], F32)
    cIDB = sbuf("cIDB", [128, 128], BF16)
    cIDF = sbuf("cIDF", [128, 128], F32)
    cSMK = sbuf("cSMK", [128, 64], F32)

    pA = ctx.enter_context(nc.psum_tensor("pA", [128, 512], F32))
    pB = ctx.enter_context(nc.psum_tensor("pB", [128, 512], F32))
    pC = ctx.enter_context(nc.psum_tensor("pC", [128, 1024], F32))
    pD = ctx.enter_context(nc.psum_tensor("pD", [128, 1024], F32))
    pT = ctx.enter_context(nc.psum_tensor("pT", [128, 512], F32))
    pM = ctx.enter_context(nc.psum_tensor("pM", [128, 456], F32))

    sems = {e: ctx.enter_context(nc.semaphore("sem_" + e)) for e in Sched.ENGS}
    S = Sched()

    vRED, vM, vVAR, vRSTD, vR2, vDD, vEE, vW1, vW2, vRD, vRR = range(11)

    def vec(i):
        return VEC[:, i:i + 1]

    st = {}

    def f_ld_const(eng):
        return [eng.dma_start(cNWC[:], dnwc[:]),
                eng.dma_start(cGWT[:].rearrange("p (k e) -> p k e", e=NEXP),
                              dgwt.rearrange("(k p) e -> p k e", p=128)),
                eng.dma_start(cGBC[:], dgbc[:]),
                eng.dma_start(cB1C[:], db1c[:]),
                eng.dma_start(cB2R[0:8, :], db2r[:]),
                eng.dma_start(cIOTA[:], diota[:]),
                eng.dma_start(cIO9[:], dio9[:]),
                eng.dma_start(cTRI[:], dtri[:]),
                eng.dma_start(cSMK[:], dsmk[:])]
    tok_const = S.dma(f_ld_const, 9)

    def f_ident_a(eng):
        eng.memset(cIDB[:], 1.0)
        eng.memset(cIDF[:], 1.0)
        eng.memset(V65[:], 1.0)
        eng.memset(VEC[:, 15:16], 1.0)
        eng.memset(SRUN[:], 0.0)
        eng.memset(VEC[:, 11:12], 1.0)
        return eng.affine_select(cIDF[:], cIDF[:], [[-1, 128]], ALU.is_equal,
                                 0.0, base=0, channel_multiplier=1)
    tok_ident_a = S.t("gpsimd", f_ident_a)

    def f_ident_b(eng):
        return eng.affine_select(cIDB[:], cIDB[:], [[-1, 128]], ALU.is_equal,
                                 0.0, base=0, channel_multiplier=1)
    tok_ident = S.t("gpsimd", f_ident_b, [tok_ident_a])

    prev = tok_const
    for wi, dw in enumerate((dwq, dwk, dwv, dwo)):
        def f_ldw(eng, dw=dw):
            return eng.dma_start(
                X1O[:, 0:8192].rearrange("p (k o) -> p k o", o=1024),
                dw.rearrange("(k p) o -> p k o", p=128))
        tok_l = S.dma(f_ldw, 1, [prev])

        def f_cv(eng):
            return eng.tensor_copy(NF[:, 0:8192], X1O[:, 0:8192])
        tok_c = S.t("vector", f_cv, [tok_l])

        def f_stw(eng, wi=wi):
            return eng.dma_start(
                swb[wi].rearrange("(k p) o -> p k o", p=128),
                NF[:, 0:8192].rearrange("p (k o) -> p k o", o=1024))
        prev = S.dma(f_stw, 1, [tok_c])
    tok_wconv = prev

    def transpose8(src_bf, dst_fm, dep, extra=()):
        atok = None
        for g in (0, 4):
            def fpe(eng, g=g, src=src_bf):
                return [eng.matmul(pT[:, j * 128:(j + 1) * 128],
                                   src[:, (g + j) * 128:(g + j + 1) * 128],
                                   cIDB[:], start=True, stop=True)
                        for j in range(4)]
            d = [dep, tok_ident, st.get("pT")] + list(extra)
            ptok = S.t("tensor", fpe, d)

            def fac(eng, g=g, dst=dst_fm):
                return eng.copy(dst[:, g * 128:(g + 4) * 128], pT[:])
            atok = S.t("scalar", fac, [ptok])
            st["pT"] = atok
        return atok

    def transpose8_strided(src_bf, c_local, dep):
        atok = None
        for g in (0, 4):
            def fpe(eng, g=g, src=src_bf):
                return [eng.matmul(pT[:, j * 128:(j + 1) * 128],
                                   src[:, (g + j) * 128:(g + j + 1) * 128],
                                   cIDB[:], start=True, stop=True)
                        for j in range(4)]
            ptok = S.t("tensor", fpe, [dep, tok_ident, st.get("pT")])

            def fac(eng, g=g, c_local=c_local):
                ov = NF[:].rearrange("p (k t) -> p k t", t=1024)[
                    :, g:g + 4, c_local * 128:(c_local + 1) * 128]
                iv = pT[:].rearrange("p (k t) -> p k t", t=128)
                return eng.copy(ov, iv)
            atok = S.t("scalar", fac, [ptok])
            st["pT"] = atok
        return atok

    def prepass_A(x_tok, x_ap, xc_ap, xm_buf, xm_key, nf_dst, nf_key,
                  extra_deps=()):
        LN2 = VEC[:, 16:18]

        def fa1(eng, x_ap=x_ap, xm_buf=xm_buf):
            bns = xm_buf.bitcast(F32)
            eng.bn_stats(bns[:, 0:6], x_ap[:, 0:512])
            eng.bn_stats(bns[:, 6:12], x_ap[:, 512:1024])
            eng.tensor_copy(VEC[:, 12:13], VEC[:, 15:16])
            eng.tensor_copy(VEC[:, 13:14], VEC[:, 15:16])
            eng.bn_aggr(VEC[:, 18:20], bns[:, 0:12])
            eng.tensor_copy(VEC[:, 12:13], VEC[:, 15:16])
            eng.tensor_copy(VEC[:, 13:14], VEC[:, 15:16])
            eng.tensor_scalar(vec(vM), VEC[:, 18:19], 1.0, None, ALU.mult)
            eng.tensor_scalar(LN2[:, 0:1], VEC[:, 19:20], EPS, None, ALU.add)
            eng.tensor_copy(VEC[:, 12:13], VEC[:, 15:16])
            eng.reciprocal(VEC[:, 20:21], LN2[:, 0:1])
            eng.tensor_copy(VEC[:, 12:13], VEC[:, 15:16])
            return eng.tensor_scalar(LN2[:, 1:2], VEC[:, 19:20], VEC[:, 20:21],
                                     EPS, ALU.mult, ALU.add)
        ta1 = S.t("vector", fa1, [x_tok, st.get(xm_key)] + list(extra_deps))

        def fa2(eng):
            eng.activation(LN2[:], LN2[:], AF.Ln)
            return eng.activation(LN2[:], LN2[:], AF.Exp, scale=-0.5)
        ta2 = S.t("scalar", fa2, [ta1])

        def fa3(eng, xc_ap=xc_ap, nf_dst=nf_dst, x_ap=x_ap):
            eng.tensor_scalar(xc_ap, x_ap, vec(vM), LN2[:, 0:1],
                              ALU.subtract, ALU.mult)
            eng.tensor_copy(VEC[:, 12:13], VEC[:, 15:16])
            return eng.scalar_tensor_tensor(nf_dst, xc_ap, LN2[:, 1:2],
                                            cNWC[:], ALU.mult, ALU.mult)
        ta3 = S.t("vector", fa3, [ta2, st.get(nf_key), tok_const])
        st[xm_key] = ta3
        st[nf_key] = ta3
        return ta3

    def prepass_B(c_local, nf_src, nf_tok, xc_ap):
        tok_nf = transpose8_strided(nf_src, c_local, nf_tok)

        def f7(eng, c_local=c_local):
            return [eng.matmul(
                pM[:, 0:8],
                NF[:, k * 1024 + c_local * 128: k * 1024 + (c_local + 1) * 128],
                cGWT[:, k * 8:(k + 1) * 8],
                start=(k == 0), stop=(k == 7)) for k in range(8)]
        t7 = S.t("tensor", f7, [tok_nf, tok_const, st.get("pMg"),
                                st.get("pMb")])

        def f8(eng):
            GAP = lambda: eng.tensor_copy(VEC[:, 12:13], VEC[:, 15:16])
            lgb = SMALL[:, 8:12].bitcast(BF16)[:, 0:8]
            eng.tensor_tensor(lgb, pM[:, 0:8], cGBC[:], ALU.add)
            GAP()
            lgf = SMALL[:, 16:24]
            eng.tensor_copy(lgf, lgb)
            GAP()
            m8 = SMALL[:, 0:8]
            eng.max(out=m8, in_=lgf)
            GAP()
            e1 = SMALL[:, 24:32]
            eng.tensor_scalar(e1, lgf, m8[:, 0:1], None, ALU.is_equal)
            GAP()
            tq = SMALL[:, 32:40]
            eng.tensor_tensor(tq, e1, cIO9[:], ALU.mult)
            GAP()
            eng.tensor_scalar(tq, tq, 9.0, None, ALU.add)
            GAP()
            eng.tensor_reduce(vec(vDD), tq, AX.X, ALU.min)
            eng.tensor_tensor(vec(vEE), m8[:, 1:2], m8[:, 0:1], ALU.subtract)
            m1 = SMALL[:, 40:48]
            eng.tensor_scalar(m1, cIOTA[:], vec(vDD), None, ALU.is_equal)
            GAP()
            lg2 = SMALL[:, 48:56]
            eng.scalar_tensor_tensor(lg2, m1, -1.0e30, lgf, ALU.mult, ALU.add)
            GAP()
            eng.tensor_scalar(e1, lg2, m8[:, 1:2], None, ALU.is_equal)
            GAP()
            eng.tensor_tensor(tq, e1, cIO9[:], ALU.mult)
            GAP()
            eng.tensor_scalar(tq, tq, 9.0, None, ALU.add)
            GAP()
            eng.tensor_reduce(vec(vDD), tq, AX.X, ALU.min)
            eng.tensor_copy(vec(vRD), vec(vEE))
            m2 = SMALL[:, 56:64]
            return eng.tensor_scalar(m2, cIOTA[:], vec(vDD), None,
                                     ALU.is_equal)
        t8 = S.t("vector", f8, [t7, tok_const])
        st["pMg"] = t8
        st["pMb"] = t8

        def f9(eng):
            return eng.activation(vec(vEE), vec(vEE), AF.Exp)
        t9 = S.t("scalar", f9, [t8])

        def f10(eng, c_local=c_local):
            den = vec(vRD)
            eng.tensor_scalar(den, vec(vEE), 1.0, None, ALU.add)
            eng.tensor_copy(VEC[:, 12:13], VEC[:, 15:16])
            eng.reciprocal(vec(vW1), den)
            eng.tensor_copy(VEC[:, 12:13], VEC[:, 15:16])
            eng.tensor_tensor(vec(vW2), vec(vEE), vec(vW1), ALU.mult)
            m1 = SMALL[:, 40:48]
            m2 = SMALL[:, 56:64]
            cmb = COMB[:, c_local * 8:(c_local + 1) * 8]
            tq = SMALL[:, 32:40]
            eng.tensor_scalar(tq, m1, vec(vW1), None, ALU.mult)
            eng.tensor_copy(VEC[:, 12:13], VEC[:, 15:16])
            return eng.scalar_tensor_tensor(cmb, m2, vec(vW2), tq,
                                            ALU.mult, ALU.add)
        t10 = S.t("vector", f10, [t9, st.get("COMB")])

        def f11(eng, c_local=c_local):
            return eng.transpose(pM[0:8, 72:200],
                                 COMB[:, c_local * 8:(c_local + 1) * 8], cIDF[:])
        t11 = S.t("tensor", f11, [t10, tok_ident_a, tok_ident,
                                  st.get("pMct"), st.get("pMb")])

        def f12(eng):
            return eng.tensor_copy(CTB[0:8, :], pM[0:8, 72:200])
        t12 = S.t("vector", f12, [t11, st.get("CTB")])
        st["pMct"] = t12
        st["pMb"] = t12

        def f13(eng):
            return [eng.matmul(pC[:, hh * 512:(hh + 1) * 512], CTB[0:8, :],
                               cB2R[0:8, hh * 512:(hh + 1) * 512],
                               start=True, stop=True) for hh in range(2)]
        t13 = S.t("tensor", f13, [t12, tok_const, st.get("pC"),
                                  st.get("pCh0"), st.get("pCh1")])
        st["CTB"] = t13

        def f14(eng, xc_ap=xc_ap):
            return eng.tensor_tensor(xc_ap, xc_ap, pC[:], ALU.add)
        t14 = S.t("vector", f14, [t13])
        st["pC"] = t14
        st["pCh0"] = t14
        st["pCh1"] = t14
        return t14, tok_nf

    def ffn_experts_sb(tok_nf_blk, comb_tok, tok_w1_free, tok_w2_free,
                       mid_cb=None):
        ep_tok = [None] * SBC
        tok_w1_read = None
        tok_w2_read = None
        for e in range(NEXP):
            def f_w1(eng, e=e):
                return eng.dma_start(
                    WA[:, 0:16384].rearrange("p (k o) -> p k o", o=2048),
                    dw1t[e].rearrange("(k p) o -> p k o", p=128))
            tok_lw1 = S.dma(f_w1, 1, [tok_w1_read if e > 0 else tok_w1_free])

            def f_w2(eng, e=e):
                return eng.dma_start(
                    WA[:, 16384:24576].rearrange("p (k o) -> p k o", o=1024),
                    dw2t[e].rearrange("(k p) o -> p k o", p=128))
            tok_lw2 = S.dma(f_w2, 1, [tok_w2_read if e > 0 else tok_w2_free])

            for blk in range(2):
                pool_last = None
                tE3_vals = []
                for jp in range(8):
                    if jp % 2 == 0:
                        pg, pl = pA[:], pB[:]
                        kg, kl = "pA", "pB"
                    else:
                        pg, pl = pC[:, 0:512], pC[:, 512:1024]
                        kg, kl = "pCh0", "pCh1"

                    def fG(eng, blk=blk, jp=jp, pg=pg):
                        return [eng.matmul(
                            pg,
                            WA[:, k * 2048 + jp * 128: k * 2048 + (jp + 1) * 128],
                            NF[:, k * 1024 + blk * 512: k * 1024 + (blk + 1) * 512],
                            start=(k == 0), stop=(k == 7)) for k in range(8)]
                    tG = S.t("tensor", fG, [tok_lw1, tok_nf_blk[blk],
                                            st.get(kg), st.get("pC") if jp % 2 else None])

                    def fL(eng, blk=blk, jp=jp, pl=pl):
                        return [eng.matmul(
                            pl,
                            WA[:, k * 2048 + 1024 + jp * 128: k * 2048 + 1024 + (jp + 1) * 128],
                            NF[:, k * 1024 + blk * 512: k * 1024 + (blk + 1) * 512],
                            start=(k == 0), stop=(k == 7)) for k in range(8)]
                    tL = S.t("tensor", fL, [st.get(kl),
                                            st.get("pC") if jp % 2 else None])
                    tok_w1_read = tL

                    par = jp % 2
                    g1 = SC1[:, par * 512:(par + 1) * 512]
                    l1 = SC2[:, par * 512:(par + 1) * 512]
                    sg = SC3[:, par * 512:(par + 1) * 512]

                    def fE1(eng, e=e, jp=jp, g1=g1, l1=l1, pg=pg, pl=pl):
                        eng.tensor_scalar(
                            l1, pl,
                            cB1C[:, e * 16 + 8 + jp: e * 16 + 8 + jp + 1],
                            LIMIT, ALU.add, ALU.min)
                        eng.tensor_scalar(g1, pg,
                                          cB1C[:, e * 16 + jp: e * 16 + jp + 1],
                                          LIMIT, ALU.add, ALU.min)
                        return eng.tensor_scalar(l1, l1, -LIMIT, 1.0,
                                                 ALU.max, ALU.add)
                    tE1 = S.t("vector", fE1, [tG, tL, tok_const,
                                              st.get(("m1sc", par))])
                    st[kg] = tE1
                    st[kl] = tE1

                    def fE2(eng, sg=sg, g1=g1):
                        return eng.activation(sg, g1, AF.Sigmoid, scale=ALPHA)
                    tE2 = S.t("scalar", fE2, [tE1])

                    def fE3(eng, g1=g1, l1=l1, sg=sg, jp=jp):
                        eng.tensor_tensor(g1, g1, sg, ALU.mult)
                        eng.tensor_copy(VEC[:, 11:12], VEC[:, 15:16])
                        return eng.tensor_tensor(AFM[:, jp * 512:(jp + 1) * 512],
                                                 g1, l1, ALU.mult)
                    tE3 = S.t("gpsimd", fE3, [tE2, st.get("AFM")])
                    st[("m1sc", par)] = tE3
                    pool_last = tE3
                    tE3_vals.append(tE3[1])

                for tch in range(4):
                    c_local = blk * 4 + tch
                    for hh in range(2):
                        ph = pD[:, hh * 512:(hh + 1) * 512]
                        pk = "pDh%d" % hh

                        def fM2(eng, tch=tch, hh=hh, ph=ph, tv=tuple(tE3_vals)):
                            r = []
                            for k2 in range(8):
                                eng.wait_ge(sems["gpsimd"], tv[k2])
                                r.append(eng.matmul(
                                    ph,
                                    AFM[:, k2 * 512 + tch * 128: k2 * 512 + (tch + 1) * 128],
                                    WA[:, 16384 + k2 * 1024 + hh * 512: 16384 + k2 * 1024 + (hh + 1) * 512],
                                    start=(k2 == 0), stop=(k2 == 7)))
                            return r
                        tM2 = S.t("tensor", fM2, [tok_lw2,
                                                  st.get(pk), st.get("pD")])
                        tok_w2_read = tM2

                        def fEp(eng, e=e, c_local=c_local, hh=hh, ph=ph):
                            a = ACC[:, c_local * 1024 + hh * 512:
                                    c_local * 1024 + (hh + 1) * 512]
                            return eng.scalar_tensor_tensor(
                                a, ph,
                                COMB[:, c_local * 8 + e: c_local * 8 + e + 1],
                                a, ALU.mult, ALU.add)
                        tEp = S.t("vector", fEp, [tM2, comb_tok[c_local]])
                        st[pk] = tEp
                        ep_tok[c_local] = tEp
                st["AFM"] = tok_w2_read
                if mid_cb is not None and e == 0 and blk == 0:
                    mid_cb()
                    mid_cb = None
        st["COMB"] = ep_tok[SBC - 1]
        st["ACC"] = ep_tok[SBC - 1]
        st["SC1"] = pool_last
        st["SC2"] = pool_last
        st["SC3"] = pool_last
        return tok_w1_read, tok_w2_read, ep_tok

    def ln_to_y2b(x_tok, x_ap, y2dst=None, y2key="SB1", extra_deps=(),
                  scratch=None):
        if scratch is None:
            bA, bB, kA, kB = SC1[:], SC2[:], "SC1", "SC2"
        else:
            bA, bB, kA, kB = scratch

        def f1(eng, x_ap=x_ap, bA=bA, bB=bB):
            eng.bn_stats(bB[:, 0:6], x_ap[:, 0:512])
            eng.bn_stats(bB[:, 6:12], x_ap[:, 512:1024])
            eng.tensor_copy(VEC[:, 12:13], VEC[:, 15:16])
            eng.tensor_copy(VEC[:, 13:14], VEC[:, 15:16])
            eng.bn_aggr(VEC[:, vM:vM + 2], bB[:, 0:12])
            eng.tensor_copy(VEC[:, 12:13], VEC[:, 15:16])
            eng.tensor_copy(VEC[:, 13:14], VEC[:, 15:16])
            eng.tensor_scalar(bA, x_ap, vec(vM), None, ALU.subtract)
            return eng.tensor_scalar(vec(vVAR), vec(vVAR), EPS, None, ALU.add)
        t1 = S.t("vector", f1, [x_tok, st.get(kA), st.get(kB)] + list(extra_deps))
        if _CFG['pre_upto'] <= 1:
            return t1, t1, t1

        def f2(eng):
            eng.activation(vec(vRSTD), vec(vVAR), AF.Ln)
            return eng.activation(vec(vRSTD), vec(vRSTD), AF.Exp, scale=-0.5)
        t2 = S.t("scalar", f2, [t1])

        if y2dst is None:
            y2dst = SB1[:]

        def f3(eng, y2dst=y2dst, bA=bA):
            return eng.tensor_scalar(y2dst, bA, vec(vRSTD), None, ALU.mult)
        t3 = S.t("vector", f3, [t2, st.get(y2key)])
        st[kA] = t3
        st[kB] = t1
        return t3

    def proj16(dst_psum, pkey, y2fm_tok, w_sb, w_base, w_tok):
        def f(eng, dst=dst_psum, w_sb=w_sb, w_base=w_base):
            r = []
            for oh in range(2):
                for k in range(8):
                    r.append(eng.matmul(
                        dst[:, oh * 512:(oh + 1) * 512],
                        Y2FM[:, k * 128:(k + 1) * 128],
                        w_sb[:, w_base + k * 1024 + oh * 512: w_base + k * 1024 + (oh + 1) * 512],
                        start=(k == 0), stop=(k == 7)))
            return r
        tok = S.t("tensor", f, [y2fm_tok, w_tok, st.get(pkey),
                                st.get(pkey + "h0"), st.get(pkey + "h1")])
        return tok

    def rope_relu(p_src, dst_bf, dst_key, c_local, rope_tok, proj_tok,
                  scratch=None):
        co = c_local * 64
        if scratch is None:
            bA, bB, kA, kB = SC1[:], SC2[:], "SC1", "SC2"
        else:
            bA, bB, kA, kB = scratch
        P1v = p_src[:].rearrange("p (h a i) -> p h a i", a=2, i=32)[:, :, 0, :]
        P2v = p_src[:].rearrange("p (h a i) -> p h a i", a=2, i=32)[:, :, 1, :]
        CC = ROPE[:, co:co + 32].unsqueeze(1).to_broadcast([128, 16, 32])
        SS = ROPE[:, co + 32:co + 64].unsqueeze(1).to_broadcast([128, 16, 32])

        def hv(buf, half):
            return buf[:, half * 512:(half + 1) * 512].rearrange(
                "p (h i) -> p h i", i=32)

        D1 = dst_bf[:].rearrange("p (h a i) -> p h a i", a=2, i=32)[:, :, 0, :]
        D2 = dst_bf[:].rearrange("p (h a i) -> p h a i", a=2, i=32)[:, :, 1, :]

        def f1(eng):
            eng.tensor_tensor(hv(bA, 0), P1v, CC, ALU.mult)
            eng.tensor_tensor(hv(bA, 1), P2v, SS, ALU.mult)
            eng.tensor_tensor(hv(bB, 0), P1v, SS, ALU.mult)
            eng.tensor_tensor(hv(bB, 1), P2v, CC, ALU.mult)
            eng.tensor_tensor(hv(bA, 0), hv(bA, 0), hv(bA, 1), ALU.subtract)
            eng.tensor_tensor(hv(bB, 0), hv(bB, 0), hv(bB, 1), ALU.add)
            eng.tensor_scalar(D1, hv(bA, 0), 0.0, None, ALU.max)
            return eng.tensor_scalar(D2, hv(bB, 0), 0.0, None, ALU.max)
        t1 = S.t("vector", f1, [proj_tok, rope_tok, st.get(kA),
                                st.get(kB), st.get(dst_key)])
        t2 = t1
        st[kA] = t2
        st[kB] = t2
        return t2, t1

    x0_par_reader = [None, None]
    tok_rope_free = None
    tok_srun = None
    tok_w1_free = tok_wconv
    tok_w2_free = tok_wconv
    own_x1_tok = [None] * N_OWN
    tok_ds_store = [None] * N_OWN
    tok_wk = tok_wv = None
    tok_fin_last_pe = None

    for sb_i in range(_CFG['p1_sbs']):
        def f_rope(eng, sb_i=sb_i):
            return eng.dma_start(
                ROPE[:].rearrange("p (c z) -> p c z", z=64),
                drope[sb_i * 1024:(sb_i + 1) * 1024, :].rearrange(
                    "(c p) z -> p c z", p=128))
        tok_rope = S.dma(f_rope, 1, [tok_rope_free])

        tok_acc = [None] * SBC
        tok_nf_blk = [None, None]
        preA = {}
        NF3 = [(SB2, "bSB2"), (SB2B, "bSB2B"), (SB1B, "bSB1B")]

        def emit_A(c, sb_i=sb_i):
            if c in preA or c >= SBC:
                return
            g = sb_i * SBC + c
            par = c % 2
            tri = c % 3

            def f_x0(eng, g=g, par=par):
                return eng.dma_start(X0P[:, par * 1024:(par + 1) * 1024],
                                     dx0[g * 128:(g + 1) * 128, :])
            tok_x0 = S.dma(f_x0, 1, [x0_par_reader[par]])
            x_ap = X0P[:, par * 1024:(par + 1) * 1024]
            nf_dst, nf_key = NF3[tri]
            ta3 = prepass_A(tok_x0, x_ap, ACC[:, c * 1024:(c + 1) * 1024],
                            SB1[:], "bSB1", nf_dst[:], nf_key)
            x0_par_reader[par] = ta3
            preA[c] = ta3

        def emit_B(c):
            tri = c % 3
            nf_src, nf_key = NF3[tri]
            t_acc, t_nf = prepass_B(c, nf_src[:], preA[c],
                                    ACC[:, c * 1024:(c + 1) * 1024])
            st[nf_key] = t_nf
            tok_acc[c] = t_acc
            if c == 3:
                tok_nf_blk[0] = t_nf
            if c == SBC - 1:
                tok_nf_blk[1] = t_nf

        def emit_pre(c0, c1, sb_i=sb_i):
            for c in range(c0, c1):
                emit_A(c)
                emit_A(c + 1)
                emit_A(c + 2)
                emit_B(c)

        emit_pre(0, 4)
        if _CFG['experts']:
            tok_w1r, tok_w2r, ep_tok = ffn_experts_sb(
                tok_nf_blk, tok_acc, tok_w1_free, tok_w2_free,
                mid_cb=lambda: emit_pre(4, SBC))
        else:
            emit_pre(4, SBC)
            tok_w1r = tok_w2r = None
            ep_tok = list(tok_acc)

        if not _CFG['finish']:
            tok_w1_free = tok_w1r
            tok_w2_free = tok_w2r
            continue

        def f_wk(eng):
            return eng.dma_start(
                WA[:, 0:8192].rearrange("p (k o) -> p k o", o=1024),
                swb[1].rearrange("(k p) o -> p k o", p=128))

        def f_wv(eng):
            return eng.dma_start(
                WA[:, 8192:16384].rearrange("p (k o) -> p k o", o=1024),
                swb[2].rearrange("(k p) o -> p k o", p=128))
        tok_wk = S.dma(f_wk, 1, [tok_w1r, tok_wconv])
        tok_wv = S.dma(f_wv, 1, [tok_w1r])

        st["AFMa"] = st.get("AFM")
        st["AFMb"] = st.get("AFM")
        pend = None

        def emit_ds(pend):
            g_, own_, oc_, t_kr_, t_v65_ = pend

            def f_ds(eng):
                r = []
                for h in range(16):
                    p = h // 2
                    dst = pA if p < 4 else pB
                    r.append(eng.matmul(
                        dst[(h % 2) * 64:(h % 2) * 64 + 64,
                            (p % 4) * 65:(p % 4) * 65 + 65],
                        SB2[:, h * 64:(h + 1) * 64],
                        V65[:, h * 65:(h + 1) * 65],
                        start=True, stop=True))
                return r
            t_ds = S.t("tensor", f_ds, [t_kr_, t_v65_, st.get("pA"),
                                        st.get("pB")])
            st["SB2"] = t_ds
            st["V65"] = t_ds

            def f_su(eng, g=g_):
                eng.scalar_tensor_tensor(SRUN[:, 0:260], pA[:, 0:260],
                                         cSMK[:, g:g + 1], SRUN[:, 0:260],
                                         ALU.mult, ALU.add)
                return eng.scalar_tensor_tensor(SRUN[:, 260:520], pB[:, 0:260],
                                                cSMK[:, g:g + 1],
                                                SRUN[:, 260:520],
                                                ALU.mult, ALU.add)
            t_su = S.t("vector", f_su, [t_ds, tok_const])
            st["pA"] = t_su
            st["pB"] = t_su
            if own_:
                def f_dss(eng, oc=oc_):
                    eng.tensor_copy(DS[:, oc * 520: oc * 520 + 260],
                                    pA[:, 0:260])
                    return eng.tensor_copy(
                        DS[:, oc * 520 + 260: oc * 520 + 520], pB[:, 0:260])
                t_dss = S.t("vector", f_dss, [t_ds])
                tok_ds_store[oc_] = t_dss
                st["pA"] = t_dss
                st["pB"] = t_dss
            return t_ds, t_su

        own = sb_i >= OWN_SB0
        finA = {}

        def emit_Af(c, sb_i=sb_i, own=own):
            g = sb_i * SBC + c
            par = c % 2
            oc = (sb_i - OWN_SB0) * SBC + c if own else None

            def f_x0f(eng, g=g, par=par):
                return eng.dma_start(X0P[:, par * 1024:(par + 1) * 1024],
                                     dx0[g * 128:(g + 1) * 128, :])
            tok_x0f = S.dma(f_x0f, 1, [x0_par_reader[par]])

            if own:
                x1_ap = X1O[:, oc * 1024:(oc + 1) * 1024]
            else:
                x1_ap = X1O[:, par * 1024:(par + 1) * 1024]

            def f_x1(eng, c=c, par=par, x1_ap=x1_ap):
                return eng.scalar_tensor_tensor(
                    x1_ap, ACC[:, c * 1024:(c + 1) * 1024], 0.5,
                    X0P[:, par * 1024:(par + 1) * 1024], ALU.mult, ALU.add)
            t_x1 = S.t("vector", f_x1, [tok_x0f, ep_tok[c]])
            x0_par_reader[par] = t_x1
            if own:
                own_x1_tok[oc] = t_x1
            y2dst = (SB1 if par == 0 else SB1B)[:]
            t_y2b = ln_to_y2b(t_x1, x1_ap, y2dst, "bSB1" if par == 0 else "bSB1B")
            finA[c] = (t_y2b, y2dst, par)

        def emit_Bf(c, sb_i=sb_i, own=own):
            nonlocal pend
            g = sb_i * SBC + c
            oc = (sb_i - OWN_SB0) * SBC + c if own else None
            t_y2b, y2dst, par = finA[c]
            t_y2fm = transpose8(y2dst, Y2FM[:], t_y2b)
            st["bSB1" if par == 0 else "bSB1B"] = t_y2fm

            t_kp = proj16(pC, "pC", t_y2fm, WA, 0, tok_wk)
            if pend is not None:
                emit_ds(pend)
            t_kr, t_kread = rope_relu(
                pC, SB2[:], "SB2", c, tok_rope, t_kp,
                scratch=(AFMf[:, 0:1024], AFMf[:, 1024:2048],
                         "AFMa", "AFMb"))
            st["pC"] = t_kread
            st["pCh0"] = t_kread
            st["pCh1"] = t_kread

            t_vp = proj16(pD, "pD", t_y2fm, WA, 8192, tok_wv)

            def f_v65(eng):
                pv = pD[:].rearrange("p (h d) -> p h d", d=64)
                ov = V65[:].rearrange("p (h z) -> p h z", z=65)[:, :, 0:64]
                return eng.tensor_copy(ov, pv)
            t_v65 = S.t("vector", f_v65, [t_vp, st.get("V65")])
            st["pD"] = t_v65
            st["pDh0"] = t_v65
            st["pDh1"] = t_v65
            pend = (g, own, oc, t_kr, t_v65)

        for c in range(SBC):
            if c == 0:
                emit_Af(0)
            if c + 1 < SBC:
                emit_Af(c + 1)
            emit_Bf(c)

        tok_fin_last_pe, tok_srun = emit_ds(pend)
        st["AFM"] = st["AFMa"]
        tok_rope_free = tok_fin_last_pe
        tok_w1_free = tok_fin_last_pe
        tok_w2_free = tok_w2r

    full_p1 = (_CFG['p1_sbs'] == NSB and _CFG['experts'] and _CFG['finish'])
    do_p2 = _CFG['p2'] and full_p1
    do_p3 = _CFG['p3'] and full_p1

    if do_p2:
        def f_wq(eng):
            return eng.dma_start(
                NF[:, 0:8192].rearrange("p (k o) -> p k o", o=1024),
                swb[0].rearrange("(k p) o -> p k o", p=128))
        tok_wq = S.dma(f_wq, 1, [tok_w1r, tok_wconv])

        def f_wo(eng):
            return eng.dma_start(
                ACC[:, 0:8192].rearrange("p (k o) -> p k o", o=1024),
                swb[3].rearrange("(k p) o -> p k o", p=128))
        tok_wo = S.dma(f_wo, 1, [own_x1_tok[N_OWN - 1]])

    tok_out_x2 = [None] * N_OWN
    tok_sbf_reader = None
    tok_rope = None
    p2_count = min(_CFG['p2_n'], N_OWN) if do_p2 else 0
    p2lnA = {}
    pend_x2 = [None]

    def emit_x2():
        poc, t_wop_p = pend_x2[0]
        pend_x2[0] = None

        def f_x2(eng, poc=poc):
            a = X1O[:, poc * 1024:(poc + 1) * 1024]
            return eng.tensor_tensor(a, a, pD[:], ALU.add)
        t_x2 = S.t("vector", f_x2, [t_wop_p])
        st["pD"] = t_x2
        st["pDh0"] = t_x2
        st["pDh1"] = t_x2
        tok_out_x2[poc] = t_x2
    AFM_SCR = (AFMf[:, 0:1024], AFMf[:, 1024:2048], "AFMa", "AFMb")
    for oc in range(p2_count):
        sb_i = OWN_SB0 + oc // SBC
        c_local = oc % SBC
        if c_local == 0:
            def f_rope2(eng, sb_i=sb_i):
                return eng.dma_start(
                    ROPE[:].rearrange("p (c z) -> p c z", z=64),
                    drope[sb_i * 1024:(sb_i + 1) * 1024, :].rearrange(
                        "(c p) z -> p c z", p=128))
            tok_rope = S.dma(f_rope2, 1, [tok_rope_free])

        x2_ap = X1O[:, oc * 1024:(oc + 1) * 1024]
        par = oc % 2
        lnbuf = (SB1 if par == 0 else SB1B)[:]
        lnkey = "bSB1" if par == 0 else "bSB1B"
        if oc in p2lnA:
            t_y2b = p2lnA.pop(oc)
        else:
            t_y2b = ln_to_y2b(own_x1_tok[oc], x2_ap, lnbuf, lnkey,
                              scratch=AFM_SCR)
        t_y2fm = transpose8(lnbuf, Y2FM[:], t_y2b)
        st[lnkey] = t_y2fm

        if _CFG['p2_upto'] <= 1:
            tok_out_x2[oc] = t_y2fm
            continue
        def f_qp(eng):
            r = []
            for oh in range(2):
                for k in range(8):
                    r.append(eng.matmul(
                        pC[:, oh * 512:(oh + 1) * 512],
                        Y2FM[:, k * 128:(k + 1) * 128],
                        NF[:, k * 1024 + oh * 512: k * 1024 + (oh + 1) * 512],
                        start=(k == 0), stop=(k == 7)))
            return r
        t_qp = S.t("tensor", f_qp, [t_y2fm, tok_wq, st.get("pC"),
                                    st.get("pCh0"), st.get("pCh1")])
        t_kp = proj16(pD, "pD", t_y2fm, WA, 0, tok_wk)
        t_qr, t_qread = rope_relu(pC, SB2[:], "SB2", c_local, tok_rope, t_qp)
        st["pC"] = t_qread
        st["pCh0"] = t_qread
        st["pCh1"] = t_qread
        t_qfm = transpose8(SB2[:], QFM[:], t_qr)
        st["SB2"] = t_qfm

        if _CFG['p2_upto'] <= 2:
            tok_out_x2[oc] = t_qfm
            continue
        t_kr, t_kread = rope_relu(pD, SB2[:], "SB2", c_local, tok_rope, t_kp)
        st["pD"] = t_kread
        st["pDh0"] = t_kread
        st["pDh1"] = t_kread
        tok_rope_free = t_kr
        if oc + 1 < p2_count:
            npar = (oc + 1) % 2
            p2lnA[oc + 1] = ln_to_y2b(
                own_x1_tok[oc + 1], X1O[:, (oc + 1) * 1024:(oc + 2) * 1024],
                (SB1 if npar == 0 else SB1B)[:],
                "bSB1" if npar == 0 else "bSB1B", scratch=AFM_SCR)

        t_vp = proj16(pC, "pC", t_y2fm, WA, 8192, tok_wv)
        t_kfm = transpose8(SB2[:], KFM[:], t_kr, extra=[st.get("KFM")])
        st["SB2"] = t_kfm

        def f_v65(eng):
            pv = pC[:].rearrange("p (h d) -> p h d", d=64)
            ov = V65[:].rearrange("p (h z) -> p h z", z=65)[:, :, 0:64]
            return eng.tensor_copy(ov, pv)
        t_v65 = S.t("vector", f_v65, [t_vp, st.get("V65")])
        st["pC"] = t_v65
        st["pCh0"] = t_v65
        st["pCh1"] = t_v65

        def f_sbf(eng):
            return eng.tensor_copy(SBF[:], SRUN[:])
        t_sbfr = S.t("vector", f_sbf, [tok_srun, tok_sbf_reader])

        if _CFG['p2_upto'] <= 4:
            tok_out_x2[oc] = t_sbfr
            continue
        SC_AP = [pA[:, 0:128], pC[:, 0:128], pA[:, 128:256], pC[:, 128:256]]
        NM_AP = [pB[:, 0:65], pC[:, 512:577], pB[:, 65:130], pC[:, 577:642]]
        X0Pb = X0P[:].bitcast(BF16)
        AB_AP = [ABUF[:, 0:128], ABUF[:, 128:256],
                 X0Pb[:, 0:128], X0Pb[:, 128:256]]
        t_mk_h = {}
        t_nm_h = {}
        t_ep_h = {}
        n_heads = min(16, _CFG['p2_nh'])

        def emit_sc_mk(h):
            hp4 = h % 4
            f_tile = h // 2
            b0 = (h % 2) * 64
            sc_ap = SC_AP[hp4]

            def f_sc(eng, f_tile=f_tile, b0=b0, sc_ap=sc_ap):
                return eng.matmul(
                    sc_ap,
                    KFM[b0:b0 + 64, f_tile * 128:(f_tile + 1) * 128],
                    QFM[b0:b0 + 64, f_tile * 128:(f_tile + 1) * 128],
                    start=True, stop=True)
            t_sc = S.t("tensor", f_sc, [t_kfm, t_qfm, t_v65,
                                        t_mk_h.get(h - 2)])

            def f_mk(eng, hp4=hp4, sc_ap=sc_ap):
                return eng.tensor_tensor(AB_AP[hp4], sc_ap, cTRI[:],
                                         ALU.mult)
            xg = [x0_par_reader[0], x0_par_reader[1]] \
                if (oc == 0 and hp4 >= 2) else []
            t_mk_h[h] = S.t("vector", f_mk, [t_sc, tok_const] + xg)

        def emit_nm_ep(h):
            hp4 = h % 4
            f_tile = h // 2
            b0 = (h % 2) * 64
            p = h // 2
            nm_ap = NM_AP[hp4]

            def f_nm(eng, h=h, hp4=hp4, f_tile=f_tile, b0=b0, p=p,
                     nm_ap=nm_ap):
                r = [eng.matmul(nm_ap,
                                AB_AP[hp4],
                                V65[:, h * 65:(h + 1) * 65],
                                start=True, stop=False)]
                base = 0 if p < 4 else 260
                sb_t = SBF[b0:b0 + 64,
                           base + (p % 4) * 65: base + (p % 4) * 65 + 65]
                r.append(eng.matmul(nm_ap,
                                    QFM[b0:b0 + 64,
                                        f_tile * 128:(f_tile + 1) * 128],
                                    sb_t, start=False, stop=True))
                return r
            t_nm = S.t("tensor", f_nm, [t_mk_h[h], t_v65, t_sbfr,
                                        t_ep_h.get(h - 2)])
            t_nm_h[h] = t_nm

            def f_ep(eng, h=h, nm_ap=nm_ap):
                eng.tensor_scalar(vec(vRD), nm_ap[:, 64:65],
                                  EPS_ATT, None, ALU.add)
                eng.tensor_copy(VEC[:, 12:13], VEC[:, 15:16])
                eng.reciprocal(vec(vRR), vec(vRD))
                eng.tensor_copy(VEC[:, 12:13], VEC[:, 15:16])
                return eng.tensor_scalar(SC2[:, h * 64:(h + 1) * 64],
                                         nm_ap[:, 0:64], vec(vRR),
                                         None, ALU.mult)
            dep_sc2 = [st.get("SC2")] if h == 0 else []
            t_ep_h[h] = S.t("vector", f_ep, [t_nm] + dep_sc2)

        for h in range(n_heads):
            emit_sc_mk(h)
            if h >= 3:
                emit_nm_ep(h - 3)
        for h in range(max(0, n_heads - 3), n_heads):
            emit_nm_ep(h)
        last_nm = t_nm_h[n_heads - 1]
        attn_tok = t_ep_h[n_heads - 1]
        st["pA"] = attn_tok
        st["pB"] = attn_tok
        st["pC"] = attn_tok
        st["pCh0"] = attn_tok
        st["pCh1"] = attn_tok
        tok_sbf_reader = last_nm
        st["V65"] = last_nm

        if _CFG['p2_upto'] <= 5:
            tok_out_x2[oc] = attn_tok
            continue
        def f_sup(eng, oc=oc):
            eng.tensor_tensor(SRUN[:, 0:260], SRUN[:, 0:260],
                              DS[:, oc * 520: oc * 520 + 260], ALU.add)
            return eng.tensor_tensor(SRUN[:, 260:520], SRUN[:, 260:520],
                                     DS[:, oc * 520 + 260: oc * 520 + 520],
                                     ALU.add)
        tok_srun = S.t("vector", f_sup, [tok_ds_store[oc], tok_srun])

        def f_atb(eng):
            return eng.tensor_copy(SB2[:], SC2[:])
        t_atb = S.t("vector", f_atb, [attn_tok, st.get("SB2")])
        st["SC2"] = t_atb
        t_afm = transpose8(SB2[:], KFM[:], t_atb, extra=[last_nm])
        st["SB2"] = t_afm
        st["KFM"] = None

        if _CFG['p2_upto'] <= 6:
            tok_out_x2[oc] = t_afm
            continue

        def f_wop(eng):
            r = []
            for oh in range(2):
                for k in range(8):
                    r.append(eng.matmul(
                        pD[:, oh * 512:(oh + 1) * 512],
                        KFM[:, k * 128:(k + 1) * 128],
                        ACC[:, k * 1024 + oh * 512: k * 1024 + (oh + 1) * 512],
                        start=(k == 0), stop=(k == 7)))
            return r
        t_wop = S.t("tensor", f_wop, [t_afm, tok_wo, st.get("pD"),
                                      st.get("pDh0"), st.get("pDh1")])
        st["KFM"] = t_wop

        def f_x2(eng, oc=oc):
            a = X1O[:, oc * 1024:(oc + 1) * 1024]
            return eng.tensor_tensor(a, a, pD[:], ALU.add)
        t_x2 = S.t("vector", f_x2, [t_wop])
        st["pD"] = t_x2
        st["pDh0"] = t_x2
        st["pDh1"] = t_x2
        tok_out_x2[oc] = t_x2

    tok_out_dma = []
    if not do_p3:
        for oc in range(N_OWN):
            def f_outd(eng, oc=oc):
                return eng.dma_start(dout[oc * 128:(oc + 1) * 128, :],
                                     X1O[:, oc * 1024:(oc + 1) * 1024])
            dep = [tok_out_x2[oc] if (do_p2 and tok_out_x2[oc]) else None,
                   ("vector", S.cnt["vector"]), ("tensor", S.cnt["tensor"]),
                   ("scalar", S.cnt["scalar"]), ("gpsimd", S.cnt["gpsimd"])]
            tok_out_dma.append(S.dma(f_outd, 1, dep))
    else:
        tok_p2_done = tok_out_x2[N_OWN - 1]
        tok_w1_free = tok_p2_done
        tok_w2_free_p3 = tok_w2r
        out_par_reader = [None, None]
        tok_out_dma = []
        tok_w1r_p3 = None
        st["ACC"] = tok_p2_done
        st["SB1"] = tok_p2_done
        st["SB2"] = tok_p2_done
        st["SC2"] = tok_p2_done
        st["AFM"] = tok_p2_done
        st["bSB1"] = tok_p2_done
        st["bSB1B"] = tok_p2_done

        for sb2 in range(2):
            tok_acc = [None] * SBC
            tok_nf_blk = [None, None]
            preA3 = {}
            NF3 = [(SB2, "bSB2"), (SB2B, "bSB2B"), (SB1B, "bSB1B")]

            def emit_A3(c, sb2=sb2):
                if c in preA3 or c >= SBC:
                    return
                oc = sb2 * SBC + c
                tri = c % 3
                x_ap = X1O[:, oc * 1024:(oc + 1) * 1024]
                extra = []
                if sb2 == 0 and c == 0:
                    extra = [tok_p2_done]
                if sb2 == 1 and c == 0:
                    extra = [out_par_reader[0], out_par_reader[1]]
                nf_dst, nf_key = NF3[tri]
                ta3 = prepass_A(tok_out_x2[oc], x_ap,
                                ACC[:, c * 1024:(c + 1) * 1024],
                                SB1[:], "bSB1", nf_dst[:], nf_key,
                                extra_deps=extra)
                preA3[c] = ta3

            def emit_B3(c):
                tri = c % 3
                nf_src, nf_key = NF3[tri]
                t_acc, t_nf = prepass_B(c, nf_src[:], preA3[c],
                                        ACC[:, c * 1024:(c + 1) * 1024])
                st[nf_key] = t_nf
                tok_acc[c] = t_acc
                if c == 3:
                    tok_nf_blk[0] = t_nf
                if c == SBC - 1:
                    tok_nf_blk[1] = t_nf

            def emit_pre3(c0, c1):
                for c in range(c0, c1):
                    emit_A3(c)
                    emit_A3(c + 1)
                    emit_A3(c + 2)
                    emit_B3(c)

            emit_pre3(0, 4)
            tok_w1r_p3, tok_w2r_p3, ep_tok = ffn_experts_sb(
                tok_nf_blk, tok_acc,
                tok_w1_free if sb2 == 0 else tok_w1r_p3,
                tok_w2_free_p3,
                mid_cb=lambda: emit_pre3(4, SBC))
            tok_w2_free_p3 = tok_w2r_p3

            for c in range(SBC):
                oc = sb2 * SBC + c
                par = c % 2
                x3buf = SC2 if par == 0 else SC3

                def f_x3(eng, oc=oc, c=c, x3buf=x3buf):
                    return eng.scalar_tensor_tensor(
                        x3buf[:], ACC[:, c * 1024:(c + 1) * 1024], 0.5,
                        X1O[:, oc * 1024:(oc + 1) * 1024], ALU.mult, ALU.add)
                t_x3 = S.t("vector", f_x3, [ep_tok[c], out_par_reader[par],
                                            st.get("SC2" if par == 0 else "SC3")])
                if par == 0:
                    st["SC2"] = t_x3
                else:
                    st["SC3"] = t_x3

                def f_out(eng, oc=oc, x3buf=x3buf):
                    return eng.dma_start(dout[oc * 128:(oc + 1) * 128, :], x3buf[:])
                tok_o = S.dma(f_out, 1, [t_x3])
                out_par_reader[par] = tok_o
                tok_out_dma.append(tok_o)

    if _CFG['dbg'] is not None:
        barrier = [("tensor", S.cnt["tensor"]), ("vector", S.cnt["vector"]),
                   ("scalar", S.cnt["scalar"]), ("gpsimd", S.cnt["gpsimd"]),
                   ("sync", S.cnt["sync"])]
        dbg_src = {"acc": ACC, "nf": NF, "xc": SB1, "sb2": SB2,
                   "comb": COMB, "ctb": CTB, "small": SMALL,
                   "x0p": X0P, "sc3": SC3, "wa": WA, "gwt": cGWT,
                   "b1c": cB1C, "b2r": cB2R}[_CFG['dbg']]
        ncols = dbg_src.shape[1] if hasattr(dbg_src, 'shape') else 8192
        ncols = {"acc": 8192, "nf": 8192, "xc": 1024, "sb2": 1024,
                 "comb": 64, "ctb": 128, "small": 64, "x0p": 2048,
                 "sc3": 1024, "wa": 16384, "gwt": 64, "b1c": 128,
                 "b2r": 1024}[_CFG['dbg']]
        prevtok = None
        for c in range((ncols + 1023) // 1024):
            w = min(1024, ncols - c * 1024)

            def f_cp(eng, c=c, w=w, dbg_src=dbg_src):
                return eng.tensor_copy(SC1[:, 0:w], dbg_src[:, c * 1024:c * 1024 + w])
            tcp = S.t("vector", f_cp, barrier + [prevtok])

            def f_dmp(eng, c=c, w=w):
                return eng.dma_start(dout[c * 128:(c + 1) * 128, 0:w], SC1[:, 0:w])
            prevtok = S.dma(f_dmp, 1, [tcp])

    def f_fin(eng):
        return eng.engine_nop()
    S.t("vector", f_fin, [("sync", S.cnt["sync"])])

    blk = ctx.enter_context(nc.Block())
    S.replay(blk, sems)
    ctx.close()
    return nc


_NC_CACHE = {}


def _get_nc():
    if "nc" not in _NC_CACHE:
        _NC_CACHE["nc"] = build_program()
    return _NC_CACHE["nc"]


def _host_prep(inputs):
    x = np.asarray(inputs["L0_attn_x"])
    wq = np.asarray(inputs["L0_attn_wq"])
    wk = np.asarray(inputs["L0_attn_wk"])
    wv = np.asarray(inputs["L0_attn_wv"])
    wo = np.asarray(inputs["L0_attn_wo"])
    nw = np.asarray(inputs["L0_ffn_norm_weight"])
    gw = np.asarray(inputs["L0_ffn_gate_weight"])
    gb = np.asarray(inputs["L0_ffn_gate_bias"])
    w1 = np.asarray(inputs["L0_ffn_mlp1_weight"])
    b1 = np.asarray(inputs["L0_ffn_mlp1_bias"])
    w2 = np.asarray(inputs["L0_ffn_mlp2_weight"])
    b2 = np.asarray(inputs["L0_ffn_mlp2_bias"])

    perm = np.concatenate([
        np.concatenate([h * 64 + 2 * np.arange(32),
                        h * 64 + 2 * np.arange(32) + 1])
        for h in range(NH)])
    wqp = np.ascontiguousarray(wq[:, perm], dtype=np.float32)
    wkp = np.ascontiguousarray(wk[:, perm], dtype=np.float32)

    w1t = np.empty((NEXP, D, DFF), dtype=BFD)
    b1c_all = np.empty((CH, 128), dtype=np.float32)
    for e in range(NEXP):
        T = w1[e].T
        w1t[e] = np.concatenate([T[:, 0::2], T[:, 1::2]], axis=1)
        bp = np.concatenate([b1[e][0::2], b1[e][1::2]]).astype(np.float32)
        b1c_all[:, e * 16:(e + 1) * 16] = bp.reshape(16, CH).T
    w2t = np.ascontiguousarray(np.transpose(w2, (0, 2, 1)))
    gwt = np.ascontiguousarray(gw.T)
    gbc = np.ascontiguousarray(np.broadcast_to(gb.astype(np.float32), (CH, NEXP)))
    nwc = np.ascontiguousarray(np.broadcast_to(nw.astype(np.float32), (CH, D)))
    b2r = np.ascontiguousarray(b2)

    tri = (np.arange(CH)[:, None] <= np.arange(CH)[None, :]).astype(np.float32)
    iota8 = np.ascontiguousarray(
        np.broadcast_to(np.arange(NEXP, dtype=np.float32), (CH, NEXP)))
    io9 = iota8 - 9.0
    inv_freq = (1.0 / THETA ** (np.arange(0, HD, 2).astype(np.float32) / HD)
                ).astype(np.float32)

    in_maps = []
    for core in range(8):
        b = core // 4
        q = core % 4
        permq = list(range(q)) + list(range(q + 1, 4)) + [q]
        x0r = np.ascontiguousarray(
            np.concatenate([x[b, 2048 * j:2048 * (j + 1)] for j in permq],
                           axis=0), dtype=np.float32)
        smask = np.zeros((CH, NCH), dtype=np.float32)
        for i in range(NCH):
            if i // 16 < q:
                smask[:, i] = 1.0
        ropet = np.empty((NCH * CH, 64), dtype=np.float32)
        for i in range(NCH):
            quarter = permq[i // 16]
            base = 2048 * quarter + 128 * (i % 16)
            t = (base + np.arange(CH, dtype=np.float32))[:, None]
            freqs = (t * inv_freq[None, :]).astype(np.float32)
            ropet[i * CH:(i + 1) * CH, 0:32] = np.cos(
                freqs.astype(np.float64)).astype(np.float32)
            ropet[i * CH:(i + 1) * CH, 32:64] = np.sin(
                freqs.astype(np.float64)).astype(np.float32)
        in_maps.append({
            "x0r": x0r, "w1t": w1t, "w2t": w2t, "b1c": b1c_all, "b2r": b2r,
            "gwt": gwt, "gbc": gbc, "nwc": nwc,
            "wqf": wqp, "wkf": wkp,
            "wvf": np.ascontiguousarray(wv, dtype=np.float32),
            "wof": np.ascontiguousarray(wo, dtype=np.float32),
            "rope": ropet, "smask": smask, "tri": tri,
            "iota8": iota8, "io9": io9,
        })
    return in_maps


def kernel(**inputs):
    in_maps = _host_prep(inputs)
    nc = _get_nc()
    res = run_bass_kernel_spmd(nc, in_maps, core_ids=list(range(8)))
    out = np.empty((B, L, D), dtype=np.float32)
    for core in range(8):
        b = core // 4
        q = core % 4
        out[b, 2048 * q:2048 * (q + 1), :] = res.results[core]["xout"]
    return out


# version 22 (final)
# speedup vs baseline: 1.1826x; 1.0580x over previous
import numpy as np
import ml_dtypes
from contextlib import ExitStack
import concourse.bass as bass
import concourse.mybir as mybir
from concourse.bass_utils import run_bass_kernel_spmd

F32 = mybir.dt.float32
BF16 = mybir.dt.bfloat16
AF = mybir.ActivationFunctionType
ALU = mybir.AluOpType
AX = mybir.AxisListType
BFD = ml_dtypes.bfloat16

D = 1024
NH, HD, NPR = 16, 64, 32
L, B = 8192, 2
CH = 128
NCH = 64
SBC = 8
NSB = NCH // SBC
NEXP = 8
DFF = 2048
OWN_SB0 = 6
N_OWN = 16
EPS = 1e-5
EPS_ATT = 1e-6
THETA = 10000.0
LIMIT = 7.0
ALPHA = 1.702

_CFG = dict(p1_sbs=NSB, experts=True, finish=True, p2=True, p3=True, pre_upto=99, dbg=None, p2_n=N_OWN, p2_upto=99, p2_nh=16, p2_hupto=99)


class Sched:
    ENGS = ("sync", "tensor", "vector", "scalar", "gpsimd")

    def __init__(self):
        self.tasks = {e: [] for e in self.ENGS}
        self.cnt = {e: 0 for e in self.ENGS}

    def _mk(self, eng, fn, deps, inc):
        w = {}
        for d in deps:
            if d is None:
                continue
            de, dv = d
            if de == eng and eng != "sync":
                continue
            if dv <= 0:
                continue
            w[de] = max(w.get(de, 0), dv)
        self.tasks[eng].append((sorted(w.items()), fn))
        self.cnt[eng] += inc
        return (eng, self.cnt[eng])

    def t(self, eng, fn, deps=()):
        return self._mk(eng, fn, deps, 1)

    def dma(self, fn, n, deps=()):
        return self._mk("sync", fn, deps, 16 * n)

    def replay(self, blk, sems):
        def runner(eng_name):
            tasks = self.tasks[eng_name]

            def run(eng):
                for waits, fn in tasks:
                    for (se, v) in waits:
                        eng.wait_ge(sems[se], v)
                    r = fn(eng)
                    rl = r if isinstance(r, (list, tuple)) else [r]
                    if eng_name == "sync":
                        for inst in rl:
                            inst.then_inc(sems["sync"], 16)
                    else:
                        rl[-1].then_inc(sems[eng_name], 1)
            return run
        blk.sync(runner("sync"))
        blk.tensor(runner("tensor"))
        blk.vector(runner("vector"))
        blk.scalar(runner("scalar"))
        blk.gpsimd(runner("gpsimd"))


def build_program():
    nc = bass.Bass("TRN2", target_bir_lowering=False,
                   dynamic_dma_scratch_size=2048)

    dx0 = nc.dram_tensor("x0r", [NCH * CH, D], F32, kind="ExternalInput")
    dw1t = nc.dram_tensor("w1t", [NEXP, D, DFF], BF16, kind="ExternalInput")
    dw2t = nc.dram_tensor("w2t", [NEXP, D, D], BF16, kind="ExternalInput")
    db1c = nc.dram_tensor("b1c", [CH, 128], F32, kind="ExternalInput")
    db2r = nc.dram_tensor("b2r", [NEXP, D], BF16, kind="ExternalInput")
    dgwt = nc.dram_tensor("gwt", [D, NEXP], BF16, kind="ExternalInput")
    dgbc = nc.dram_tensor("gbc", [CH, NEXP], F32, kind="ExternalInput")
    dnwc = nc.dram_tensor("nwc", [CH, D], F32, kind="ExternalInput")
    dwq = nc.dram_tensor("wqf", [D, D], F32, kind="ExternalInput")
    dwk = nc.dram_tensor("wkf", [D, D], F32, kind="ExternalInput")
    dwv = nc.dram_tensor("wvf", [D, D], F32, kind="ExternalInput")
    dwo = nc.dram_tensor("wof", [D, D], F32, kind="ExternalInput")
    drope = nc.dram_tensor("rope", [NCH * CH, 64], F32, kind="ExternalInput")
    dsmk = nc.dram_tensor("smask", [CH, NCH], F32, kind="ExternalInput")
    dtri = nc.dram_tensor("tri", [CH, CH], F32, kind="ExternalInput")
    didb = nc.dram_tensor("idb", [CH, CH], BF16, kind="ExternalInput")
    didf = nc.dram_tensor("idf", [CH, CH], F32, kind="ExternalInput")
    diota = nc.dram_tensor("iota8", [CH, NEXP], F32, kind="ExternalInput")
    dio9 = nc.dram_tensor("io9", [CH, NEXP], F32, kind="ExternalInput")
    dout = nc.dram_tensor("xout", [N_OWN * CH, D], F32, kind="ExternalOutput")
    swb = nc.dram_tensor("wbf_scratch", [4, D, D], BF16, kind="Internal")

    ctx = ExitStack()

    def sbuf(name, shape, dtype):
        return ctx.enter_context(nc.sbuf_tensor(name, shape, dtype))

    NF = sbuf("NF", [128, 8192], BF16)
    ACC = sbuf("ACCb", [128, 8192], BF16)
    WA = sbuf("WA", [128, 24576], BF16)
    X1O = sbuf("X1O", [128, 16384], F32)
    DS = sbuf("DSo", [128, N_OWN * 520], BF16)
    SRUN = sbuf("SRUN", [128, 520], F32)
    SBF = sbuf("SBF", [128, 520], BF16)
    ROPE = sbuf("ROPEt", [128, SBC * 64], F32)
    AFM = sbuf("ACTFM", [128, 4096], BF16)
    X0P = sbuf("X0P", [128, 2048], F32)
    SC1 = sbuf("SC1", [128, 1024], F32)
    SC2 = sbuf("SC2", [128, 1024], F32)
    SC3 = sbuf("SC3", [128, 1024], F32)
    SB1 = sbuf("SB1", [128, 1024], BF16)
    SB2 = sbuf("SB2", [128, 1024], BF16)
    SB1B = sbuf("SB1B", [128, 1024], BF16)
    SB2B = sbuf("SB2B", [128, 1024], BF16)
    Y2FM = sbuf("Y2FM", [128, 1024], BF16)
    QFM = sbuf("QFM", [128, 1024], BF16)
    KFM = sbuf("KFM", [128, 1024], BF16)
    V65 = sbuf("V65", [128, 1040], BF16)
    ABUF = sbuf("ABUF", [128, 256], BF16)
    CTB = sbuf("CTB", [128, 128], BF16)
    COMB = sbuf("COMB", [128, 64], F32)
    SMALL = sbuf("SMALL", [128, 64], F32)
    VEC = sbuf("VEC", [128, 24], F32)
    AFMf = AFM[:].bitcast(F32)
    cNWC = sbuf("cNWC", [128, 1024], F32)
    cGWT = sbuf("cGWT", [128, 64], BF16)
    cGBC = sbuf("cGBC", [128, 8], F32)
    cB1C = sbuf("cB1C", [128, 128], F32)
    cB2R = sbuf("cB2R", [128, 1024], BF16)
    cIOTA = sbuf("cIOTA", [128, 8], F32)
    cIO9 = sbuf("cIO9", [128, 8], F32)
    cTRI = sbuf("cTRI", [128, 128], F32)
    cIDB = sbuf("cIDB", [128, 128], BF16)
    cIDF = sbuf("cIDF", [128, 128], F32)
    cSMK = sbuf("cSMK", [128, 64], F32)

    pA = ctx.enter_context(nc.psum_tensor("pA", [128, 512], F32))
    pB = ctx.enter_context(nc.psum_tensor("pB", [128, 512], F32))
    pC = ctx.enter_context(nc.psum_tensor("pC", [128, 1024], F32))
    pD = ctx.enter_context(nc.psum_tensor("pD", [128, 1024], F32))
    pT = ctx.enter_context(nc.psum_tensor("pT", [128, 512], F32))
    pM = ctx.enter_context(nc.psum_tensor("pM", [128, 456], F32))

    sems = {e: ctx.enter_context(nc.semaphore("sem_" + e)) for e in Sched.ENGS}
    S = Sched()

    vRED, vM, vVAR, vRSTD, vR2, vDD, vEE, vW1, vW2, vRD, vRR = range(11)

    def vec(i):
        return VEC[:, i:i + 1]

    st = {}

    def f_ld_const(eng):
        return [eng.dma_start(cNWC[:], dnwc[:]),
                eng.dma_start(cGWT[:].rearrange("p (k e) -> p k e", e=NEXP),
                              dgwt.rearrange("(k p) e -> p k e", p=128)),
                eng.dma_start(cGBC[:], dgbc[:]),
                eng.dma_start(cB1C[:], db1c[:]),
                eng.dma_start(cB2R[0:8, :], db2r[:]),
                eng.dma_start(cIOTA[:], diota[:]),
                eng.dma_start(cIO9[:], dio9[:]),
                eng.dma_start(cTRI[:], dtri[:]),
                eng.dma_start(cSMK[:], dsmk[:]),
                eng.dma_start(cIDB[:], didb[:]),
                eng.dma_start(cIDF[:], didf[:])]
    tok_const = S.dma(f_ld_const, 11)

    def f_ident_a(eng):
        eng.memset(V65[:], 1.0)
        eng.memset(VEC[:, 15:16], 1.0)
        eng.memset(SRUN[:], 0.0)
        return eng.memset(VEC[:, 11:12], 1.0)
    tok_ident_a = S.t("gpsimd", f_ident_a)
    tok_ident = tok_const
    st["V65"] = tok_ident_a
    bar_v = S.t("vector", lambda eng: eng.engine_nop(), [tok_ident_a])
    bar_s = S.t("scalar", lambda eng: eng.nop(), [tok_ident_a])

    prev = tok_const
    for wi, dw in enumerate((dwq, dwk, dwv, dwo)):
        def f_ldw(eng, dw=dw):
            return eng.dma_start(
                X1O[:, 0:8192].rearrange("p (k o) -> p k o", o=1024),
                dw.rearrange("(k p) o -> p k o", p=128))
        tok_l = S.dma(f_ldw, 1, [prev])

        def f_cv(eng):
            return eng.tensor_copy(NF[:, 0:8192], X1O[:, 0:8192])
        tok_c = S.t("vector", f_cv, [tok_l])

        def f_stw(eng, wi=wi):
            return eng.dma_start(
                swb[wi].rearrange("(k p) o -> p k o", p=128),
                NF[:, 0:8192].rearrange("p (k o) -> p k o", o=1024))
        prev = S.dma(f_stw, 1, [tok_c])
    tok_wconv = prev

    def transpose8(src_bf, dst_fm, dep, extra=()):
        atok = None
        for g in (0, 4):
            def fpe(eng, g=g, src=src_bf):
                return [eng.matmul(pT[:, j * 128:(j + 1) * 128],
                                   src[:, (g + j) * 128:(g + j + 1) * 128],
                                   cIDB[:], start=True, stop=True)
                        for j in range(4)]
            d = [dep, tok_ident, st.get("pT")] + list(extra)
            ptok = S.t("tensor", fpe, d)

            def fac(eng, g=g, dst=dst_fm):
                return eng.copy(dst[:, g * 128:(g + 4) * 128], pT[:])
            atok = S.t("scalar", fac, [ptok])
            st["pT"] = atok
        return atok

    def transpose8_strided(src_bf, c_local, dep):
        atok = None
        for g in (0, 4):
            def fpe(eng, g=g, src=src_bf):
                return [eng.matmul(pT[:, j * 128:(j + 1) * 128],
                                   src[:, (g + j) * 128:(g + j + 1) * 128],
                                   cIDB[:], start=True, stop=True)
                        for j in range(4)]
            ptok = S.t("tensor", fpe, [dep, tok_ident, st.get("pT")])

            def fac(eng, g=g, c_local=c_local):
                ov = NF[:].rearrange("p (k t) -> p k t", t=1024)[
                    :, g:g + 4, c_local * 128:(c_local + 1) * 128]
                iv = pT[:].rearrange("p (k t) -> p k t", t=128)
                return eng.copy(ov, iv)
            atok = S.t("scalar", fac, [ptok])
            st["pT"] = atok
        return atok

    def prepass_A(x_tok, x_ap, xc_ap, xm_buf, xm_key, nf_dst, nf_key,
                  extra_deps=()):
        LN2 = VEC[:, 16:18]

        def fa1(eng, x_ap=x_ap, xm_buf=xm_buf):
            bns = xm_buf.bitcast(F32)
            eng.bn_stats(bns[:, 0:6], x_ap[:, 0:512])
            eng.bn_stats(bns[:, 6:12], x_ap[:, 512:1024])
            eng.tensor_copy(VEC[:, 12:13], VEC[:, 15:16])
            eng.tensor_copy(VEC[:, 13:14], VEC[:, 15:16])
            eng.bn_aggr(VEC[:, 18:20], bns[:, 0:12])
            eng.tensor_copy(VEC[:, 12:13], VEC[:, 15:16])
            eng.tensor_copy(VEC[:, 13:14], VEC[:, 15:16])
            eng.tensor_scalar(vec(vM), VEC[:, 18:19], 1.0, None, ALU.mult)
            eng.tensor_scalar(LN2[:, 0:1], VEC[:, 19:20], EPS, None, ALU.add)
            eng.tensor_copy(VEC[:, 12:13], VEC[:, 15:16])
            eng.reciprocal(VEC[:, 20:21], LN2[:, 0:1])
            eng.tensor_copy(VEC[:, 12:13], VEC[:, 15:16])
            return eng.tensor_scalar(LN2[:, 1:2], VEC[:, 19:20], VEC[:, 20:21],
                                     EPS, ALU.mult, ALU.add)
        ta1 = S.t("vector", fa1, [x_tok, st.get(xm_key)] + list(extra_deps))

        def fa2(eng):
            eng.activation(LN2[:], LN2[:], AF.Ln)
            return eng.activation(LN2[:], LN2[:], AF.Exp, scale=-0.5)
        ta2 = S.t("scalar", fa2, [ta1])

        def fa3(eng, xc_ap=xc_ap, nf_dst=nf_dst, x_ap=x_ap):
            eng.tensor_scalar(xc_ap, x_ap, vec(vM), LN2[:, 0:1],
                              ALU.subtract, ALU.mult)
            eng.tensor_copy(VEC[:, 12:13], VEC[:, 15:16])
            return eng.scalar_tensor_tensor(nf_dst, xc_ap, LN2[:, 1:2],
                                            cNWC[:], ALU.mult, ALU.mult)
        ta3 = S.t("vector", fa3, [ta2, st.get(nf_key), tok_const])
        st[xm_key] = ta3
        st[nf_key] = ta3
        return ta3

    def prepass_B(c_local, nf_src, nf_tok, xc_ap):
        tok_nf = transpose8_strided(nf_src, c_local, nf_tok)

        def f7(eng, c_local=c_local):
            return [eng.matmul(
                pM[:, 0:8],
                NF[:, k * 1024 + c_local * 128: k * 1024 + (c_local + 1) * 128],
                cGWT[:, k * 8:(k + 1) * 8],
                start=(k == 0), stop=(k == 7)) for k in range(8)]
        t7 = S.t("tensor", f7, [tok_nf, tok_const, st.get("pMg"),
                                st.get("pMb")])

        def f8(eng):
            GAP = lambda: eng.tensor_copy(VEC[:, 12:13], VEC[:, 15:16])
            lgb = SMALL[:, 8:12].bitcast(BF16)[:, 0:8]
            eng.tensor_tensor(lgb, pM[:, 0:8], cGBC[:], ALU.add)
            GAP()
            lgf = SMALL[:, 16:24]
            eng.tensor_copy(lgf, lgb)
            GAP()
            m8 = SMALL[:, 0:8]
            eng.max(out=m8, in_=lgf)
            GAP()
            e1 = SMALL[:, 24:32]
            eng.tensor_scalar(e1, lgf, m8[:, 0:1], None, ALU.is_equal)
            GAP()
            tq = SMALL[:, 32:40]
            eng.tensor_tensor(tq, e1, cIO9[:], ALU.mult)
            GAP()
            eng.tensor_scalar(tq, tq, 9.0, None, ALU.add)
            GAP()
            eng.tensor_reduce(vec(vDD), tq, AX.X, ALU.min)
            eng.tensor_tensor(vec(vEE), m8[:, 1:2], m8[:, 0:1], ALU.subtract)
            m1 = SMALL[:, 40:48]
            eng.tensor_scalar(m1, cIOTA[:], vec(vDD), None, ALU.is_equal)
            GAP()
            lg2 = SMALL[:, 48:56]
            eng.scalar_tensor_tensor(lg2, m1, -1.0e30, lgf, ALU.mult, ALU.add)
            GAP()
            eng.tensor_scalar(e1, lg2, m8[:, 1:2], None, ALU.is_equal)
            GAP()
            eng.tensor_tensor(tq, e1, cIO9[:], ALU.mult)
            GAP()
            eng.tensor_scalar(tq, tq, 9.0, None, ALU.add)
            GAP()
            eng.tensor_reduce(vec(vDD), tq, AX.X, ALU.min)
            eng.tensor_copy(vec(vRD), vec(vEE))
            m2 = SMALL[:, 56:64]
            return eng.tensor_scalar(m2, cIOTA[:], vec(vDD), None,
                                     ALU.is_equal)
        t8 = S.t("vector", f8, [t7, tok_const])
        st["pMg"] = t8
        st["pMb"] = t8

        def f9(eng):
            return eng.activation(vec(vEE), vec(vEE), AF.Exp)
        t9 = S.t("scalar", f9, [t8])

        def f10(eng, c_local=c_local):
            den = vec(vRD)
            eng.tensor_scalar(den, vec(vEE), 1.0, None, ALU.add)
            eng.tensor_copy(VEC[:, 12:13], VEC[:, 15:16])
            eng.reciprocal(vec(vW1), den)
            eng.tensor_copy(VEC[:, 12:13], VEC[:, 15:16])
            eng.tensor_tensor(vec(vW2), vec(vEE), vec(vW1), ALU.mult)
            m1 = SMALL[:, 40:48]
            m2 = SMALL[:, 56:64]
            cmb = COMB[:, c_local * 8:(c_local + 1) * 8]
            tq = SMALL[:, 32:40]
            eng.tensor_scalar(tq, m1, vec(vW1), None, ALU.mult)
            eng.tensor_copy(VEC[:, 12:13], VEC[:, 15:16])
            return eng.scalar_tensor_tensor(cmb, m2, vec(vW2), tq,
                                            ALU.mult, ALU.add)
        t10 = S.t("vector", f10, [t9, st.get("COMB")])

        def f11(eng, c_local=c_local):
            return eng.transpose(pM[0:8, 72:200],
                                 COMB[:, c_local * 8:(c_local + 1) * 8], cIDF[:])
        t11 = S.t("tensor", f11, [t10, tok_ident_a, tok_ident,
                                  st.get("pMct"), st.get("pMb")])

        def f12(eng):
            return eng.tensor_copy(CTB[0:8, :], pM[0:8, 72:200])
        t12 = S.t("vector", f12, [t11, st.get("CTB")])
        st["pMct"] = t12
        st["pMb"] = t12

        def f13(eng):
            return [eng.matmul(pC[:, hh * 512:(hh + 1) * 512], CTB[0:8, :],
                               cB2R[0:8, hh * 512:(hh + 1) * 512],
                               start=True, stop=True) for hh in range(2)]
        t13 = S.t("tensor", f13, [t12, tok_const, st.get("pC"),
                                  st.get("pCh0"), st.get("pCh1")])
        st["CTB"] = t13

        def f14(eng, xc_ap=xc_ap):
            return eng.tensor_tensor(xc_ap, xc_ap, pC[:], ALU.add)
        t14 = S.t("vector", f14, [t13])
        st["pC"] = t14
        st["pCh0"] = t14
        st["pCh1"] = t14
        return t14, tok_nf

    def ffn_experts_sb(tok_nf_blk, comb_tok, tok_w1_free, tok_w2_free,
                       mid_cb=None):
        ep_tok = [None] * SBC
        tok_w1_read = None
        tok_w2_read = None
        for e in range(NEXP):
            def f_w1(eng, e=e):
                return eng.dma_start(
                    WA[:, 0:16384].rearrange("p (k o) -> p k o", o=2048),
                    dw1t[e].rearrange("(k p) o -> p k o", p=128))
            tok_lw1 = S.dma(f_w1, 1, [tok_w1_read if e > 0 else tok_w1_free])

            def f_w2(eng, e=e):
                return eng.dma_start(
                    WA[:, 16384:24576].rearrange("p (k o) -> p k o", o=1024),
                    dw2t[e].rearrange("(k p) o -> p k o", p=128))
            tok_lw2 = S.dma(f_w2, 1, [tok_w2_read if e > 0 else tok_w2_free])

            for blk in range(2):
                pool_last = None
                tE3_vals = []
                for jp in range(8):
                    if jp % 2 == 0:
                        pg, pl = pA[:], pB[:]
                        kg, kl = "pA", "pB"
                    else:
                        pg, pl = pC[:, 0:512], pC[:, 512:1024]
                        kg, kl = "pCh0", "pCh1"

                    def fG(eng, blk=blk, jp=jp, pg=pg):
                        return [eng.matmul(
                            pg,
                            WA[:, k * 2048 + jp * 128: k * 2048 + (jp + 1) * 128],
                            NF[:, k * 1024 + blk * 512: k * 1024 + (blk + 1) * 512],
                            start=(k == 0), stop=(k == 7)) for k in range(8)]
                    tG = S.t("tensor", fG, [tok_lw1, tok_nf_blk[blk],
                                            st.get(kg), st.get("pC") if jp % 2 else None])

                    def fL(eng, blk=blk, jp=jp, pl=pl):
                        return [eng.matmul(
                            pl,
                            WA[:, k * 2048 + 1024 + jp * 128: k * 2048 + 1024 + (jp + 1) * 128],
                            NF[:, k * 1024 + blk * 512: k * 1024 + (blk + 1) * 512],
                            start=(k == 0), stop=(k == 7)) for k in range(8)]
                    tL = S.t("tensor", fL, [st.get(kl),
                                            st.get("pC") if jp % 2 else None])
                    tok_w1_read = tL

                    par = jp % 2
                    g1 = SC1[:, par * 512:(par + 1) * 512]
                    l1 = SC2[:, par * 512:(par + 1) * 512]
                    sg = SC3[:, par * 512:(par + 1) * 512]

                    def fE1(eng, e=e, jp=jp, g1=g1, l1=l1, pg=pg, pl=pl):
                        eng.tensor_scalar(
                            l1, pl,
                            cB1C[:, e * 16 + 8 + jp: e * 16 + 8 + jp + 1],
                            LIMIT, ALU.add, ALU.min)
                        eng.tensor_scalar(g1, pg,
                                          cB1C[:, e * 16 + jp: e * 16 + jp + 1],
                                          LIMIT, ALU.add, ALU.min)
                        return eng.tensor_scalar(l1, l1, -LIMIT, 1.0,
                                                 ALU.max, ALU.add)
                    tE1 = S.t("vector", fE1, [tG, tL, tok_const,
                                              st.get(("m1sc", par))])
                    st[kg] = tE1
                    st[kl] = tE1

                    def fE2(eng, sg=sg, g1=g1):
                        return eng.activation(sg, g1, AF.Sigmoid, scale=ALPHA)
                    tE2 = S.t("scalar", fE2, [tE1])

                    def fE3(eng, g1=g1, l1=l1, sg=sg, jp=jp):
                        eng.tensor_tensor(g1, g1, sg, ALU.mult)
                        eng.tensor_copy(VEC[:, 11:12], VEC[:, 15:16])
                        return eng.tensor_tensor(AFM[:, jp * 512:(jp + 1) * 512],
                                                 g1, l1, ALU.mult)
                    tE3 = S.t("gpsimd", fE3, [tE2, st.get("AFM")])
                    st[("m1sc", par)] = tE3
                    pool_last = tE3
                    tE3_vals.append(tE3[1])

                for tch in range(4):
                    c_local = blk * 4 + tch
                    for hh in range(2):
                        ph = pD[:, hh * 512:(hh + 1) * 512]
                        pk = "pDh%d" % hh

                        def fM2(eng, tch=tch, hh=hh, ph=ph, tv=tuple(tE3_vals)):
                            r = []
                            for k2 in range(8):
                                eng.wait_ge(sems["gpsimd"], tv[k2])
                                r.append(eng.matmul(
                                    ph,
                                    AFM[:, k2 * 512 + tch * 128: k2 * 512 + (tch + 1) * 128],
                                    WA[:, 16384 + k2 * 1024 + hh * 512: 16384 + k2 * 1024 + (hh + 1) * 512],
                                    start=(k2 == 0), stop=(k2 == 7)))
                            return r
                        tM2 = S.t("tensor", fM2, [tok_lw2,
                                                  st.get(pk), st.get("pD")])
                        tok_w2_read = tM2

                        def fEp(eng, e=e, c_local=c_local, hh=hh, ph=ph):
                            a = ACC[:, c_local * 1024 + hh * 512:
                                    c_local * 1024 + (hh + 1) * 512]
                            return eng.scalar_tensor_tensor(
                                a, ph,
                                COMB[:, c_local * 8 + e: c_local * 8 + e + 1],
                                a, ALU.mult, ALU.add)
                        tEp = S.t("vector", fEp, [tM2, comb_tok[c_local]])
                        st[pk] = tEp
                        ep_tok[c_local] = tEp
                st["AFM"] = tok_w2_read
                if mid_cb is not None and e == 0 and blk == 0:
                    mid_cb()
                    mid_cb = None
        st["COMB"] = ep_tok[SBC - 1]
        st["ACC"] = ep_tok[SBC - 1]
        st["SC1"] = pool_last
        st["SC2"] = pool_last
        st["SC3"] = pool_last
        return tok_w1_read, tok_w2_read, ep_tok

    def ln_to_y2b(x_tok, x_ap, y2dst=None, y2key="SB1", extra_deps=(),
                  scratch=None):
        if scratch is None:
            bA, bB, kA, kB = SC1[:], SC2[:], "SC1", "SC2"
        else:
            bA, bB, kA, kB = scratch

        def f1(eng, x_ap=x_ap, bA=bA, bB=bB):
            eng.bn_stats(bB[:, 0:6], x_ap[:, 0:512])
            eng.bn_stats(bB[:, 6:12], x_ap[:, 512:1024])
            eng.tensor_copy(VEC[:, 12:13], VEC[:, 15:16])
            eng.tensor_copy(VEC[:, 13:14], VEC[:, 15:16])
            eng.bn_aggr(VEC[:, vM:vM + 2], bB[:, 0:12])
            eng.tensor_copy(VEC[:, 12:13], VEC[:, 15:16])
            eng.tensor_copy(VEC[:, 13:14], VEC[:, 15:16])
            eng.tensor_scalar(bA, x_ap, vec(vM), None, ALU.subtract)
            return eng.tensor_scalar(vec(vVAR), vec(vVAR), EPS, None, ALU.add)
        t1 = S.t("vector", f1, [x_tok, st.get(kA), st.get(kB)] + list(extra_deps))
        if _CFG['pre_upto'] <= 1:
            return t1, t1, t1

        def f2(eng):
            eng.activation(vec(vRSTD), vec(vVAR), AF.Ln)
            return eng.activation(vec(vRSTD), vec(vRSTD), AF.Exp, scale=-0.5)
        t2 = S.t("scalar", f2, [t1])

        if y2dst is None:
            y2dst = SB1[:]

        def f3(eng, y2dst=y2dst, bA=bA):
            return eng.tensor_scalar(y2dst, bA, vec(vRSTD), None, ALU.mult)
        t3 = S.t("vector", f3, [t2, st.get(y2key)])
        st[kA] = t3
        st[kB] = t1
        return t3

    def proj16(dst_psum, pkey, y2fm_tok, w_sb, w_base, w_tok):
        def f(eng, dst=dst_psum, w_sb=w_sb, w_base=w_base):
            r = []
            for oh in range(2):
                for k in range(8):
                    r.append(eng.matmul(
                        dst[:, oh * 512:(oh + 1) * 512],
                        Y2FM[:, k * 128:(k + 1) * 128],
                        w_sb[:, w_base + k * 1024 + oh * 512: w_base + k * 1024 + (oh + 1) * 512],
                        start=(k == 0), stop=(k == 7)))
            return r
        tok = S.t("tensor", f, [y2fm_tok, w_tok, st.get(pkey),
                                st.get(pkey + "h0"), st.get(pkey + "h1")])
        return tok

    def rope_relu(p_src, dst_bf, dst_key, c_local, rope_tok, proj_tok,
                  scratch=None):
        co = c_local * 64
        if scratch is None:
            bA, bB, kA, kB = SC1[:], SC2[:], "SC1", "SC2"
        else:
            bA, bB, kA, kB = scratch
        P1v = p_src[:].rearrange("p (h a i) -> p h a i", a=2, i=32)[:, :, 0, :]
        P2v = p_src[:].rearrange("p (h a i) -> p h a i", a=2, i=32)[:, :, 1, :]
        CC = ROPE[:, co:co + 32].unsqueeze(1).to_broadcast([128, 16, 32])
        SS = ROPE[:, co + 32:co + 64].unsqueeze(1).to_broadcast([128, 16, 32])

        def hv(buf, half):
            return buf[:, half * 512:(half + 1) * 512].rearrange(
                "p (h i) -> p h i", i=32)

        D1 = dst_bf[:].rearrange("p (h a i) -> p h a i", a=2, i=32)[:, :, 0, :]
        D2 = dst_bf[:].rearrange("p (h a i) -> p h a i", a=2, i=32)[:, :, 1, :]

        def f1(eng):
            eng.tensor_tensor(hv(bA, 0), P1v, CC, ALU.mult)
            eng.tensor_tensor(hv(bA, 1), P2v, SS, ALU.mult)
            eng.tensor_tensor(hv(bB, 0), P1v, SS, ALU.mult)
            eng.tensor_tensor(hv(bB, 1), P2v, CC, ALU.mult)
            eng.tensor_tensor(hv(bA, 0), hv(bA, 0), hv(bA, 1), ALU.subtract)
            eng.tensor_tensor(hv(bB, 0), hv(bB, 0), hv(bB, 1), ALU.add)
            eng.tensor_scalar(D1, hv(bA, 0), 0.0, None, ALU.max)
            return eng.tensor_scalar(D2, hv(bB, 0), 0.0, None, ALU.max)
        t1 = S.t("vector", f1, [proj_tok, rope_tok, st.get(kA),
                                st.get(kB), st.get(dst_key)])
        t2 = t1
        st[kA] = t2
        st[kB] = t2
        return t2, t1

    x0_par_reader = [None, None]
    tok_rope_free = None
    tok_srun = tok_ident_a
    tok_w1_free = tok_wconv
    tok_w2_free = tok_wconv
    own_x1_tok = [None] * N_OWN
    tok_ds_store = [None] * N_OWN
    tok_wk = tok_wv = None
    tok_fin_last_pe = None

    for sb_i in range(_CFG['p1_sbs']):
        def f_rope(eng, sb_i=sb_i):
            return eng.dma_start(
                ROPE[:].rearrange("p (c z) -> p c z", z=64),
                drope[sb_i * 1024:(sb_i + 1) * 1024, :].rearrange(
                    "(c p) z -> p c z", p=128))
        tok_rope = S.dma(f_rope, 1, [tok_rope_free])

        tok_acc = [None] * SBC
        tok_nf_blk = [None, None]
        preA = {}
        NF3 = [(SB2, "bSB2"), (SB2B, "bSB2B"), (SB1B, "bSB1B")]

        def emit_A(c, sb_i=sb_i):
            if c in preA or c >= SBC:
                return
            g = sb_i * SBC + c
            par = c % 2
            tri = c % 3

            def f_x0(eng, g=g, par=par):
                return eng.dma_start(X0P[:, par * 1024:(par + 1) * 1024],
                                     dx0[g * 128:(g + 1) * 128, :])
            tok_x0 = S.dma(f_x0, 1, [x0_par_reader[par]])
            x_ap = X0P[:, par * 1024:(par + 1) * 1024]
            nf_dst, nf_key = NF3[tri]
            ta3 = prepass_A(tok_x0, x_ap, ACC[:, c * 1024:(c + 1) * 1024],
                            SB1[:], "bSB1", nf_dst[:], nf_key)
            x0_par_reader[par] = ta3
            preA[c] = ta3

        def emit_B(c):
            tri = c % 3
            nf_src, nf_key = NF3[tri]
            t_acc, t_nf = prepass_B(c, nf_src[:], preA[c],
                                    ACC[:, c * 1024:(c + 1) * 1024])
            st[nf_key] = t_nf
            tok_acc[c] = t_acc
            if c == 3:
                tok_nf_blk[0] = t_nf
            if c == SBC - 1:
                tok_nf_blk[1] = t_nf

        def emit_pre(c0, c1, sb_i=sb_i):
            for c in range(c0, c1):
                emit_A(c)
                emit_A(c + 1)
                emit_A(c + 2)
                emit_B(c)

        emit_pre(0, 4)
        if _CFG['experts']:
            tok_w1r, tok_w2r, ep_tok = ffn_experts_sb(
                tok_nf_blk, tok_acc, tok_w1_free, tok_w2_free,
                mid_cb=lambda: emit_pre(4, SBC))
        else:
            emit_pre(4, SBC)
            tok_w1r = tok_w2r = None
            ep_tok = list(tok_acc)

        if not _CFG['finish']:
            tok_w1_free = tok_w1r
            tok_w2_free = tok_w2r
            continue

        def f_wk(eng):
            return eng.dma_start(
                WA[:, 0:8192].rearrange("p (k o) -> p k o", o=1024),
                swb[1].rearrange("(k p) o -> p k o", p=128))

        def f_wv(eng):
            return eng.dma_start(
                WA[:, 8192:16384].rearrange("p (k o) -> p k o", o=1024),
                swb[2].rearrange("(k p) o -> p k o", p=128))
        tok_wk = S.dma(f_wk, 1, [tok_w1r, tok_wconv])
        tok_wv = S.dma(f_wv, 1, [tok_w1r])

        st["AFMa"] = st.get("AFM")
        st["AFMb"] = st.get("AFM")
        pend = None

        def emit_ds(pend):
            g_, own_, oc_, t_kr_, t_v65_ = pend

            def f_ds(eng):
                r = []
                for h in range(16):
                    p = h // 2
                    dst = pA if p < 4 else pB
                    r.append(eng.matmul(
                        dst[(h % 2) * 64:(h % 2) * 64 + 64,
                            (p % 4) * 65:(p % 4) * 65 + 65],
                        SB2[:, h * 64:(h + 1) * 64],
                        V65[:, h * 65:(h + 1) * 65],
                        start=True, stop=True))
                return r
            t_ds = S.t("tensor", f_ds, [t_kr_, t_v65_, st.get("pA"),
                                        st.get("pB")])
            st["SB2"] = t_ds
            st["V65"] = t_ds

            def f_su(eng, g=g_):
                eng.scalar_tensor_tensor(SRUN[:, 0:260], pA[:, 0:260],
                                         cSMK[:, g:g + 1], SRUN[:, 0:260],
                                         ALU.mult, ALU.add)
                return eng.scalar_tensor_tensor(SRUN[:, 260:520], pB[:, 0:260],
                                                cSMK[:, g:g + 1],
                                                SRUN[:, 260:520],
                                                ALU.mult, ALU.add)
            t_su = S.t("vector", f_su, [t_ds, tok_const])
            st["pA"] = t_su
            st["pB"] = t_su
            if own_:
                def f_dss(eng, oc=oc_):
                    eng.tensor_copy(DS[:, oc * 520: oc * 520 + 260],
                                    pA[:, 0:260])
                    return eng.tensor_copy(
                        DS[:, oc * 520 + 260: oc * 520 + 520], pB[:, 0:260])
                t_dss = S.t("vector", f_dss, [t_ds])
                tok_ds_store[oc_] = t_dss
                st["pA"] = t_dss
                st["pB"] = t_dss
            return t_ds, t_su

        own = sb_i >= OWN_SB0
        finA = {}

        def emit_Af(c, sb_i=sb_i, own=own):
            g = sb_i * SBC + c
            par = c % 2
            oc = (sb_i - OWN_SB0) * SBC + c if own else None

            def f_x0f(eng, g=g, par=par):
                return eng.dma_start(X0P[:, par * 1024:(par + 1) * 1024],
                                     dx0[g * 128:(g + 1) * 128, :])
            tok_x0f = S.dma(f_x0f, 1, [x0_par_reader[par]])

            if own:
                x1_ap = X1O[:, oc * 1024:(oc + 1) * 1024]
            else:
                x1_ap = X1O[:, par * 1024:(par + 1) * 1024]

            def f_x1(eng, c=c, par=par, x1_ap=x1_ap):
                return eng.scalar_tensor_tensor(
                    x1_ap, ACC[:, c * 1024:(c + 1) * 1024], 0.5,
                    X0P[:, par * 1024:(par + 1) * 1024], ALU.mult, ALU.add)
            t_x1 = S.t("vector", f_x1, [tok_x0f, ep_tok[c]])
            x0_par_reader[par] = t_x1
            if own:
                own_x1_tok[oc] = t_x1
            y2dst = (SB1 if par == 0 else SB1B)[:]
            t_y2b = ln_to_y2b(t_x1, x1_ap, y2dst, "bSB1" if par == 0 else "bSB1B")
            finA[c] = (t_y2b, y2dst, par)

        def emit_Bf(c, sb_i=sb_i, own=own):
            nonlocal pend
            g = sb_i * SBC + c
            oc = (sb_i - OWN_SB0) * SBC + c if own else None
            t_y2b, y2dst, par = finA[c]
            t_y2fm = transpose8(y2dst, Y2FM[:], t_y2b)
            st["bSB1" if par == 0 else "bSB1B"] = t_y2fm

            t_kp = proj16(pC, "pC", t_y2fm, WA, 0, tok_wk)
            if pend is not None:
                emit_ds(pend)
            t_kr, t_kread = rope_relu(
                pC, SB2[:], "SB2", c, tok_rope, t_kp,
                scratch=(AFMf[:, 0:1024], AFMf[:, 1024:2048],
                         "AFMa", "AFMb"))
            st["pC"] = t_kread
            st["pCh0"] = t_kread
            st["pCh1"] = t_kread

            t_vp = proj16(pD, "pD", t_y2fm, WA, 8192, tok_wv)

            def f_v65(eng):
                pv = pD[:].rearrange("p (h d) -> p h d", d=64)
                ov = V65[:].rearrange("p (h z) -> p h z", z=65)[:, :, 0:64]
                return eng.tensor_copy(ov, pv)
            t_v65 = S.t("vector", f_v65, [t_vp, st.get("V65")])
            st["pD"] = t_v65
            st["pDh0"] = t_v65
            st["pDh1"] = t_v65
            pend = (g, own, oc, t_kr, t_v65)

        for c in range(SBC):
            if c == 0:
                emit_Af(0)
            if c + 1 < SBC:
                emit_Af(c + 1)
            emit_Bf(c)

        tok_fin_last_pe, tok_srun = emit_ds(pend)
        st["AFM"] = st["AFMa"]
        tok_rope_free = tok_fin_last_pe
        tok_w1_free = tok_fin_last_pe
        tok_w2_free = tok_w2r

    full_p1 = (_CFG['p1_sbs'] == NSB and _CFG['experts'] and _CFG['finish'])
    do_p2 = _CFG['p2'] and full_p1
    do_p3 = _CFG['p3'] and full_p1

    if do_p2:
        def f_wq(eng):
            return eng.dma_start(
                NF[:, 0:8192].rearrange("p (k o) -> p k o", o=1024),
                swb[0].rearrange("(k p) o -> p k o", p=128))
        tok_wq = S.dma(f_wq, 1, [tok_w1r, tok_wconv])

        def f_wo(eng):
            return eng.dma_start(
                ACC[:, 0:8192].rearrange("p (k o) -> p k o", o=1024),
                swb[3].rearrange("(k p) o -> p k o", p=128))
        tok_wo = S.dma(f_wo, 1, [own_x1_tok[N_OWN - 1]])

    tok_out_x2 = [None] * N_OWN
    tok_sbf_reader = None
    tok_rope = None
    p2_count = min(_CFG['p2_n'], N_OWN) if do_p2 else 0
    p2lnA = {}
    pend_x2 = [None]

    def emit_x2():
        poc, t_wop_p = pend_x2[0]
        pend_x2[0] = None

        def f_x2(eng, poc=poc):
            a = X1O[:, poc * 1024:(poc + 1) * 1024]
            return eng.tensor_tensor(a, a, pD[:], ALU.add)
        t_x2 = S.t("vector", f_x2, [t_wop_p])
        st["pD"] = t_x2
        st["pDh0"] = t_x2
        st["pDh1"] = t_x2
        tok_out_x2[poc] = t_x2
    AFM_SCR = (AFMf[:, 0:1024], AFMf[:, 1024:2048], "AFMa", "AFMb")
    for oc in range(p2_count):
        sb_i = OWN_SB0 + oc // SBC
        c_local = oc % SBC
        if c_local == 0:
            def f_rope2(eng, sb_i=sb_i):
                return eng.dma_start(
                    ROPE[:].rearrange("p (c z) -> p c z", z=64),
                    drope[sb_i * 1024:(sb_i + 1) * 1024, :].rearrange(
                        "(c p) z -> p c z", p=128))
            tok_rope = S.dma(f_rope2, 1, [tok_rope_free])

        x2_ap = X1O[:, oc * 1024:(oc + 1) * 1024]
        par = oc % 2
        lnbuf = (SB1 if par == 0 else SB1B)[:]
        lnkey = "bSB1" if par == 0 else "bSB1B"
        if oc in p2lnA:
            t_y2b = p2lnA.pop(oc)
        else:
            t_y2b = ln_to_y2b(own_x1_tok[oc], x2_ap, lnbuf, lnkey,
                              scratch=AFM_SCR)
        t_y2fm = transpose8(lnbuf, Y2FM[:], t_y2b)
        st[lnkey] = t_y2fm

        if _CFG['p2_upto'] <= 1:
            tok_out_x2[oc] = t_y2fm
            continue
        def f_qp(eng):
            r = []
            for oh in range(2):
                for k in range(8):
                    r.append(eng.matmul(
                        pC[:, oh * 512:(oh + 1) * 512],
                        Y2FM[:, k * 128:(k + 1) * 128],
                        NF[:, k * 1024 + oh * 512: k * 1024 + (oh + 1) * 512],
                        start=(k == 0), stop=(k == 7)))
            return r
        t_qp = S.t("tensor", f_qp, [t_y2fm, tok_wq, st.get("pC"),
                                    st.get("pCh0"), st.get("pCh1")])
        t_kp = proj16(pD, "pD", t_y2fm, WA, 0, tok_wk)
        t_qr, t_qread = rope_relu(pC, SB2[:], "SB2", c_local, tok_rope, t_qp)
        st["pC"] = t_qread
        st["pCh0"] = t_qread
        st["pCh1"] = t_qread
        t_qfm = transpose8(SB2[:], QFM[:], t_qr)
        st["SB2"] = t_qfm

        if _CFG['p2_upto'] <= 2:
            tok_out_x2[oc] = t_qfm
            continue
        t_kr, t_kread = rope_relu(pD, SB2[:], "SB2", c_local, tok_rope, t_kp)
        st["pD"] = t_kread
        st["pDh0"] = t_kread
        st["pDh1"] = t_kread
        tok_rope_free = t_kr
        if oc + 1 < p2_count:
            npar = (oc + 1) % 2
            p2lnA[oc + 1] = ln_to_y2b(
                own_x1_tok[oc + 1], X1O[:, (oc + 1) * 1024:(oc + 2) * 1024],
                (SB1 if npar == 0 else SB1B)[:],
                "bSB1" if npar == 0 else "bSB1B", scratch=AFM_SCR)

        t_vp = proj16(pC, "pC", t_y2fm, WA, 8192, tok_wv)
        t_kfm = transpose8(SB2[:], KFM[:], t_kr, extra=[st.get("KFM")])
        st["SB2"] = t_kfm

        def f_v65(eng):
            pv = pC[:].rearrange("p (h d) -> p h d", d=64)
            ov = V65[:].rearrange("p (h z) -> p h z", z=65)[:, :, 0:64]
            return eng.tensor_copy(ov, pv)
        t_v65 = S.t("vector", f_v65, [t_vp, st.get("V65")])
        st["pC"] = t_v65
        st["pCh0"] = t_v65
        st["pCh1"] = t_v65

        def f_sbf(eng):
            return eng.tensor_copy(SBF[:], SRUN[:])
        t_sbfr = S.t("vector", f_sbf, [tok_srun, tok_sbf_reader])

        if _CFG['p2_upto'] <= 4:
            tok_out_x2[oc] = t_sbfr
            continue
        SC_AP = [pA[:, 0:128], pC[:, 0:128], pA[:, 128:256], pC[:, 128:256]]
        NM_AP = [pB[:, 0:65], pC[:, 512:577], pB[:, 65:130], pC[:, 577:642]]
        X0Pb = X0P[:].bitcast(BF16)
        AB_AP = [ABUF[:, 0:128], ABUF[:, 128:256],
                 X0Pb[:, 0:128], X0Pb[:, 128:256]]
        t_mk_h = {}
        t_nm_h = {}
        t_ep_h = {}
        n_heads = min(16, _CFG['p2_nh'])

        def emit_sc_mk(h):
            hp4 = h % 4
            f_tile = h // 2
            b0 = (h % 2) * 64
            sc_ap = SC_AP[hp4]

            def f_sc(eng, f_tile=f_tile, b0=b0, sc_ap=sc_ap):
                return eng.matmul(
                    sc_ap,
                    KFM[b0:b0 + 64, f_tile * 128:(f_tile + 1) * 128],
                    QFM[b0:b0 + 64, f_tile * 128:(f_tile + 1) * 128],
                    start=True, stop=True)
            t_sc = S.t("tensor", f_sc, [t_kfm, t_qfm, t_v65,
                                        t_mk_h.get(h - 2)])

            def f_mk(eng, hp4=hp4, sc_ap=sc_ap):
                return eng.tensor_tensor(AB_AP[hp4], sc_ap, cTRI[:],
                                         ALU.mult)
            xg = [x0_par_reader[0], x0_par_reader[1]] \
                if (oc == 0 and hp4 >= 2) else []
            t_mk_h[h] = S.t("vector", f_mk, [t_sc, tok_const] + xg)

        def emit_nm_ep(h):
            hp4 = h % 4
            f_tile = h // 2
            b0 = (h % 2) * 64
            p = h // 2
            nm_ap = NM_AP[hp4]

            def f_nm(eng, h=h, hp4=hp4, f_tile=f_tile, b0=b0, p=p,
                     nm_ap=nm_ap):
                r = [eng.matmul(nm_ap,
                                AB_AP[hp4],
                                V65[:, h * 65:(h + 1) * 65],
                                start=True, stop=False)]
                base = 0 if p < 4 else 260
                sb_t = SBF[b0:b0 + 64,
                           base + (p % 4) * 65: base + (p % 4) * 65 + 65]
                r.append(eng.matmul(nm_ap,
                                    QFM[b0:b0 + 64,
                                        f_tile * 128:(f_tile + 1) * 128],
                                    sb_t, start=False, stop=True))
                return r
            t_nm = S.t("tensor", f_nm, [t_mk_h[h], t_v65, t_sbfr,
                                        t_ep_h.get(h - 2)])
            t_nm_h[h] = t_nm

            def f_ep(eng, h=h, nm_ap=nm_ap):
                eng.tensor_scalar(vec(vRD), nm_ap[:, 64:65],
                                  EPS_ATT, None, ALU.add)
                eng.tensor_copy(VEC[:, 12:13], VEC[:, 15:16])
                eng.reciprocal(vec(vRR), vec(vRD))
                eng.tensor_copy(VEC[:, 12:13], VEC[:, 15:16])
                return eng.tensor_scalar(SC2[:, h * 64:(h + 1) * 64],
                                         nm_ap[:, 0:64], vec(vRR),
                                         None, ALU.mult)
            dep_sc2 = [st.get("SC2")] if h == 0 else []
            t_ep_h[h] = S.t("vector", f_ep, [t_nm] + dep_sc2)

        for h in range(n_heads):
            emit_sc_mk(h)
            if h >= 3:
                emit_nm_ep(h - 3)
        for h in range(max(0, n_heads - 3), n_heads):
            emit_nm_ep(h)
        last_nm = t_nm_h[n_heads - 1]
        attn_tok = t_ep_h[n_heads - 1]
        st["pA"] = attn_tok
        st["pB"] = attn_tok
        st["pC"] = attn_tok
        st["pCh0"] = attn_tok
        st["pCh1"] = attn_tok
        tok_sbf_reader = last_nm
        st["V65"] = last_nm

        if _CFG['p2_upto'] <= 5:
            tok_out_x2[oc] = attn_tok
            continue
        def f_sup(eng, oc=oc):
            eng.tensor_tensor(SRUN[:, 0:260], SRUN[:, 0:260],
                              DS[:, oc * 520: oc * 520 + 260], ALU.add)
            return eng.tensor_tensor(SRUN[:, 260:520], SRUN[:, 260:520],
                                     DS[:, oc * 520 + 260: oc * 520 + 520],
                                     ALU.add)
        tok_srun = S.t("vector", f_sup, [tok_ds_store[oc], tok_srun])

        def f_atb(eng):
            return eng.tensor_copy(SB2[:], SC2[:])
        t_atb = S.t("vector", f_atb, [attn_tok, st.get("SB2")])
        st["SC2"] = t_atb
        t_afm = transpose8(SB2[:], KFM[:], t_atb, extra=[last_nm])
        st["SB2"] = t_afm
        st["KFM"] = None

        if _CFG['p2_upto'] <= 6:
            tok_out_x2[oc] = t_afm
            continue

        def f_wop(eng):
            r = []
            for oh in range(2):
                for k in range(8):
                    r.append(eng.matmul(
                        pD[:, oh * 512:(oh + 1) * 512],
                        KFM[:, k * 128:(k + 1) * 128],
                        ACC[:, k * 1024 + oh * 512: k * 1024 + (oh + 1) * 512],
                        start=(k == 0), stop=(k == 7)))
            return r
        t_wop = S.t("tensor", f_wop, [t_afm, tok_wo, st.get("pD"),
                                      st.get("pDh0"), st.get("pDh1")])
        st["KFM"] = t_wop

        def f_x2(eng, oc=oc):
            a = X1O[:, oc * 1024:(oc + 1) * 1024]
            return eng.tensor_tensor(a, a, pD[:], ALU.add)
        t_x2 = S.t("vector", f_x2, [t_wop])
        st["pD"] = t_x2
        st["pDh0"] = t_x2
        st["pDh1"] = t_x2
        tok_out_x2[oc] = t_x2

    tok_out_dma = []
    if not do_p3:
        for oc in range(N_OWN):
            def f_outd(eng, oc=oc):
                return eng.dma_start(dout[oc * 128:(oc + 1) * 128, :],
                                     X1O[:, oc * 1024:(oc + 1) * 1024])
            dep = [tok_out_x2[oc] if (do_p2 and tok_out_x2[oc]) else None,
                   ("vector", S.cnt["vector"]), ("tensor", S.cnt["tensor"]),
                   ("scalar", S.cnt["scalar"]), ("gpsimd", S.cnt["gpsimd"])]
            tok_out_dma.append(S.dma(f_outd, 1, dep))
    else:
        tok_p2_done = tok_out_x2[N_OWN - 1]
        tok_w1_free = tok_p2_done
        tok_w2_free_p3 = tok_w2r
        out_par_reader = [None, None]
        tok_out_dma = []
        tok_w1r_p3 = None
        st["ACC"] = tok_p2_done
        st["SB1"] = tok_p2_done
        st["SB2"] = tok_p2_done
        st["SC2"] = tok_p2_done
        st["AFM"] = tok_p2_done
        st["bSB1"] = tok_p2_done
        st["bSB1B"] = tok_p2_done

        for sb2 in range(2):
            tok_acc = [None] * SBC
            tok_nf_blk = [None, None]
            preA3 = {}
            NF3 = [(SB2, "bSB2"), (SB2B, "bSB2B"), (SB1B, "bSB1B")]

            def emit_A3(c, sb2=sb2):
                if c in preA3 or c >= SBC:
                    return
                oc = sb2 * SBC + c
                tri = c % 3
                x_ap = X1O[:, oc * 1024:(oc + 1) * 1024]
                extra = []
                if sb2 == 0 and c == 0:
                    extra = [tok_p2_done]
                if sb2 == 1 and c == 0:
                    extra = [out_par_reader[0], out_par_reader[1]]
                nf_dst, nf_key = NF3[tri]
                ta3 = prepass_A(tok_out_x2[oc], x_ap,
                                ACC[:, c * 1024:(c + 1) * 1024],
                                SB1[:], "bSB1", nf_dst[:], nf_key,
                                extra_deps=extra)
                preA3[c] = ta3

            def emit_B3(c):
                tri = c % 3
                nf_src, nf_key = NF3[tri]
                t_acc, t_nf = prepass_B(c, nf_src[:], preA3[c],
                                        ACC[:, c * 1024:(c + 1) * 1024])
                st[nf_key] = t_nf
                tok_acc[c] = t_acc
                if c == 3:
                    tok_nf_blk[0] = t_nf
                if c == SBC - 1:
                    tok_nf_blk[1] = t_nf

            def emit_pre3(c0, c1):
                for c in range(c0, c1):
                    emit_A3(c)
                    emit_A3(c + 1)
                    emit_A3(c + 2)
                    emit_B3(c)

            emit_pre3(0, 4)
            tok_w1r_p3, tok_w2r_p3, ep_tok = ffn_experts_sb(
                tok_nf_blk, tok_acc,
                tok_w1_free if sb2 == 0 else tok_w1r_p3,
                tok_w2_free_p3,
                mid_cb=lambda: emit_pre3(4, SBC))
            tok_w2_free_p3 = tok_w2r_p3

            for c in range(SBC):
                oc = sb2 * SBC + c
                par = c % 2
                x3buf = SC2 if par == 0 else SC3

                def f_x3(eng, oc=oc, c=c, x3buf=x3buf):
                    return eng.scalar_tensor_tensor(
                        x3buf[:], ACC[:, c * 1024:(c + 1) * 1024], 0.5,
                        X1O[:, oc * 1024:(oc + 1) * 1024], ALU.mult, ALU.add)
                t_x3 = S.t("vector", f_x3, [ep_tok[c], out_par_reader[par],
                                            st.get("SC2" if par == 0 else "SC3")])
                if par == 0:
                    st["SC2"] = t_x3
                else:
                    st["SC3"] = t_x3

                def f_out(eng, oc=oc, x3buf=x3buf):
                    return eng.dma_start(dout[oc * 128:(oc + 1) * 128, :], x3buf[:])
                tok_o = S.dma(f_out, 1, [t_x3])
                out_par_reader[par] = tok_o
                tok_out_dma.append(tok_o)

    if _CFG['dbg'] is not None:
        barrier = [("tensor", S.cnt["tensor"]), ("vector", S.cnt["vector"]),
                   ("scalar", S.cnt["scalar"]), ("gpsimd", S.cnt["gpsimd"]),
                   ("sync", S.cnt["sync"])]
        dbg_src = {"acc": ACC, "nf": NF, "xc": SB1, "sb2": SB2,
                   "comb": COMB, "ctb": CTB, "small": SMALL,
                   "x0p": X0P, "sc3": SC3, "wa": WA, "gwt": cGWT,
                   "b1c": cB1C, "b2r": cB2R}[_CFG['dbg']]
        ncols = dbg_src.shape[1] if hasattr(dbg_src, 'shape') else 8192
        ncols = {"acc": 8192, "nf": 8192, "xc": 1024, "sb2": 1024,
                 "comb": 64, "ctb": 128, "small": 64, "x0p": 2048,
                 "sc3": 1024, "wa": 16384, "gwt": 64, "b1c": 128,
                 "b2r": 1024}[_CFG['dbg']]
        prevtok = None
        for c in range((ncols + 1023) // 1024):
            w = min(1024, ncols - c * 1024)

            def f_cp(eng, c=c, w=w, dbg_src=dbg_src):
                return eng.tensor_copy(SC1[:, 0:w], dbg_src[:, c * 1024:c * 1024 + w])
            tcp = S.t("vector", f_cp, barrier + [prevtok])

            def f_dmp(eng, c=c, w=w):
                return eng.dma_start(dout[c * 128:(c + 1) * 128, 0:w], SC1[:, 0:w])
            prevtok = S.dma(f_dmp, 1, [tcp])

    def f_fin(eng):
        return eng.engine_nop()
    S.t("vector", f_fin, [("sync", S.cnt["sync"])])

    blk = ctx.enter_context(nc.Block())
    S.replay(blk, sems)
    ctx.close()
    return nc


_NC_CACHE = {}


def _get_nc():
    if "nc" not in _NC_CACHE:
        _NC_CACHE["nc"] = build_program()
    return _NC_CACHE["nc"]


def _host_prep(inputs):
    x = np.asarray(inputs["L0_attn_x"])
    wq = np.asarray(inputs["L0_attn_wq"])
    wk = np.asarray(inputs["L0_attn_wk"])
    wv = np.asarray(inputs["L0_attn_wv"])
    wo = np.asarray(inputs["L0_attn_wo"])
    nw = np.asarray(inputs["L0_ffn_norm_weight"])
    gw = np.asarray(inputs["L0_ffn_gate_weight"])
    gb = np.asarray(inputs["L0_ffn_gate_bias"])
    w1 = np.asarray(inputs["L0_ffn_mlp1_weight"])
    b1 = np.asarray(inputs["L0_ffn_mlp1_bias"])
    w2 = np.asarray(inputs["L0_ffn_mlp2_weight"])
    b2 = np.asarray(inputs["L0_ffn_mlp2_bias"])

    perm = np.concatenate([
        np.concatenate([h * 64 + 2 * np.arange(32),
                        h * 64 + 2 * np.arange(32) + 1])
        for h in range(NH)])
    wqp = np.ascontiguousarray(wq[:, perm], dtype=np.float32)
    wkp = np.ascontiguousarray(wk[:, perm], dtype=np.float32)

    w1t = np.empty((NEXP, D, DFF), dtype=BFD)
    b1c_all = np.empty((CH, 128), dtype=np.float32)
    for e in range(NEXP):
        T = w1[e].T
        w1t[e] = np.concatenate([T[:, 0::2], T[:, 1::2]], axis=1)
        bp = np.concatenate([b1[e][0::2], b1[e][1::2]]).astype(np.float32)
        b1c_all[:, e * 16:(e + 1) * 16] = bp.reshape(16, CH).T
    w2t = np.ascontiguousarray(np.transpose(w2, (0, 2, 1)))
    gwt = np.ascontiguousarray(gw.T)
    gbc = np.ascontiguousarray(np.broadcast_to(gb.astype(np.float32), (CH, NEXP)))
    nwc = np.ascontiguousarray(np.broadcast_to(nw.astype(np.float32), (CH, D)))
    b2r = np.ascontiguousarray(b2)

    tri = (np.arange(CH)[:, None] <= np.arange(CH)[None, :]).astype(np.float32)
    iota8 = np.ascontiguousarray(
        np.broadcast_to(np.arange(NEXP, dtype=np.float32), (CH, NEXP)))
    io9 = iota8 - 9.0
    inv_freq = (1.0 / THETA ** (np.arange(0, HD, 2).astype(np.float32) / HD)
                ).astype(np.float32)

    in_maps = []
    for core in range(8):
        b = core // 4
        q = core % 4
        permq = list(range(q)) + list(range(q + 1, 4)) + [q]
        x0r = np.ascontiguousarray(
            np.concatenate([x[b, 2048 * j:2048 * (j + 1)] for j in permq],
                           axis=0), dtype=np.float32)
        smask = np.zeros((CH, NCH), dtype=np.float32)
        for i in range(NCH):
            if i // 16 < q:
                smask[:, i] = 1.0
        ropet = np.empty((NCH * CH, 64), dtype=np.float32)
        for i in range(NCH):
            quarter = permq[i // 16]
            base = 2048 * quarter + 128 * (i % 16)
            t = (base + np.arange(CH, dtype=np.float32))[:, None]
            freqs = (t * inv_freq[None, :]).astype(np.float32)
            ropet[i * CH:(i + 1) * CH, 0:32] = np.cos(
                freqs.astype(np.float64)).astype(np.float32)
            ropet[i * CH:(i + 1) * CH, 32:64] = np.sin(
                freqs.astype(np.float64)).astype(np.float32)
        in_maps.append({
            "x0r": x0r, "w1t": w1t, "w2t": w2t, "b1c": b1c_all, "b2r": b2r,
            "gwt": gwt, "gbc": gbc, "nwc": nwc,
            "wqf": wqp, "wkf": wkp,
            "wvf": np.ascontiguousarray(wv, dtype=np.float32),
            "wof": np.ascontiguousarray(wo, dtype=np.float32),
            "rope": ropet, "smask": smask, "tri": tri,
            "iota8": iota8, "io9": io9,
            "idb": np.eye(CH, dtype=ml_dtypes.bfloat16),
            "idf": np.eye(CH, dtype=np.float32),
        })
    return in_maps


def kernel(**inputs):
    in_maps = _host_prep(inputs)
    nc = _get_nc()
    res = run_bass_kernel_spmd(nc, in_maps, core_ids=list(range(8)))
    out = np.empty((B, L, D), dtype=np.float32)
    for core in range(8):
        b = core // 4
        q = core % 4
        out[b, 2048 * q:2048 * (q + 1), :] = res.results[core]["xout"]
    return out
